# version 04
# baseline (speedup 1.0000x reference)
import contextlib
import numpy as np
import concourse.bass as bass
import concourse.mybir as mybir
from concourse.bass_utils import run_bass_kernel_spmd

F32 = mybir.dt.float32
BF16 = mybir.dt.bfloat16
AF = mybir.ActivationFunctionType
ALU = mybir.AluOpType

NB, S, D = 4, 4096, 1024
T = 2048
NC = 8
WIN = 512
KVT = T + WIN
E, I, IP = 8, 938, 1024
DI, DS, DTR = 2048, 16, 64
INNER = 4096
CH = 512
NCH = T // CH
RND = 12582912.0



def stage_inputs(inp):
    f32 = np.float32
    x = np.asarray(inp["L0_attn_x"], f32)

    wqkv = np.asarray(inp["L0_attn_Wqkv"], f32)
    wqkv_hp = np.zeros((8, 128, 3 * 8 * 128), f32)
    for hp in range(8):
        for j in range(3):
            blk = wqkv[:, j * 1024 + hp * 128: j * 1024 + (hp + 1) * 128]
            t = blk.reshape(8, 128, 128)
            wqkv_hp[hp, :, j * 1024:(j + 1) * 1024] = (
                t.transpose(1, 0, 2).reshape(128, 1024))
    wout = np.asarray(inp["L0_attn_Wout"], f32).reshape(8, 128, 1024)
    wgate_t = np.ascontiguousarray(
        np.asarray(inp["L0_ffn_gate_weight"], f32).T.reshape(8, 128, 8)
        .transpose(1, 0, 2).reshape(128, 64))
    sh_gu = np.asarray(inp["L0_ffn_shared_gate_up"], f32)
    sh_dn = np.asarray(inp["L0_ffn_shared_down"], f32)
    ex_gu = np.asarray(inp["L0_ffn_experts_gate_up"], f32)
    ex_dn = np.asarray(inp["L0_ffn_experts_down"], f32)

    def gu_tiles(gu):
        g = np.zeros((IP, D), f32); u = np.zeros((IP, D), f32)
        g[:I] = gu[:I]; u[:I] = gu[I:]
        out = np.zeros((8, 16, 128, 128), f32)
        for kc in range(8):
            for ic in range(8):
                out[kc, ic] = g[ic * 128:(ic + 1) * 128, kc * 128:(kc + 1) * 128].T
                out[kc, 8 + ic] = u[ic * 128:(ic + 1) * 128, kc * 128:(kc + 1) * 128].T
        return out

    def dn_tiles(dn):
        dpad = np.zeros((D, IP), f32); dpad[:, :I] = dn
        out = np.zeros((8, 8, 128, 128), f32)
        for ic in range(8):
            for dc in range(8):
                out[ic, dc] = dpad[dc * 128:(dc + 1) * 128, ic * 128:(ic + 1) * 128].T
        return out

    wgu = np.stack([gu_tiles(ex_gu[e]) for e in range(E)] + [gu_tiles(sh_gu)])
    wgu = np.ascontiguousarray(wgu.transpose(0, 3, 1, 2, 4).reshape(9, 128, 8 * 16 * 128))
    wdn = np.stack([dn_tiles(ex_dn[e]) for e in range(E)] + [dn_tiles(sh_dn)])
    wdn = np.ascontiguousarray(wdn.transpose(0, 3, 1, 2, 4).reshape(9, 128, 8 * 8 * 128))

    def mamba_pack(tag):
        g = lambda n: np.asarray(inp["L1_attn_" + n + "_" + tag], f32)
        in_w = g("in_w")
        win = np.zeros((8, 128, 2048), f32)
        for grp in range(8):
            for kc in range(8):
                for m2 in range(2):
                    mc = 2 * grp + m2
                    til = in_w[mc * 128:(mc + 1) * 128, kc * 128:(kc + 1) * 128].T
                    win[grp, :, (kc * 2 + m2) * 128:(kc * 2 + m2 + 1) * 128] = til
        wz = np.zeros((16, 128, 1024), f32)
        for dc in range(16):
            for kc in range(8):
                til = in_w[2048 + dc * 128:2048 + (dc + 1) * 128,
                           kc * 128:(kc + 1) * 128].T
                wz[dc, :, kc * 128:(kc + 1) * 128] = til
        xproj = g("xproj_w")
        xpg = np.zeros((128, 16 * 96), f32)
        for kc in range(16):
            xpg[:, kc * 96:(kc + 1) * 96] = xproj[:, kc * 128:(kc + 1) * 128].T
        mout = g("mout_w")
        mg = np.zeros((8, 128, 2048), f32)
        for mc in range(8):
            for ic in range(16):
                mg[mc, :, ic * 128:(ic + 1) * 128] = (
                    mout[mc * 128:(mc + 1) * 128, ic * 128:(ic + 1) * 128].T)
        return dict(
            win=win, wz=wz, conv=g("conv_w")[:, 0, :],
            convb=np.ascontiguousarray(g("conv_b").reshape(16, 128).T),
            xproj=xpg,
            dtw=np.ascontiguousarray(g("dt_w").T),
            dtb=np.ascontiguousarray(g("dt_b").reshape(16, 128).T),
            alog=np.ascontiguousarray(g("A_log")[0:1, :]),
            dv=np.ascontiguousarray(g("D").reshape(16, 128).T),
            wmout=mg,
        )

    mbd = {"f": mamba_pack("f"), "b": mamba_pack("b")}
    outw_t = np.ascontiguousarray(np.asarray(inp["L1_attn_out_w"], f32).T)
    outb = np.ascontiguousarray(np.asarray(inp["L1_attn_out_b"], f32).reshape(8, 128).T)
    gamma = np.asarray(inp["L1_attn_gamma"], f32)
    beta = np.asarray(inp["L1_attn_beta"], f32)
    def ffn_mc(w):
        o = np.zeros((32, 128, 1024), f32)
        for mc in range(32):
            for kc in range(8):
                o[mc, :, kc * 128:(kc + 1) * 128] = (
                    w[mc * 128:(mc + 1) * 128, kc * 128:(kc + 1) * 128].T)
        return o

    def ffn_dc(w):
        o = np.zeros((8, 128, 4096), f32)
        for dc in range(8):
            for ic in range(32):
                o[dc, :, ic * 128:(ic + 1) * 128] = (
                    w[dc * 128:(dc + 1) * 128, ic * 128:(ic + 1) * 128].T)
        return o
    wg2t = ffn_mc(np.asarray(inp["L1_ffn_w_gate"], f32))
    wu2t = ffn_mc(np.asarray(inp["L1_ffn_w_up"], f32))
    wd2t = ffn_dc(np.asarray(inp["L1_ffn_w_down"], f32))

    maps = []
    for c in range(NC):
        b, h = c // 2, c % 2
        if h == 0:
            idx = np.arange(T)
            xk = np.concatenate([np.zeros((WIN, D), f32), x[b, 0:T]], 0)
        else:
            idx = np.arange(S - 1, T - 1, -1)
            halo = x[b, T - 1:T - 1 - WIN:-1]
            xk = np.concatenate([x[b, idx], halo], 0)
        r = np.arange(128)[:, None]; cc = np.arange(640)[None, :]
        am = np.zeros((16, 128, 640), f32)
        for qb in range(16):
            if h == 0:
                am[qb] = ((cc - r >= 1) & (cc - r <= WIN)
                          & (qb * 128 + cc >= WIN)).astype(f32)
            else:
                am[qb] = ((cc - r >= 0) & (cc - r <= WIN - 1)).astype(f32)
        pselbig = np.zeros((128, 8, 304), f32)
        pselbig[:, c ^ 1, :] = 1.0
        L, R = ("f", "b") if h == 0 else ("b", "f")
        mbl, mbr = mbd[L], mbd[R]
        convd = np.zeros((2, 8, 128, 1024), f32)
        for dci in range(16):
            sl = slice(dci * 128, (dci + 1) * 128)
            for j in range(4):
                o = ((dci % 2) * 4 + j) * 128
                convd[0, dci // 2, :, o:o + 128] = np.diag(mbl["conv"][sl, j])
                convd[1, dci // 2, :, o:o + 128] = np.diag(mbr["conv"][sl, 3 - j])
        owLh = outw_t[0:1024] if h == 0 else outw_t[1024:2048]
        owRh = outw_t[1024:2048] if h == 0 else outw_t[0:1024]
        owL = np.zeros((8, 128, 1024), f32)
        owR = np.zeros((8, 128, 1024), f32)
        for dci in range(8):
            for kc in range(8):
                owL[dci, :, kc * 128:(kc + 1) * 128] = (
                    owLh[kc * 128:(kc + 1) * 128, dci * 128:(dci + 1) * 128])
                owR[dci, :, kc * 128:(kc + 1) * 128] = (
                    owRh[kc * 128:(kc + 1) * 128, dci * 128:(dci + 1) * 128])
        m = {
            "xin": np.ascontiguousarray(x[b, idx].T.reshape(8, 128, T)),
            "xkv": np.ascontiguousarray(xk.T.reshape(8, 128, KVT)),
            "amask": am, "idm": np.eye(128, dtype=f32),
            "ones_row": np.ones((1, 128), f32),
            "cbias": np.tile(np.array([[1e-5, 1e-30, np.log(0.125),
                                        12582912.0, -12582912.0, 0.0]], f32),
                             (128, 1)),
            "ones_col": np.ones((128, 1), f32),
            "nsel": np.ascontiguousarray(
                np.eye(16, dtype=f32)[:, :, None].repeat(128, 2).reshape(16, 16 * 128)),
            "pselbig": pselbig.reshape(128, 8 * 304),
            "qoff": np.array([[WIN if h == 0 else 0]], np.int32).view(f32),
            "wqkv": wqkv_hp, "wout": wout, "wgatelog": wgate_t,
            "wgu": wgu, "wdn": wdn,
            "winL": mbl["win"], "winR": mbr["win"], "convd": convd,
            "convbL": mbl["convb"], "convbR": mbr["convb"],
            "xprojL": mbl["xproj"], "xprojR": mbr["xproj"],
            "dtwL": mbl["dtw"], "dtwR": mbr["dtw"],
            "dtbL": mbl["dtb"], "dtbR": mbr["dtb"],
            "alogL": mbl["alog"], "alogR": mbr["alog"],
            "dvL": mbl["dv"], "dvR": mbr["dv"],
            "wmoutL": mbl["wmout"], "wmoutR": mbr["wmout"],
            "owL": owL, "owR": owR,
            "wzL": mbl["wz"], "wzR": mbr["wz"],
            "outb": outb,
            "gammarow": gamma.reshape(1, 1024),
            "betacol": np.ascontiguousarray(beta.reshape(8, 128).T),
            "wg2t": wg2t, "wu2t": wu2t, "wd2t": wd2t,
        }
        maps.append(m)
    return maps


def unstage_output(results):
    y = np.zeros((NB, S, D), np.float32)
    for c in range(NC):
        b, h = c // 2, c % 2
        o = results[c]["out"].reshape(D, T).T
        if h == 0:
            y[b, 0:T] = o
        else:
            y[b, S - 1:T - 1:-1] = o
    return y



class Sem:
    def __init__(self, h):
        self.h = h
        self.n = 0


class Q:
    ENGS = ("sync", "vector", "scalar", "tensor", "gpsimd")

    def __init__(self, self_sems=None):
        self.q = {e: [] for e in self.ENGS}
        self.self_sems = self_sems or {}

    def emit(self, eng, fn, inc=None, inck=1):
        ss = self.self_sems.get(eng)
        if ss is not None and ss.n > 0:
            self.wait(eng, ss)
        if inc is not None:
            inc.n += inck
        def thunk(e, fn=fn, inc=inc, inck=inck):
            ins = fn(e)
            if inc is not None:
                ins.then_inc(inc.h, inck)
        self.q[eng].append(thunk)

    def wait(self, eng, sem, val=None):
        v = sem.n if val is None else val
        self.q[eng].append(lambda e, s=sem, v=v: e.wait_ge(s.h, v))

    def barrier(self, sems, engs=None):
        for e in (engs or self.ENGS):
            for s in sems:
                self.wait(e, s)


def build_program(stage_limit=99):
    nc = bass.Bass("TRN2", target_bir_lowering=False)
    din = lambda n, s: nc.dram_tensor(n, s, F32, kind="ExternalInput").ap()

    xin_h = din("xin", [8, 128, T])
    xkv_h = din("xkv", [8, 128, KVT])
    am_h = din("amask", [16, 128, 640])
    idm_h = din("idm", [128, 128])
    onesr_h = din("ones_row", [1, 128])
    onesc_h = din("ones_col", [128, 1])
    cbias_h = din("cbias", [128, 6])
    nsel_h = din("nsel", [16, 16 * 128])
    pselbig_h = din("pselbig", [128, 8 * 304])
    qoff_h = din("qoff", [1, 1])
    wqkv_h = din("wqkv", [8, 128, 3 * 8 * 128])
    wout_h = din("wout", [8, 128, 1024])
    wgl_h = din("wgatelog", [128, 64])
    wgu_h = din("wgu", [9, 128, 8 * 16 * 128])
    wdn_h = din("wdn", [9, 128, 8 * 8 * 128])
    win_h = {"L": din("winL", [8, 128, 2048]), "R": din("winR", [8, 128, 2048])}
    wz_h = {"L": din("wzL", [16, 128, 1024]), "R": din("wzR", [16, 128, 1024])}
    convd_h = din("convd", [2, 8, 128, 1024])
    convb_h = {"L": din("convbL", [128, 16]), "R": din("convbR", [128, 16])}
    xproj_h = {"L": din("xprojL", [128, 16 * 96]), "R": din("xprojR", [128, 16 * 96])}
    dtw_h = {"L": din("dtwL", [64, DI]), "R": din("dtwR", [64, DI])}
    dtb_h = {"L": din("dtbL", [128, 16]), "R": din("dtbR", [128, 16])}
    alog_h = {"L": din("alogL", [1, 16]), "R": din("alogR", [1, 16])}
    dv_h = {"L": din("dvL", [128, 16]), "R": din("dvR", [128, 16])}
    wmout_h = {"L": din("wmoutL", [8, 128, 2048]), "R": din("wmoutR", [8, 128, 2048])}
    ow_h = {"L": din("owL", [8, 128, 1024]), "R": din("owR", [8, 128, 1024])}
    outb_h = din("outb", [128, 8])
    gr_h = din("gammarow", [1, 1024])
    bc_h = din("betacol", [128, 8])
    wg2_h = din("wg2t", [32, 128, 1024])
    wu2_h = din("wu2t", [32, 128, 1024])
    wd2_h = din("wd2t", [8, 128, 4096])
    out_h = nc.dram_tensor("out", [8, 128, T], F32, kind="ExternalOutput").ap()

    bncBC = nc.dram_tensor("bncBC", [32, CH], F32, kind="Internal").ap()
    fwdscr = nc.dram_tensor("fwdscr", [NCH, 128, 8 * CH], BF16, kind="Internal").ap()
    bncCW = nc.dram_tensor("bncCW", [8, T], F32, kind="Internal").ap()
    cc_in = nc.dram_tensor("cc_in", [128, 304], F32, kind="Internal").ap()
    cc_out = nc.dram_tensor("cc_out", [8, 128, 304], F32, kind="Internal").ap()
    wqg_scr = nc.dram_tensor("wqg_scr", [32, 128, 1024], BF16, kind="Internal").ap()
    wqu_scr = nc.dram_tensor("wqu_scr", [32, 128, 1024], BF16, kind="Internal").ap()
    wqd_scr = nc.dram_tensor("wqd_scr", [8, 128, 4096], BF16, kind="Internal").ap()

    sa = nc.alloc_sbuf_tensor_at
    off = [16544]

    def alloc(name, shape, dtype, cur=None):
        c = cur if cur is not None else off
        t = sa(name, shape, dtype, offset=c[0])
        nbytes = int(np.prod(shape[1:])) * (2 if dtype == BF16 else 4)
        c[0] = (c[0] + nbytes + 31) & ~31
        assert c[0] <= 224 * 1024, f"SBUF overflow at {name}: {c[0]}"
        return t

    x_sb = [alloc(f"x{i}", [128, T], F32) for i in range(8)]
    onesr_bf = alloc("onesr_bf", [1, 128], BF16)
    onesr_f = alloc("onesr_f", [1, 128], F32)
    onesc_bf = alloc("onesc_bf", [128, 1], BF16)
    onesc_f = alloc("onesc_f", [128, 1], F32)
    idm_bf = alloc("idm_bf", [128, 128], BF16)
    idm_f = alloc("idm_f", [128, 128], F32)
    qoff_sb = alloc("qoff_sb", [1, 1], F32)
    cbias_sb = alloc("cbias_sb", [128, 6], F32)
    outb_sb = alloc("outb_sb", [128, 8], F32)
    dtb_sb = {ch: alloc(f"dtb{ch}", [128, 16], F32) for ch in "LR"}
    dv_sb = {ch: alloc(f"dv{ch}", [128, 16], F32) for ch in "LR"}
    convb_sb = {ch: alloc(f"convb{ch}", [128, 16], F32) for ch in "LR"}
    avec_sb = {ch: alloc(f"avec{ch}", [128, 16], F32) for ch in "LR"}
    grow_sb = alloc("grow_sb", [1, 1024], F32)
    bcol_sb = alloc("bcol_sb", [128, 8], F32)
    stbuf = {"L": alloc("stL", [128, 256], BF16), "R": alloc("stR", [128, 256], BF16)}
    ximg = {"L": alloc("ximgL", [128, 48], BF16), "R": alloc("ximgR", [128, 48], BF16)}
    rowA_t = alloc("rowA", [1, 2560], F32)
    rowB_t = alloc("rowB", [1, 2560], F32)
    rowA = rowA_t[0:1, :]
    rowB = rowB_t[0:1, :]
    rowbf = alloc("rowbf", [1, 2048], BF16)
    lnt1 = alloc("lnt1", [128, 512], F32)
    casA = alloc("casA", [128, 512], BF16)
    casB = alloc("casB", [128, 512], BF16)
    wmean = alloc("wmean", [128, 4], F32)
    svec = alloc("svec", [128, 4], F32)
    smean = alloc("smean", [1, 8], F32)
    cmean_sb = alloc("cmean_sb", [1, 4], F32)

    ARENA0 = off[0]

    p4 = nc.alloc_psum_tensor("p4", [128, 2048], F32)
    p2 = nc.alloc_psum_tensor("p2", [128, 1024], F32)
    p1a = nc.alloc_psum_tensor("p1a", [128, 512], F32)
    p1b = nc.alloc_psum_tensor("p1b", [128, 512], F32)

    estack = contextlib.ExitStack()
    mk = lambda n: Sem(estack.enter_context(nc.semaphore(n)))
    SY = mk("SY"); GP = mk("GP"); VE = mk("VE"); SC = mk("SC"); TE = mk("TE")
    CC = mk("CC")
    ALLS = [SY, GP, VE, SC, TE, CC]

    q = Q(self_sems={"vector": VE, "scalar": SC})

    def dma(eng, out, in_, sem=None):
        s = sem if sem is not None else (GP if eng == "gpsimd" else SY)
        q.emit(eng, lambda e, o=out, i=in_: e.dma_start(out=o, in_=i), inc=s, inck=16)
        q.wait(eng, s)

    def preload(fn):
        pass

    dma("sync", qoff_sb[:], qoff_h[:])
    dma("sync", cbias_sb[:], cbias_h[:])
    nc.const_aps.register(F32, float(RND), cbias_sb[:, 3:4])
    nc.const_aps.register(F32, float(-RND), cbias_sb[:, 4:5])
    nc.const_aps.register(F32, 1e-5, cbias_sb[:, 0:1])
    nc.const_aps.register(F32, 1e-30, cbias_sb[:, 1:2])
    nc.const_aps.register(F32, float(np.log(0.125)), cbias_sb[:, 2:3])
    dma("sync", onesr_f[:], onesr_h[:])
    dma("gpsimd", onesr_bf[:], onesr_h[:])
    dma("gpsimd", onesc_bf[:], onesc_h[:])
    dma("sync", onesc_f[:], onesc_h[:])
    dma("gpsimd", idm_bf[:], idm_h[:])
    dma("sync", idm_f[:], idm_h[:])
    dma("sync", outb_sb[:], outb_h[:])
    dma("sync", grow_sb[:], gr_h[:])
    dma("sync", bcol_sb[:], bc_h[:])
    for ch in "LR":
        dma("sync", dtb_sb[ch][:], dtb_h[ch][:])
        dma("sync", dv_sb[ch][:], dv_h[ch][:])
        dma("sync", convb_sb[ch][:], convb_h[ch][:])
    for i in range(8):
        dma("sync", x_sb[i][:], xin_h[i])
    q.barrier(ALLS)
    preload(AF.Exp)
    for ch in "LR":
        q.wait("sync", SC)
        dma("sync", rowA[0:1, 0:16], alog_h[ch][:])
        q.wait("scalar", SY)
        q.wait("scalar", TE)
        q.emit("scalar", lambda e: e.activation(
            out=rowB[0:1, 0:16], in_=rowA[0:1, 0:16], func=AF.Exp), inc=SC)
        q.wait("vector", SC)
        q.emit("vector", lambda e: e.tensor_scalar_mul(
            out=rowB[0:1, 0:16], in0=rowB[0:1, 0:16], scalar1=-1.0), inc=VE)
        q.wait("tensor", VE)
        q.wait("tensor", SC)
        q.emit("tensor", lambda e: e.matmul(
            out=p1a[0:128, 0:16], lhsT=onesr_f[:], rhs=rowB[0:1, 0:16],
            start=True, stop=True), inc=TE)
        q.wait("scalar", TE)
        q.emit("scalar", lambda e, ch=ch: e.copy(
            out=avec_sb[ch][:], in_=p1a[:, 0:16]), inc=SC)
    q.barrier(ALLS)

    def layer_norm(W, src_fn, outs, out32=None, affine=False, parts=3, maxiter=999, double=False):
        nw = (W + 511) // 512
        it = 0
        for w in range(nw):
            ws = w * 512
            wn = min(512, W - ws)
            q.wait("tensor", VE)
            for dc in range(8):
                it += 1
                if it > maxiter:
                    return
                sap = src_fn(dc, ws, wn)
                ca, cb = casA, casB
                q.wait("scalar", TE)
                q.emit("scalar", lambda e, sap=sap, wn=wn, ca=ca: e.copy(
                    out=ca[:, 0:wn], in_=sap), inc=SC)
                q.emit("scalar", lambda e, sap=sap, wn=wn, cb=cb: e.activation(
                    out=cb[:, 0:wn], in_=sap, func=AF.Square), inc=SC)
                q.wait("tensor", SC)
                q.emit("tensor", lambda e, dc=dc, wn=wn, ca=ca: e.matmul(
                    out=p1a[0:1, 0:wn], lhsT=onesc_bf[:], rhs=ca[:, 0:wn],
                    start=(dc == 0), stop=(dc == 7)), inc=TE)
                q.emit("tensor", lambda e, dc=dc, wn=wn, cb=cb: e.matmul(
                    out=p1b[0:1, 0:wn], lhsT=onesc_bf[:], rhs=cb[:, 0:wn],
                    start=(dc == 0), stop=(dc == 7)), inc=TE)
            q.wait("vector", TE)
            q.emit("vector", lambda e, ws=ws, wn=wn: e.tensor_scalar_mul(
                out=rowA[0:1, ws:ws + wn], in0=p1a[0:1, 0:wn], scalar1=1.0 / D),
                inc=VE)
            q.emit("vector", lambda e, ws=ws, wn=wn: e.tensor_scalar_mul(
                out=rowB[0:1, ws:ws + wn], in0=p1b[0:1, 0:wn], scalar1=1.0 / D),
                inc=VE)
        if parts < 2:
            return
        q.wait("scalar", VE)
        for w in range(nw):
            ws = w * 512
            wn = min(512, W - ws)
            q.emit("scalar", lambda e, ws=ws, wn=wn: e.activation(
                out=lnt1[0:1, 0:wn], in_=rowA[0:1, ws:ws + wn],
                func=AF.Square), inc=SC)
            q.wait("vector", SC)
            q.emit("vector", lambda e, ws=ws, wn=wn: e.tensor_sub(
                out=rowB[0:1, ws:ws + wn], in0=rowB[0:1, ws:ws + wn],
                in1=lnt1[0:1, 0:wn]), inc=VE)
            q.wait("scalar", VE)
        if not double:
            q.emit("scalar", lambda e: e.activation(
                out=rowB[0:1, 0:W], in_=rowB[0:1, 0:W], func=AF.Ln,
                bias=cbias_sb[0:1, 0:1]), inc=SC)
            q.wait("scalar", SC)
            q.emit("scalar", lambda e: e.activation(
                out=rowB[0:1, 0:W], in_=rowB[0:1, 0:W], func=AF.Exp, scale=-0.5),
                inc=SC)
        else:
            for w in range(nw):
                ws = w * 512
                wn = min(512, W - ws)
                rB = lambda: rowB[0:1, ws:ws + wn]
                q.emit("scalar", lambda e, ws=ws, wn=wn: e.activation(
                    out=lnt1[0:1, 0:wn], in_=rowB[0:1, ws:ws + wn], func=AF.Ln,
                    bias=cbias_sb[0:1, 0:1]), inc=SC)
                q.emit("scalar", lambda e, ws=ws, wn=wn: e.activation(
                    out=rowB[0:1, ws:ws + wn], in_=rowB[0:1, ws:ws + wn],
                    func=AF.Ln), inc=SC)
                q.wait("vector", SC)
                q.emit("vector", lambda e, ws=ws, wn=wn: e.tensor_sub(
                    out=rowB[0:1, ws:ws + wn], in0=rowB[0:1, ws:ws + wn],
                    in1=lnt1[0:1, 0:wn]), inc=VE)
                q.wait("scalar", VE)
                q.emit("scalar", lambda e, ws=ws, wn=wn: e.activation(
                    out=rowB[0:1, ws:ws + wn], in_=rowB[0:1, ws:ws + wn],
                    func=AF.Exp), inc=SC)
                q.emit("scalar", lambda e, ws=ws, wn=wn: e.activation(
                    out=rowB[0:1, ws:ws + wn], in_=rowB[0:1, ws:ws + wn],
                    func=AF.Ln, bias=cbias_sb[0:1, 0:1]), inc=SC)
                q.wait("vector", SC)
                q.emit("vector", lambda e, ws=ws, wn=wn: e.tensor_add(
                    out=rowB[0:1, ws:ws + wn], in0=rowB[0:1, ws:ws + wn],
                    in1=lnt1[0:1, 0:wn]), inc=VE)
                q.wait("scalar", VE)
                q.emit("scalar", lambda e, ws=ws, wn=wn: e.activation(
                    out=rowB[0:1, ws:ws + wn], in_=rowB[0:1, ws:ws + wn],
                    func=AF.Exp, scale=-0.5), inc=SC)
        q.wait("vector", SC)
        q.emit("vector", lambda e: e.scalar_tensor_tensor(
            out=rowA[0:1, 0:W], in0=rowA[0:1, 0:W], scalar=-1.0,
            in1=rowB[0:1, 0:W], op0=ALU.mult, op1=ALU.mult), inc=VE)
        if parts < 3:
            return
        for w in range(nw):
            ws = w * 512
            wn = min(512, W - ws)
            for dc in range(8):
                sap = src_fn(dc, ws, wn)
                q.wait("tensor", VE)
                lhs_r = (grow_sb[0:1, dc * 128:(dc + 1) * 128] if affine
                         else onesr_f[0:1, :])
                q.emit("tensor", lambda e, lhs_r=lhs_r, ws=ws, wn=wn: e.matmul(
                    out=p1a[:, 0:wn], lhsT=lhs_r, rhs=rowB[0:1, ws:ws + wn],
                    start=True, stop=True), inc=TE)
                q.emit("tensor", lambda e, lhs_r=lhs_r, ws=ws, wn=wn: e.matmul(
                    out=p1b[:, 0:wn], lhsT=lhs_r, rhs=rowA[0:1, ws:ws + wn],
                    start=True, stop=True), inc=TE)
                q.wait("vector", TE)
                q.emit("vector", lambda e, sap=sap, wn=wn: e.tensor_mul(
                    out=lnt1[:, 0:wn], in0=sap, in1=p1a[:, 0:wn]), inc=VE)
                if affine:
                    q.emit("vector", lambda e, dc=dc, ws=ws, wn=wn: e.scalar_tensor_tensor(
                        out=outs[dc][:, ws:ws + wn], in0=lnt1[:, 0:wn],
                        scalar=bcol_sb[:, dc:dc + 1], in1=p1b[:, 0:wn],
                        op0=ALU.add, op1=ALU.add), inc=VE)
                else:
                    q.emit("vector", lambda e, dc=dc, ws=ws, wn=wn: e.tensor_add(
                        out=outs[dc][:, ws:ws + wn], in0=lnt1[:, 0:wn],
                        in1=p1b[:, 0:wn]), inc=VE)
                if out32 is not None:
                    q.emit("vector", lambda e, dc=dc, ws=ws, wn=wn: e.tensor_add(
                        out=out32[dc][:, ws:ws + wn], in0=lnt1[:, 0:wn],
                        in1=p1b[:, 0:wn]), inc=VE)

    cur = [ARENA0]
    lnkv = alloc("lnkv", [128, 8 * KVT], BF16, cur)
    lnkv_c = [lnkv[:, i * KVT:(i + 1) * KVT] for i in range(8)]
    lnown = alloc("lnown", [128, 8 * T], BF16, cur)
    lnown_c = [lnown[:, i * T:(i + 1) * T] for i in range(8)]
    MOE_ARENA = list(cur)
    kvring = [alloc(f"kvr{i}", [128, 512], F32, cur) for i in range(4)]
    kv_cnt = [0]

    def kv_src(dc, ws, wn):
        i = kv_cnt[0] % 4
        kv_cnt[0] += 1
        q.wait("sync", SC)
        q.wait("sync", VE)
        dma("sync", kvring[i][:, 0:wn], xkv_h[dc][:, ws:ws + wn])
        q.wait("scalar", SY)
        q.wait("vector", SY)
        return kvring[i][:, 0:wn]

    if stage_limit == 18:
        q.barrier(ALLS)
    if stage_limit == 19:
        preload(AF.Exp)
    if stage_limit == 14 or 20 <= stage_limit <= 60:
        preload(AF.Exp)
        layer_norm(T, lambda dc, ws, wn: x_sb[dc][:, ws:ws + wn], lnown_c,
                   parts=1, maxiter=(999 if stage_limit == 14 else stage_limit - 20))
        q.barrier(ALLS)
    elif stage_limit >= 1:
        preload(AF.Exp)
        layer_norm(KVT, kv_src, lnkv_c,
                   parts=(1 if stage_limit == 11 else 2 if stage_limit == 12 else 3))
        q.barrier(ALLS)
        if stage_limit not in (11, 12, 13):
            layer_norm(T, lambda dc, ws, wn: x_sb[dc][:, ws:ws + wn], lnown_c)
            q.barrier(ALLS)

    if 2 <= stage_limit < 11:
        acur = list(MOE_ARENA)
        Y2 = alloc("Y2", [128, T], BF16, acur)
        nB = alloc("nB", [128, 2 * 16], F32, acur)
        expb = alloc("expb", [128, 640], BF16, acur)
        mexp = alloc("mexp", [128, 640], BF16, acur)
        attnT = alloc("attnT", [128, 640], BF16, acur)
        rsum = alloc("rsum", [128, 1], F32, acur)
        rrec = alloc("rrec", [128, 1], F32, acur)
        amsk = alloc("amsk", [128, 2 * 640], BF16, acur)
        wqp = alloc("wqp", [128, 3 * 8 * 128], BF16, acur)
        woutp = alloc("woutp", [128, 1024], BF16, acur)
        Q2 = alloc("Q2", [128, T], BF16, acur)
        K2 = alloc("K2", [128, KVT], BF16, acur)
        V2 = alloc("V2", [128, 20 * 128], BF16, acur)
        preload(AF.Exp)
        for hp in range(8):
            q.wait("gpsimd", TE)
            dma("gpsimd", wqp[:], wqkv_h[hp])
            dma("gpsimd", woutp[:], wout_h[hp])
            q.wait("tensor", GP)
            q.wait("tensor", SY)
            for tc in range(4):
                q.wait("tensor", SC)
                q.wait("tensor", VE)
                for kc in range(8):
                    q.emit("tensor", lambda e, kc=kc, tc=tc: e.matmul(
                        out=p1a[:], lhsT=wqp[:, kc * 128:(kc + 1) * 128],
                        rhs=lnown_c[kc][:, tc * 512:(tc + 1) * 512],
                        start=(kc == 0), stop=(kc == 7)),
                        inc=(TE if kc == 7 else None))
                q.wait("scalar", TE)
                q.emit("scalar", lambda e, tc=tc: e.copy(
                    out=Q2[:, tc * 512:(tc + 1) * 512], in_=p1a[:]), inc=SC)
            for tk in range(5):
                q.wait("tensor", SC)
                q.wait("tensor", VE)
                for kc in range(8):
                    q.emit("tensor", lambda e, kc=kc, tk=tk: e.matmul(
                        out=p1b[:], lhsT=wqp[:, 1024 + kc * 128:1024 + (kc + 1) * 128],
                        rhs=lnkv_c[kc][:, tk * 512:(tk + 1) * 512],
                        start=(kc == 0), stop=(kc == 7)),
                        inc=(TE if kc == 7 else None))
                q.wait("scalar", TE)
                q.emit("scalar", lambda e, tk=tk: e.copy(
                    out=K2[:, tk * 512:(tk + 1) * 512], in_=p1b[:]), inc=SC)
            for kb in range(20):
                q.wait("tensor", SC)
                q.wait("tensor", VE)
                for kc in range(8):
                    q.emit("tensor", lambda e, kb=kb, kc=kc: e.matmul(
                        out=p1a[:, 0:128],
                        lhsT=lnkv_c[kc][:, kb * 128:(kb + 1) * 128],
                        rhs=wqp[:, 2048 + kc * 128:2048 + (kc + 1) * 128],
                        start=(kc == 0), stop=(kc == 7)),
                        inc=(TE if kc == 7 else None))
                q.wait("scalar", TE)
                q.emit("scalar", lambda e, kb=kb: e.copy(
                    out=V2[:, kb * 128:(kb + 1) * 128], in_=p1a[:, 0:128]), inc=SC)
            for hd in range(2):
                dst = rowA if hd == 0 else rowB
                for tc in range(4):
                    q.wait("scalar", TE)
                    q.emit("scalar", lambda e, hd=hd, tc=tc: e.activation(
                        out=casA[hd * 64:hd * 64 + 64, :],
                        in_=Q2[hd * 64:hd * 64 + 64, tc * 512:(tc + 1) * 512],
                        func=AF.Square), inc=SC)
                    q.wait("tensor", SC)
                    q.wait("tensor", VE)
                    q.emit("tensor", lambda e, hd=hd, tc=tc: e.matmul(
                        out=p1a[0:1, 0:512],
                        lhsT=onesc_bf[hd * 64:hd * 64 + 64, :],
                        rhs=casA[hd * 64:hd * 64 + 64, :],
                        start=True, stop=True), inc=TE)
                    q.wait("vector", TE)
                    q.emit("vector", lambda e, dst=dst, tc=tc: e.tensor_copy(
                        out=dst[0:1, tc * 512:(tc + 1) * 512],
                        in_=p1a[0:1, 0:512]), inc=VE)
            for hd in range(2):
                nq = rowA if hd == 0 else rowB
                for tk in range(5):
                    q.wait("scalar", TE)
                    q.emit("scalar", lambda e, hd=hd, tk=tk: e.activation(
                        out=casB[hd * 64:hd * 64 + 64, :],
                        in_=K2[hd * 64:hd * 64 + 64, tk * 512:(tk + 1) * 512],
                        func=AF.Square), inc=SC)
                    q.wait("tensor", SC)
                    q.wait("tensor", VE)
                    q.emit("tensor", lambda e, hd=hd, tk=tk: e.matmul(
                        out=p1b[0:1, 0:512],
                        lhsT=onesc_bf[hd * 64:hd * 64 + 64, :],
                        rhs=casB[hd * 64:hd * 64 + 64, :],
                        start=True, stop=True), inc=TE)
                    q.wait("vector", TE)
                    q.emit("vector", lambda e, nq=nq, tk=tk: e.reduce_max(
                        out=nq[0:1, 2049 + tk:2050 + tk], in_=p1b[0:1, 0:512],
                        axis=mybir.AxisListType.X), inc=VE)
                q.emit("vector", lambda e, nq=nq: e.reduce_max(
                    out=nq[0:1, 2056:2057], in_=nq[0:1, 2049:2054],
                    axis=mybir.AxisListType.X), inc=VE)
                q.emit("vector", lambda e, nq=nq: e.tensor_single_scalar(
                    out=nq[0:1, 0:T], in_=nq[0:1, 0:T],
                    scalar=nq[0:1, 2056:2057], op=ALU.mult), inc=VE)
                q.wait("scalar", VE)
                q.emit("scalar", lambda e, nq=nq: e.activation(
                    out=nq[0:1, 0:T], in_=nq[0:1, 0:T], func=AF.Ln,
                    bias=cbias_sb[0:1, 1:2]), inc=SC)
                q.wait("scalar", SC)
                q.emit("scalar", lambda e, nq=nq: e.activation(
                    out=nq[0:1, 0:T], in_=nq[0:1, 0:T], func=AF.Exp,
                    scale=0.5, bias=cbias_sb[0:1, 2:3]), inc=SC)
                q.wait("vector", SC)
                q.emit("vector", lambda e, nq=nq: e.tensor_scalar_mul(
                    out=rowbf[0:1, 0:T], in0=nq[0:1, 0:T], scalar1=-1.0), inc=VE)
                q.wait("tensor", VE)
                for qb in range(16):
                    q.emit("tensor", lambda e, qb=qb: e.transpose(
                        out=p1a.bitcast(BF16)[0:128, 0:1],
                        in_=rowbf[0:1, qb * 128: qb * 128 + 128],
                        identity=idm_bf[0:1, 0:1]), inc=TE)
                    q.wait("scalar", TE)
                    q.emit("scalar", lambda e, hd=hd, qb=qb: e.copy(
                        out=nB[:, hd * 16 + qb:hd * 16 + qb + 1],
                        in_=p1a.bitcast(BF16)[0:128, 0:1]), inc=SC)
                    q.wait("tensor", SC)
            for qb in range(16):
                mtile = amsk[:, (qb % 2) * 640:(qb % 2) * 640 + 640]
                q.wait("gpsimd", VE)
                dma("gpsimd", mtile, am_h[qb])
                q.wait("vector", GP)
                for hd in range(2):
                    b0 = hd * 64
                    q.wait("tensor", SC)
                    q.emit("tensor", lambda e, b0=b0, qb=qb: e.matmul(
                        out=p2[:, 0:512],
                        lhsT=Q2[b0:b0 + 64, qb * 128:(qb + 1) * 128],
                        rhs=K2[b0:b0 + 64, qb * 128:qb * 128 + 512],
                        start=True, stop=True), inc=None)
                    q.emit("tensor", lambda e, b0=b0, qb=qb: e.matmul(
                        out=p2[:, 512:640],
                        lhsT=Q2[b0:b0 + 64, qb * 128:(qb + 1) * 128],
                        rhs=K2[b0:b0 + 64, qb * 128 + 512:qb * 128 + 640],
                        start=True, stop=True), inc=TE)
                    q.wait("scalar", TE)
                    q.emit("scalar", lambda e, hd=hd, qb=qb: e.activation(
                        out=expb[:], in_=p2[:, 0:640], func=AF.Exp, scale=0.125,
                        bias=nB[:, hd * 16 + qb:hd * 16 + qb + 1]), inc=SC)
                    q.wait("vector", SC)
                    q.emit("vector", lambda e, mtile=mtile: e.scalar_tensor_tensor(
                        out=mexp[:], in0=expb[:], scalar=1.0, in1=mtile,
                        op0=ALU.mult, op1=ALU.mult, accum_out=rsum[:]), inc=VE)
                    q.emit("vector", lambda e: e.reciprocal(
                        out=rrec[:], in_=rsum[:]), inc=VE)
                    q.emit("vector", lambda e: e.tensor_scalar_mul(
                        out=mexp[:], in0=mexp[:], scalar1=rrec[:]), inc=VE)
                    q.wait("tensor", VE)
                    for kb in range(5):
                        q.emit("tensor", lambda e, kb=kb: e.transpose(
                            out=p1a.bitcast(BF16)[:, kb * 128:(kb + 1) * 128],
                            in_=mexp[:, kb * 128:(kb + 1) * 128],
                            identity=idm_bf[:]), inc=(TE if kb == 4 else None))
                    q.wait("scalar", TE)
                    q.emit("scalar", lambda e: e.copy(
                        out=attnT[:], in_=p1a.bitcast(BF16)[:, 0:640]), inc=SC)
                    q.wait("tensor", SC)
                    for kb in range(5):
                        q.emit("tensor", lambda e, kb=kb, qb=qb, b0=b0: e.matmul(
                            out=p1b[b0:b0 + 64, 0:128],
                            lhsT=V2[:, (qb + kb) * 128 + b0:(qb + kb) * 128 + b0 + 64],
                            rhs=attnT[:, kb * 128:(kb + 1) * 128],
                            start=(kb == 0), stop=(kb == 4)),
                            inc=(TE if kb == 4 else None))
                    q.wait("scalar", TE)
                    q.emit("scalar", lambda e, qb=qb, b0=b0: e.copy(
                        out=Y2[b0:b0 + 64, qb * 128:(qb + 1) * 128],
                        in_=p1b[b0:b0 + 64, 0:128]), inc=SC)
            q.wait("tensor", SC)
            for tc in range(4):
                for dc in range(8):
                    q.wait("tensor", VE)
                    q.emit("tensor", lambda e, tc=tc, dc=dc: e.matmul(
                        out=p1a[:], lhsT=woutp[:, dc * 128:(dc + 1) * 128],
                        rhs=Y2[:, tc * 512:(tc + 1) * 512],
                        start=True, stop=True), inc=TE)
                    q.wait("vector", TE)
                    q.emit("vector", lambda e, tc=tc, dc=dc: e.tensor_add(
                        out=x_sb[dc][:, tc * 512:(tc + 1) * 512],
                        in0=x_sb[dc][:, tc * 512:(tc + 1) * 512],
                        in1=p1a[:]), inc=VE)
        q.barrier(ALLS)

    if 3 <= stage_limit < 11:
        mcur = [ARENA0]
        lnf = alloc("lnf", [128, 8 * T], BF16, mcur)
        lnf_c = [lnf[:, i * T:(i + 1) * T] for i in range(8)]
        MSNAP = list(mcur)
        lgbuf = alloc("lgbuf", [8, T], F32, mcur)
        lg_tm = alloc("lg_tm", [128, 8], F32, mcur)
        mcolA = alloc("mcolA", [128, 1], F32, mcur)
        mcolB = alloc("mcolB", [128, 1], F32, mcur)
        mcolC = alloc("mcolC", [128, 8], F32, mcur)
        cwf = alloc("cwf", [8, T], F32, mcur)
        cwb0 = alloc("cwb0", [128, 8], F32, mcur)
        ecur = list(MSNAP)
        cwrow = alloc("cwrow", [1, T], F32, ecur)
        cwb = alloc("cwb", [128, 256], BF16, ecur)
        wgl_sb = alloc("wgl_sb", [128, 64], BF16, mcur)
        wgu_sb = alloc("wgu_sb", [128, 8 * 16 * 128], BF16, ecur)
        wdn_sb = alloc("wdn_sb", [128, 8 * 8 * 128], BF16, ecur)
        gact = alloc("gact", [128, 256], BF16, ecur)
        uact = alloc("uact", [128, 256], BF16, ecur)
        act = [alloc(f"act{i}", [128, 256], BF16, ecur) for i in range(8)]

        layer_norm(T, lambda dc, ws, wn: x_sb[dc][:, ws:ws + wn], lnf_c)
        dma("gpsimd", wgl_sb[:], wgl_h[:])
        q.barrier(ALLS)
        for tc4 in range(4):
            q.wait("tensor", SC)
            for kc in range(8):
                q.emit("tensor", lambda e, tc4=tc4, kc=kc: e.matmul(
                    out=p1a[0:8, 0:512], lhsT=wgl_sb[:, kc * 8:(kc + 1) * 8],
                    rhs=lnf_c[kc][:, tc4 * 512:(tc4 + 1) * 512],
                    start=(kc == 0), stop=(kc == 7)),
                    inc=(TE if kc == 7 else None))
            q.wait("scalar", TE)
            q.emit("scalar", lambda e, tc4=tc4: e.copy(
                out=lgbuf[:, tc4 * 512:(tc4 + 1) * 512], in_=p1a[0:8, 0:512]),
                inc=SC)
        for tb in range(16):
            q.wait("tensor", SC)
            q.wait("tensor", VE)
            q.emit("tensor", lambda e, tb=tb: e.transpose(
                out=p1b[0:128, 0:8], in_=lgbuf[:, tb * 128:(tb + 1) * 128],
                identity=idm_f[0:8, 0:8]), inc=TE)
            q.wait("vector", TE)
            q.emit("vector", lambda e: e.tensor_copy(out=lg_tm[:], in_=p1b[:, 0:8]),
                   inc=VE)
            q.emit("vector", lambda e: e.reduce_max(
                out=mcolA[:], in_=lg_tm[:], axis=mybir.AxisListType.X,
                negate=True), inc=VE)
            q.wait("scalar", VE)
            q.emit("scalar", lambda e: e.activation(
                out=lg_tm[:], in_=lg_tm[:], func=AF.Exp, bias=mcolA[:],
                accum_out=mcolB[:]), inc=SC)
            q.wait("vector", SC)
            q.emit("vector", lambda e: e.reciprocal(out=mcolB[:], in_=mcolB[:]),
                   inc=VE)
            q.emit("vector", lambda e: e.tensor_scalar_mul(
                out=lg_tm[:], in0=lg_tm[:], scalar1=mcolB[:]), inc=VE)
            q.emit("vector", lambda e: e.reduce_max(
                out=mcolA[:], in_=lg_tm[:], axis=mybir.AxisListType.X), inc=VE)
            q.emit("vector", lambda e: e.tensor_scalar(
                out=mcolC[:, 0:8], in0=lg_tm[:], scalar1=mcolA[:], scalar2=None,
                op0=ALU.is_equal), inc=VE)
            q.emit("vector", lambda e: e.tensor_scalar(
                out=mcolC[:, 0:8], in0=mcolC[:, 0:8], scalar1=-1.0, scalar2=1.0,
                op0=ALU.mult, op1=ALU.add), inc=VE)
            q.emit("vector", lambda e: e.tensor_mul(
                out=mcolC[:, 0:8], in0=mcolC[:, 0:8], in1=lg_tm[:]), inc=VE)
            q.emit("vector", lambda e: e.reduce_max(
                out=mcolB[:], in_=mcolC[:, 0:8], axis=mybir.AxisListType.X),
                inc=VE)
            q.emit("vector", lambda e: e.scalar_tensor_tensor(
                out=cwb0[:, 1:2], in0=mcolA[:], scalar=1e-8, in1=mcolB[:],
                op0=ALU.add, op1=ALU.add), inc=VE)
            q.emit("vector", lambda e: e.reciprocal(
                out=cwb0[:, 1:2], in_=cwb0[:, 1:2]), inc=VE)
            q.emit("vector", lambda e: e.tensor_mul(
                out=cwb0[:, 2:3], in0=mcolA[:], in1=cwb0[:, 1:2]), inc=VE)
            q.emit("vector", lambda e: e.tensor_mul(
                out=cwb0[:, 3:4], in0=mcolB[:], in1=cwb0[:, 1:2]), inc=VE)
            q.emit("vector", lambda e: e.tensor_scalar(
                out=mcolC[:, 0:8], in0=mcolC[:, 0:8], scalar1=mcolB[:],
                scalar2=None, op0=ALU.is_equal), inc=VE)
            q.emit("vector", lambda e: e.tensor_scalar_mul(
                out=mcolC[:, 0:8], in0=mcolC[:, 0:8], scalar1=cwb0[:, 3:4]),
                inc=VE)
            q.emit("vector", lambda e: e.tensor_scalar(
                out=cwb0[:, 8:16] if False else lg_tm[:], in0=lg_tm[:],
                scalar1=mcolA[:], scalar2=None, op0=ALU.is_equal), inc=VE)
            q.emit("vector", lambda e: e.tensor_scalar_mul(
                out=lg_tm[:], in0=lg_tm[:], scalar1=cwb0[:, 2:3]), inc=VE)
            q.emit("vector", lambda e: e.tensor_add(
                out=mcolC[:, 0:8], in0=mcolC[:, 0:8], in1=lg_tm[:]), inc=VE)
            q.wait("tensor", VE)
            q.emit("tensor", lambda e, tb=tb: e.transpose(
                out=p1b[0:8, 0:128], in_=mcolC[:, 0:8],
                identity=idm_f[:]), inc=TE)
            q.wait("scalar", TE)
            q.emit("scalar", lambda e, tb=tb: e.copy(
                out=cwf[:, tb * 128:(tb + 1) * 128], in_=p1b[0:8, 0:128]), inc=SC)
        q.wait("sync", SC)
        dma("sync", bncCW[:], cwf[:])
        q.barrier(ALLS)
        for ex in range(9):
            q.wait("gpsimd", TE)
            dma("gpsimd", wgu_sb[:], wgu_h[ex])
            dma("gpsimd", wdn_sb[:], wdn_h[ex])
            if ex < 8:
                q.wait("sync", TE)
                dma("sync", cwrow[:], bncCW[ex:ex + 1, :])
            q.wait("tensor", GP)
            q.wait("tensor", SY)
            for tc in range(8):
                if ex < 8:
                    q.wait("tensor", SC)
                    q.emit("tensor", lambda e, tc=tc: e.matmul(
                        out=p1b[:, 0:256], lhsT=onesr_f[:],
                        rhs=cwrow[0:1, tc * 256:(tc + 1) * 256],
                        start=True, stop=True), inc=TE)
                    q.wait("scalar", TE)
                    q.emit("scalar", lambda e: e.copy(
                        out=cwb[:], in_=p1b[:, 0:256]), inc=SC)
                for ic in range(8):
                    q.wait("tensor", SC)
                    q.wait("tensor", VE)
                    for kc in range(8):
                        q.emit("tensor", lambda e, tc=tc, ic=ic, kc=kc: e.matmul(
                            out=p2[:, 0:256],
                            lhsT=wgu_sb[:, (kc * 16 + ic) * 128:(kc * 16 + ic + 1) * 128],
                            rhs=lnf_c[kc][:, tc * 256:(tc + 1) * 256],
                            start=(kc == 0), stop=(kc == 7)), inc=None)
                        q.emit("tensor", lambda e, tc=tc, ic=ic, kc=kc: e.matmul(
                            out=p2[:, 512:768],
                            lhsT=wgu_sb[:, (kc * 16 + 8 + ic) * 128:(kc * 16 + 9 + ic) * 128],
                            rhs=lnf_c[kc][:, tc * 256:(tc + 1) * 256],
                            start=(kc == 0), stop=(kc == 7)),
                            inc=(TE if kc == 7 else None))
                    q.wait("scalar", TE)
                    q.emit("scalar", lambda e: e.activation(
                        out=gact[:], in_=p2[:, 0:256], func=AF.Silu), inc=SC)
                    q.emit("scalar", lambda e: e.copy(
                        out=uact[:], in_=p2[:, 512:768]), inc=SC)
                    q.wait("vector", SC)
                    q.emit("vector", lambda e: e.tensor_mul(
                        out=gact[:], in0=gact[:], in1=uact[:]), inc=VE)
                    if ex < 8:
                        q.emit("vector", lambda e, ic=ic: e.tensor_mul(
                            out=act[ic][:], in0=gact[:], in1=cwb[:]), inc=VE)
                    else:
                        q.emit("vector", lambda e, ic=ic: e.tensor_copy(
                            out=act[ic][:], in_=gact[:]), inc=VE)
                for dc in range(8):
                    q.wait("tensor", VE)
                    for ic in range(8):
                        q.emit("tensor", lambda e, dc=dc, ic=ic: e.matmul(
                            out=p1a[:, 0:256],
                            lhsT=wdn_sb[:, (ic * 8 + dc) * 128:(ic * 8 + dc + 1) * 128],
                            rhs=act[ic][:], start=(ic == 0), stop=(ic == 7)),
                            inc=(TE if ic == 7 else None))
                    q.wait("vector", TE)
                    q.emit("vector", lambda e, dc=dc, tc=tc: e.tensor_add(
                        out=x_sb[dc][:, tc * 256:(tc + 1) * 256],
                        in0=x_sb[dc][:, tc * 256:(tc + 1) * 256],
                        in1=p1a[:, 0:256]), inc=VE)
        q.barrier(ALLS)


    if 4 <= stage_limit < 11:
        ccur = [ARENA0]
        hlnc = alloc("hlnc", [128, 8 * CH], BF16, ccur)
        hlnc_c = [hlnc[:, i * CH:(i + 1) * CH] for i in range(8)]
        xit = [alloc(f"xit{i}", [128, 515], BF16, ccur) for i in range(2)]
        xcb = alloc("xcb", [128, 16 * CH], BF16, ccur)
        dtrows = alloc("dtrows", [64, CH], BF16, ccur)
        Bsb = alloc("Bsb", [16, CH], BF16, ccur)
        Csb = alloc("Csb", [16, CH], BF16, ccur)
        nsel_sb = alloc("nsel_sb", [16, 8 * 128], BF16, ccur)
        Brep = alloc("Brep", [128, 16 * CH], BF16, ccur)
        Crep = alloc("Crep", [128, 16 * CH], BF16, ccur)
        dtile = alloc("dtile", [128, CH], BF16, ccur)
        dutile = alloc("dutile", [128, CH], BF16, ccur)
        atile = alloc("atile", [128, CH], BF16, ccur)
        wtile = alloc("wtile", [128, CH], BF16, ccur)
        htile = alloc("htile", [128, CH], BF16, ccur)
        hctmp = alloc("hctmp", [128, CH], BF16, ccur)
        yacc = alloc("yacc", [128, 8 * CH], BF16, ccur)
        ytile = lnt1[:, 0:256].bitcast(BF16)[:, 0:CH]
        zst = alloc("zst", [128, CH], BF16, ccur)
        ygated = xcb
        winr = alloc("winr", [128, 2048], BF16, ccur)
        zwr = alloc("zwr", [128, 1024], BF16, ccur)
        moutr = alloc("moutr", [128, 2048], BF16, ccur)
        xprojw = alloc("xprojw", [128, 16 * 96], BF16, ccur)
        dtwsb = alloc("dtwsb", [64, DI], BF16, ccur)
        convr = alloc("convr", [128, 1024], BF16, ccur)
        fwdt = alloc("fwdt", [128, CH], BF16, ccur)
        fwdin = alloc("fwdin", [128, 8 * CH], BF16, ccur)
        owr = alloc("owr", [128, 8 * 128], BF16, ccur)
        kcur = [ARENA0]
        gath = alloc("gath", [128, 8 * 304], F32, kcur)
        pselbig_sb = alloc("pselbig_sb", [128, 8 * 304], F32, kcur)
        ccbuf = alloc("ccbuf", [128, 304], F32, kcur)
        ext_sb = alloc("ext_sb", [128, 304], F32, kcur)

        def chain(ch, phase):
            chunks = range(NCH) if phase == 0 else range(NCH - 1, -1, -1)
            cdb = 0 if ch == "L" else 1
            q.wait("gpsimd", TE)
            dma("gpsimd", xprojw[:], xproj_h[ch][:])
            dma("gpsimd", dtwsb[:], dtw_h[ch][:])
            rv = slice(None, None, -1) if phase == 1 else slice(None, None, None)
            for cc in chunks:
                t0 = cc * CH
                layer_norm(CH, lambda dc, ws, wn, t0=t0:
                           x_sb[dc][:, t0 + ws:t0 + ws + wn],
                           hlnc_c, affine=True, double=True)
                q.wait("tensor", VE)
                for dc in range(16):
                    g = dc % 2
                    if g == 0:
                        q.wait("gpsimd", TE)
                        dma("gpsimd", winr[:], win_h[ch][dc // 2])
                        dma("gpsimd", convr[:], convd_h[cdb, dc // 2])
                        q.wait("tensor", GP)
                    xi = xit[g]
                    q.wait("scalar", TE)
                    if phase == 0:
                        q.emit("scalar", lambda e, dc=dc, ch=ch, xi=xi: e.copy(
                            out=xi[:, 0:3], in_=ximg[ch][:, dc * 3:dc * 3 + 3]),
                            inc=SC)
                    else:
                        q.emit("scalar", lambda e, dc=dc, ch=ch, xi=xi: e.copy(
                            out=xi[:, 512:515], in_=ximg[ch][:, dc * 3:dc * 3 + 3]),
                            inc=SC)
                    q.wait("tensor", SC)
                    for kc in range(8):
                        q.emit("tensor", lambda e, g=g, kc=kc: e.matmul(
                            out=p2[:, 0:512],
                            lhsT=winr[:, (kc * 2 + g) * 128:(kc * 2 + g + 1) * 128],
                            rhs=hlnc_c[kc], start=(kc == 0), stop=(kc == 7)),
                            inc=(TE if kc == 7 else None))
                    q.wait("scalar", TE)
                    xoff = 3 if phase == 0 else 0
                    q.emit("scalar", lambda e, xi=xi, xoff=xoff: e.copy(
                        out=xi[:, xoff:xoff + 512], in_=p2[:, 0:512]), inc=SC)
                    moff = 512 if phase == 0 else 0
                    q.emit("scalar", lambda e, xi=xi, dc=dc, ch=ch, moff=moff: e.copy(
                        out=ximg[ch][:, dc * 3:dc * 3 + 3], in_=xi[:, moff:moff + 3]),
                        inc=SC)
                    q.wait("tensor", SC)
                    for j in range(4):
                        q.emit("tensor", lambda e, g=g, j=j, xi=xi: e.matmul(
                            out=p1a[:, 0:512],
                            lhsT=convr[:, (g * 4 + j) * 128:(g * 4 + j + 1) * 128],
                            rhs=xi[:, j:j + 512], start=(j == 0), stop=(j == 3)),
                            inc=(TE if j == 3 else None))
                    q.wait("scalar", TE)
                    q.emit("scalar", lambda e, dc=dc, ch=ch: e.activation(
                        out=xcb[:, dc * CH:(dc + 1) * CH], in_=p1a[:, 0:512],
                        func=AF.Silu, bias=convb_sb[ch][:, dc:dc + 1]), inc=SC)
                q.wait("tensor", SC)
                q.wait("tensor", VE)
                for kc in range(16):
                    q.emit("tensor", lambda e, kc=kc: e.matmul(
                        out=p4[0:64, 0:512], lhsT=xprojw[:, kc * 96:kc * 96 + 64],
                        rhs=xcb[:, kc * CH:(kc + 1) * CH],
                        start=(kc == 0), stop=(kc == 15)), inc=None)
                    q.emit("tensor", lambda e, kc=kc: e.matmul(
                        out=p4[0:16, 512:1024],
                        lhsT=xprojw[:, kc * 96 + 64:kc * 96 + 80],
                        rhs=xcb[:, kc * CH:(kc + 1) * CH],
                        start=(kc == 0), stop=(kc == 15)), inc=None)
                    q.emit("tensor", lambda e, kc=kc: e.matmul(
                        out=p4[0:16, 1024:1536],
                        lhsT=xprojw[:, kc * 96 + 80:kc * 96 + 96],
                        rhs=xcb[:, kc * CH:(kc + 1) * CH],
                        start=(kc == 0), stop=(kc == 15)),
                        inc=(TE if kc == 15 else None))
                q.wait("scalar", TE)
                q.emit("scalar", lambda e: e.copy(out=dtrows[:], in_=p4[0:64, 0:512]),
                       inc=SC)
                q.emit("scalar", lambda e: e.copy(out=Bsb[:], in_=p4[0:16, 512:1024]),
                       inc=SC)
                q.emit("scalar", lambda e: e.copy(out=Csb[:], in_=p4[0:16, 1024:1536]),
                       inc=SC)
                for n in range(16):
                    if n % 8 == 0:
                        q.wait("gpsimd", TE)
                        dma("gpsimd", nsel_sb[:], nsel_h[:, n * 128:(n + 8) * 128])
                        q.wait("tensor", GP)
                    q.wait("tensor", SC)
                    q.emit("tensor", lambda e, n=n: e.matmul(
                        out=p1b[:, 0:512], lhsT=nsel_sb[:, (n % 8) * 128:(n % 8 + 1) * 128],
                        rhs=Bsb[:], start=True, stop=True), inc=TE)
                    q.wait("scalar", TE)
                    q.emit("scalar", lambda e, n=n: e.copy(
                        out=Brep[:, n * CH:(n + 1) * CH], in_=p1b[:, 0:512]), inc=SC)
                    q.wait("tensor", SC)
                    q.emit("tensor", lambda e, n=n: e.matmul(
                        out=p1b[:, 0:512], lhsT=nsel_sb[:, (n % 8) * 128:(n % 8 + 1) * 128],
                        rhs=Csb[:], start=True, stop=True), inc=TE)
                    q.wait("scalar", TE)
                    q.emit("scalar", lambda e, n=n: e.copy(
                        out=Crep[:, n * CH:(n + 1) * CH], in_=p1b[:, 0:512]), inc=SC)
                for dc in range(16):
                    q.wait("tensor", SC)
                    q.wait("tensor", VE)
                    q.emit("tensor", lambda e, dc=dc: e.matmul(
                        out=p1b[:, 0:512], lhsT=dtwsb[:, dc * 128:(dc + 1) * 128],
                        rhs=dtrows[:], start=True, stop=True), inc=TE)
                    q.wait("scalar", TE)
                    q.emit("scalar", lambda e, dc=dc, ch=ch: e.activation(
                        out=lnt1[:, 0:512], in_=p1b[:, 0:512], func=AF.Exp,
                        bias=dtb_sb[ch][:, dc:dc + 1]), inc=SC)
                    q.wait("scalar", SC)
                    q.emit("scalar", lambda e: e.activation(
                        out=dtile[:], in_=lnt1[:, 0:512], func=AF.Ln, bias=1.0),
                        inc=SC)
                    q.wait("vector", SC)
                    q.emit("vector", lambda e, dc=dc: e.tensor_mul(
                        out=dutile[:], in0=dtile[:],
                        in1=xcb[:, dc * CH:(dc + 1) * CH]), inc=VE)
                    for n in range(16):
                        q.wait("scalar", VE)
                        q.emit("scalar", lambda e, n=n, ch=ch: e.activation(
                            out=atile[:], in_=dtile[:], func=AF.Exp,
                            scale=avec_sb[ch][:, n:n + 1]), inc=SC)
                        q.wait("vector", SC)
                        q.emit("vector", lambda e, n=n: e.tensor_mul(
                            out=wtile[:], in0=dutile[:],
                            in1=Brep[:, n * CH:(n + 1) * CH]), inc=VE)
                        q.emit("vector", lambda e, n=n, dc=dc, ch=ch, rv=rv:
                               e.tensor_tensor_scan(
                            out=htile[:, rv], data0=atile[:, rv],
                            data1=wtile[:, rv],
                            initial=stbuf[ch][:, dc * 16 + n:dc * 16 + n + 1],
                            op0=ALU.mult, op1=ALU.add), inc=VE)
                        sv = 511 if phase == 0 else 0
                        q.wait("scalar", VE)
                        q.emit("scalar", lambda e, n=n, dc=dc, ch=ch, sv=sv: e.copy(
                            out=stbuf[ch][:, dc * 16 + n:dc * 16 + n + 1],
                            in_=htile[:, sv:sv + 1]), inc=SC)
                        q.wait("vector", SC)
                        if n % 2 == 0:
                            q.emit("vector", lambda e, n=n: e.tensor_mul(
                                out=hctmp[:], in0=htile[:],
                                in1=Crep[:, n * CH:(n + 1) * CH]), inc=VE)
                        else:
                            q.emit("vector", lambda e, n=n: e.tensor_mul(
                                out=htile[:], in0=htile[:],
                                in1=Crep[:, n * CH:(n + 1) * CH]), inc=VE)
                            q.emit("vector", lambda e, n=n: e.tensor_add(
                                out=yacc[:, (n // 2) * CH:(n // 2 + 1) * CH],
                                in0=hctmp[:], in1=htile[:]), inc=VE)
                    for st_, half in ((0, 2), (0, 4), (0, 6), (0, 1)):
                        pass
                    q.emit("vector", lambda e: e.tensor_add(
                        out=yacc[:, 0:CH], in0=yacc[:, 0:CH], in1=yacc[:, CH:2 * CH]),
                        inc=VE)
                    q.emit("vector", lambda e: e.tensor_add(
                        out=yacc[:, 2 * CH:3 * CH], in0=yacc[:, 2 * CH:3 * CH],
                        in1=yacc[:, 3 * CH:4 * CH]), inc=VE)
                    q.emit("vector", lambda e: e.tensor_add(
                        out=yacc[:, 4 * CH:5 * CH], in0=yacc[:, 4 * CH:5 * CH],
                        in1=yacc[:, 5 * CH:6 * CH]), inc=VE)
                    q.emit("vector", lambda e: e.tensor_add(
                        out=yacc[:, 6 * CH:7 * CH], in0=yacc[:, 6 * CH:7 * CH],
                        in1=yacc[:, 7 * CH:8 * CH]), inc=VE)
                    q.emit("vector", lambda e: e.tensor_add(
                        out=yacc[:, 0:CH], in0=yacc[:, 0:CH],
                        in1=yacc[:, 2 * CH:3 * CH]), inc=VE)
                    q.emit("vector", lambda e: e.tensor_add(
                        out=yacc[:, 4 * CH:5 * CH], in0=yacc[:, 4 * CH:5 * CH],
                        in1=yacc[:, 6 * CH:7 * CH]), inc=VE)
                    q.emit("vector", lambda e: e.tensor_add(
                        out=ytile[:], in0=yacc[:, 0:CH], in1=yacc[:, 4 * CH:5 * CH]),
                        inc=VE)
                    q.wait("gpsimd", TE)
                    dma("gpsimd", zwr[:], wz_h[ch][dc])
                    q.wait("tensor", GP)
                    q.wait("tensor", SC)
                    for kc in range(8):
                        q.emit("tensor", lambda e, kc=kc: e.matmul(
                            out=p2[:, 512:1024],
                            lhsT=zwr[:, kc * 128:(kc + 1) * 128],
                            rhs=hlnc_c[kc], start=(kc == 0), stop=(kc == 7)),
                            inc=(TE if kc == 7 else None))
                    q.wait("scalar", TE)
                    q.emit("scalar", lambda e: e.activation(
                        out=zst[:], in_=p2[:, 512:1024], func=AF.Silu), inc=SC)
                    q.wait("vector", SC)
                    q.emit("vector", lambda e, dc=dc, ch=ch: e.scalar_tensor_tensor(
                        out=hctmp[:], in0=xcb[:, dc * CH:(dc + 1) * CH],
                        scalar=dv_sb[ch][:, dc:dc + 1], in1=ytile[:],
                        op0=ALU.mult, op1=ALU.add), inc=VE)
                    q.emit("vector", lambda e, dc=dc: e.tensor_mul(
                        out=ygated[:, dc * CH:(dc + 1) * CH], in0=hctmp[:],
                        in1=zst[:]), inc=VE)
                for mc in range(8):
                    q.wait("gpsimd", TE)
                    q.wait("gpsimd", SY)
                    dma("gpsimd", moutr[:], wmout_h[ch][mc])
                    q.wait("tensor", GP)
                    q.wait("tensor", SC)
                    q.wait("tensor", VE)
                    for ic in range(16):
                        q.emit("tensor", lambda e, ic=ic: e.matmul(
                            out=p2[:, 0:512], lhsT=moutr[:, ic * 128:(ic + 1) * 128],
                            rhs=ygated[:, ic * CH:(ic + 1) * CH],
                            start=(ic == 0), stop=(ic == 15)),
                            inc=(TE if ic == 15 else None))
                    q.wait("scalar", TE)
                    if phase == 0:
                        q.wait("scalar", SY)
                        q.emit("scalar", lambda e, mc=mc: e.copy(
                            out=fwdt[:], in_=p2[:, 0:512]), inc=SC)
                        q.wait("sync", SC)
                        q.emit("sync", lambda e, mc=mc, cc=cc: e.dma_start(
                            out=fwdscr[cc][:, mc * CH:(mc + 1) * CH],
                            in_=fwdt[:]), inc=SY, inck=16)
                    else:
                        q.emit("scalar", lambda e, mc=mc: e.copy(
                            out=yacc[:, mc * CH:(mc + 1) * CH], in_=p2[:, 0:512]),
                            inc=SC)
                if phase == 1:
                    layer_norm(CH, lambda dc, ws, wn, t0=t0:
                               x_sb[dc][:, t0 + ws:t0 + ws + wn], hlnc_c)
                    q.wait("gpsimd", TE)
                    dma("gpsimd", fwdin[:], fwdscr[cc])
                    q.wait("tensor", GP)
                    for dc in range(8):
                        q.wait("gpsimd", TE)
                        dma("gpsimd", owr[:], ow_h["L"][dc])
                        q.wait("tensor", GP)
                        q.wait("tensor", SC)
                        q.wait("tensor", VE)
                        for kc in range(8):
                            q.emit("tensor", lambda e, kc=kc: e.matmul(
                                out=p1a[:, 0:512],
                                lhsT=owr[:, kc * 128:(kc + 1) * 128],
                                rhs=fwdin[:, kc * CH:(kc + 1) * CH],
                                start=(kc == 0), stop=False), inc=TE)
                        q.wait("gpsimd", TE)
                        dma("gpsimd", owr[:], ow_h["R"][dc])
                        q.wait("tensor", GP)
                        for kc in range(8):
                            q.emit("tensor", lambda e, kc=kc: e.matmul(
                                out=p1a[:, 0:512],
                                lhsT=owr[:, kc * 128:(kc + 1) * 128],
                                rhs=yacc[:, kc * CH:(kc + 1) * CH],
                                start=False, stop=(kc == 7)),
                                inc=(TE if kc == 7 else None))
                        q.wait("vector", TE)
                        q.emit("vector", lambda e, dc=dc: e.scalar_tensor_tensor(
                            out=lnt1[:, 0:512], in0=p1a[:, 0:512],
                            scalar=outb_sb[:, dc:dc + 1], in1=hlnc_c[dc],
                            op0=ALU.add, op1=ALU.add), inc=VE)
                        q.emit("vector", lambda e, dc=dc, t0=t0: e.tensor_add(
                            out=x_sb[dc][:, t0:t0 + CH], in0=x_sb[dc][:, t0:t0 + CH],
                            in1=lnt1[:, 0:512]), inc=VE)

        q.emit("vector", lambda e: e.memset(stbuf["L"][:], 0.0), inc=VE)
        q.emit("vector", lambda e: e.memset(stbuf["R"][:], 0.0), inc=VE)
        q.emit("vector", lambda e: e.memset(ximg["L"][:], 0.0), inc=VE)
        q.emit("vector", lambda e: e.memset(ximg["R"][:], 0.0), inc=VE)
        q.barrier([VE])
        chain("L", 0)
        q.barrier(ALLS)
        if stage_limit == 6:
            pass
        if stage_limit != 6:
            dma("sync", pselbig_sb[:], pselbig_h[:])
            q.wait("vector", SY)
            q.emit("vector", lambda e: e.tensor_copy(out=ccbuf[:, 0:256],
                                                     in_=stbuf["L"][:]), inc=VE)
            q.emit("vector", lambda e: e.tensor_copy(out=ccbuf[:, 256:304],
                                                     in_=ximg["L"][:]), inc=VE)
            q.wait("sync", VE)
            dma("sync", cc_in[:], ccbuf[:])
            q.wait("gpsimd", SY)
            q.emit("gpsimd", lambda e: e.collective_compute(
                "AllGather", ALU.bypass, replica_groups=[list(range(8))],
                ins=[cc_in[:]], outs=[cc_out[:]]), inc=CC)
            q.wait("sync", CC)
            dma("sync", gath[:].rearrange("p (a b) -> p a b", a=8),
                cc_out.transpose([1, 0, 2]))
            q.wait("vector", SY)
            q.emit("vector", lambda e: e.tensor_mul(
                out=gath[:], in0=gath[:], in1=pselbig_sb[:]), inc=VE)
            for pr in ((0, 1), (2, 3), (4, 5), (6, 7), (0, 2), (4, 6), (0, 4)):
                q.emit("vector", lambda e, pr=pr: e.tensor_add(
                    out=gath[:, pr[0] * 304:pr[0] * 304 + 304],
                    in0=gath[:, pr[0] * 304:pr[0] * 304 + 304],
                    in1=gath[:, pr[1] * 304:pr[1] * 304 + 304]), inc=VE)
            q.wait("scalar", VE)
            q.emit("scalar", lambda e: e.copy(out=stbuf["R"][:], in_=gath[:, 0:256]),
                   inc=SC)
            for dc in range(16):
                q.emit("scalar", lambda e, dc=dc: e.copy(
                    out=ximg["R"][:, dc * 3:dc * 3 + 3],
                    in_=gath[:, 256 + dc * 3:256 + dc * 3 + 3][:, ::-1]), inc=SC)
            q.barrier(ALLS)
            chain("R", 1)
            q.barrier(ALLS)


    if 8 <= stage_limit < 11:
        fcur = [ARENA0]
        wring = alloc("wring", [128, 2 * 1024], BF16, fcur)
        wfr = [alloc(f"wfr{i}", [128, 1024], F32, fcur) for i in range(2)]
        ln3b = alloc("ln3b", [128, 8 * 512], BF16, fcur)
        ln3b_c = [ln3b[:, i * 512:(i + 1) * 512] for i in range(8)]
        ln3f = alloc("ln3f", [128, 8 * 512], F32, fcur)
        ln3f_c = [ln3f[:, i * 512:(i + 1) * 512] for i in range(8)]
        xq = alloc("xq", [128, 8 * 512], BF16, fcur)
        hact = alloc("hact", [128, 32 * 512], BF16, fcur)
        mxc = alloc("mxc", [128, 8], F32, fcur)
        sxrow = alloc("sxrow", [1, 512], F32, fcur)
        s2row = alloc("s2row", [1, 512], F32, fcur)
        dqg = alloc("dqg", [128, 512], F32, fcur)
        dqu = alloc("dqu", [128, 512], F32, fcur)
        dqd = alloc("dqd", [128, 512], F32, fcur)
        sxrep = alloc("sxrep", [128, 512], F32, fcur)
        gtmp = alloc("gtmp", [128, 512], BF16, fcur)
        wdr = alloc("wdr", [128, 4096], BF16, fcur)

        q.wait("vector", VE)
        q.emit("vector", lambda e: e.memset(wmean[:], 0.0), inc=VE)
        for j, wh in enumerate((wg2_h, wu2_h)):
            for mc in range(32):
                q.wait("sync", VE)
                dma("sync", wfr[mc % 2][:], wh[mc])
                q.wait("vector", SY)
                q.emit("vector", lambda e, mc=mc: e.reduce_sum(
                    out=lnt1[:, 0:1], in_=wfr[mc % 2][:],
                    axis=mybir.AxisListType.X, apply_absolute_value=True), inc=VE)
                q.emit("vector", lambda e, j=j: e.tensor_add(
                    out=wmean[:, j:j + 1], in0=wmean[:, j:j + 1],
                    in1=lnt1[:, 0:1]), inc=VE)
        for dc in range(8):
            for half in range(2):
                q.wait("sync", VE)
                dma("sync", wdr[:].bitcast(F32)[:, 0:2048],
                    wd2_h[dc][:, half * 2048:(half + 1) * 2048])
                q.wait("vector", SY)
                q.emit("vector", lambda e: e.reduce_sum(
                    out=lnt1[:, 0:1], in_=wdr[:].bitcast(F32)[:, 0:2048],
                    axis=mybir.AxisListType.X, apply_absolute_value=True), inc=VE)
                q.emit("vector", lambda e: e.tensor_add(
                    out=wmean[:, 2:3], in0=wmean[:, 2:3], in1=lnt1[:, 0:1]), inc=VE)
        q.wait("tensor", VE)
        q.emit("tensor", lambda e: e.matmul(
            out=p1a[0:1, 0:4], lhsT=onesc_f[:], rhs=wmean[:, 0:4],
            start=True, stop=True), inc=TE)
        q.wait("vector", TE)
        q.emit("vector", lambda e: e.tensor_scalar(
            out=cmean_sb[0:1, 0:4], in0=p1a[0:1, 0:4],
            scalar1=1.0 / (INNER * D), scalar2=1e-5, op0=ALU.mult,
            op1=ALU.max), inc=VE)
        q.emit("vector", lambda e: e.reciprocal(
            out=smean[0:1, 0:4], in_=cmean_sb[0:1, 0:4]), inc=VE)
        q.wait("tensor", VE)
        q.emit("tensor", lambda e: e.matmul(
            out=p1a[:, 0:4], lhsT=onesr_f[:], rhs=smean[0:1, 0:4],
            start=True, stop=True), inc=TE)
        q.wait("scalar", TE)
        q.emit("scalar", lambda e: e.copy(out=svec[:, 0:4], in_=p1a[:, 0:4]), inc=SC)
        for j, (wh, scr, nt) in enumerate(
                ((wg2_h, wqg_scr, 32), (wu2_h, wqu_scr, 32), (wd2_h, wqd_scr, 8))):
            sj = 2 if j == 2 else j
            for ti in range(nt):
                wf = wfr[ti % 2]
                wide = 4096 if j == 2 else 1024
                if j == 2:
                    q.wait("sync", SY)
                    q.wait("sync", SC)
                    dma("sync", wfr[0][:], wh[ti][:, 0:1024])
                    dma("sync", wfr[1][:], wh[ti][:, 1024:2048])
                    for half in range(2):
                        q.wait("scalar", SY)
                        q.wait("scalar", VE)
                        q.emit("scalar", lambda e, half=half, sj=sj: e.activation(
                            out=lnt1[:, 0:512], in_=wfr[half][:, 0:512],
                            func=AF.Copy, scale=svec[:, sj:sj + 1],
                            bias=float(RND)), inc=SC)
                        q.emit("scalar", lambda e, half=half, sj=sj: e.activation(
                            out=wfr[half][:, 0:512], in_=lnt1[:, 0:512],
                            func=AF.Copy, bias=float(-RND)), inc=SC)
                        q.emit("scalar", lambda e, half=half, sj=sj: e.activation(
                            out=lnt1[:, 0:512], in_=wfr[half][:, 512:1024],
                            func=AF.Copy, scale=svec[:, sj:sj + 1],
                            bias=float(RND)), inc=SC)
                        q.emit("scalar", lambda e, half=half, sj=sj: e.activation(
                            out=wfr[half][:, 512:1024], in_=lnt1[:, 0:512],
                            func=AF.Copy, bias=float(-RND)), inc=SC)
                        q.wait("vector", SC)
                        q.emit("vector", lambda e, half=half: e.tensor_scalar(
                            out=wring[:, half * 1024:(half + 1) * 1024],
                            in0=wfr[half][:], scalar1=1.0, scalar2=-1.0,
                            op0=ALU.min, op1=ALU.max), inc=VE)
                    q.wait("sync", VE)
                    q.emit("sync", lambda e, ti=ti: e.dma_start(
                        out=scr[ti][:, 0:2048], in_=wring[:]), inc=SY, inck=16)
                    q.wait("sync", SY)
                    dma("sync", wfr[0][:], wh[ti][:, 2048:3072])
                    dma("sync", wfr[1][:], wh[ti][:, 3072:4096])
                    for half in range(2):
                        q.wait("scalar", SY)
                        q.wait("scalar", VE)
                        q.emit("scalar", lambda e, half=half, sj=sj: e.activation(
                            out=lnt1[:, 0:512], in_=wfr[half][:, 0:512],
                            func=AF.Copy, scale=svec[:, sj:sj + 1],
                            bias=float(RND)), inc=SC)
                        q.emit("scalar", lambda e, half=half, sj=sj: e.activation(
                            out=wfr[half][:, 0:512], in_=lnt1[:, 0:512],
                            func=AF.Copy, bias=float(-RND)), inc=SC)
                        q.emit("scalar", lambda e, half=half, sj=sj: e.activation(
                            out=lnt1[:, 0:512], in_=wfr[half][:, 512:1024],
                            func=AF.Copy, scale=svec[:, sj:sj + 1],
                            bias=float(RND)), inc=SC)
                        q.emit("scalar", lambda e, half=half, sj=sj: e.activation(
                            out=wfr[half][:, 512:1024], in_=lnt1[:, 0:512],
                            func=AF.Copy, bias=float(-RND)), inc=SC)
                        q.wait("vector", SC)
                        q.emit("vector", lambda e, half=half: e.tensor_scalar(
                            out=wring[:, half * 1024:(half + 1) * 1024],
                            in0=wfr[half][:], scalar1=1.0, scalar2=-1.0,
                            op0=ALU.min, op1=ALU.max), inc=VE)
                    q.wait("sync", VE)
                    q.emit("sync", lambda e, ti=ti: e.dma_start(
                        out=scr[ti][:, 2048:4096], in_=wring[:]), inc=SY, inck=16)
                    q.wait("sync", SY)
                else:
                    q.wait("sync", SC)
                    q.wait("sync", VE)
                    dma("sync", wf[:], wh[ti])
                    q.wait("scalar", SY)
                    q.wait("scalar", VE)
                    q.emit("scalar", lambda e, wf=wf, sj=sj: e.activation(
                        out=lnt1[:, 0:512], in_=wf[:, 0:512], func=AF.Copy,
                        scale=svec[:, sj:sj + 1], bias=float(RND)), inc=SC)
                    q.emit("scalar", lambda e, wf=wf: e.activation(
                        out=wf[:, 0:512], in_=lnt1[:, 0:512], func=AF.Copy,
                        bias=float(-RND)), inc=SC)
                    q.emit("scalar", lambda e, wf=wf, sj=sj: e.activation(
                        out=lnt1[:, 0:512], in_=wf[:, 512:1024], func=AF.Copy,
                        scale=svec[:, sj:sj + 1], bias=float(RND)), inc=SC)
                    q.emit("scalar", lambda e, wf=wf: e.activation(
                        out=wf[:, 512:1024], in_=lnt1[:, 0:512], func=AF.Copy,
                        bias=float(-RND)), inc=SC)
                    q.wait("vector", SC)
                    q.emit("vector", lambda e, wf=wf, ti=ti: e.tensor_scalar(
                        out=wring[:, (ti % 2) * 1024:(ti % 2 + 1) * 1024],
                        in0=wf[:], scalar1=1.0, scalar2=-1.0,
                        op0=ALU.min, op1=ALU.max), inc=VE)
                    q.wait("sync", VE)
                    q.emit("sync", lambda e, ti=ti, scr=scr: e.dma_start(
                        out=scr[ti], in_=wring[:, (ti % 2) * 1024:(ti % 2 + 1) * 1024]),
                        inc=SY, inck=16)
                    q.wait("sync", SY)
        q.barrier(ALLS)
        for tc in range(4):
            t0 = tc * 512
            layer_norm(512, lambda dc, ws, wn, t0=t0:
                       x_sb[dc][:, t0 + ws:t0 + ws + wn],
                       ln3b_c, out32=ln3f_c)
            for tb in range(4):
                q.wait("tensor", VE)
                q.wait("tensor", SC)
                for dc in range(8):
                    q.emit("tensor", lambda e, tb=tb, dc=dc: e.transpose(
                        out=p2.bitcast(BF16)[:, dc * 128:(dc + 1) * 128],
                        in_=ln3b_c[dc][:, tb * 128:(tb + 1) * 128],
                        identity=idm_bf[:]), inc=(TE if dc == 7 else None))
                q.wait("vector", TE)
                q.emit("vector", lambda e, tb=tb: e.reduce_max(
                    out=mxc[:, tb:tb + 1], in_=p2.bitcast(BF16)[:, 0:1024],
                    axis=mybir.AxisListType.X, apply_absolute_value=True), inc=VE)
                q.emit("vector", lambda e, tb=tb: e.tensor_scalar(
                    out=mxc[:, 4 + tb:5 + tb], in0=mxc[:, tb:tb + 1],
                    scalar1=1e-5, scalar2=None, op0=ALU.max), inc=VE)
                q.emit("vector", lambda e, tb=tb: e.reciprocal(
                    out=mxc[:, 4 + tb:5 + tb], in_=mxc[:, 4 + tb:5 + tb]), inc=VE)
                q.emit("vector", lambda e, tb=tb: e.tensor_scalar_mul(
                    out=mxc[:, 4 + tb:5 + tb], in0=mxc[:, 4 + tb:5 + tb],
                    scalar1=127.0), inc=VE)
                q.wait("scalar", VE)
                q.emit("scalar", lambda e, tb=tb: e.copy(
                    out=gtmp[:, 0:1], in_=mxc[:, 4 + tb:5 + tb]), inc=SC)
                q.wait("tensor", SC)
                q.emit("tensor", lambda e, tb=tb: e.transpose(
                    out=p1b.bitcast(BF16)[0:1, 0:128],
                    in_=gtmp[:, 0:1], identity=idm_bf[:]), inc=TE)
                q.wait("vector", TE)
                q.emit("vector", lambda e, tb=tb: e.tensor_copy(
                    out=sxrow[0:1, tb * 128:(tb + 1) * 128],
                    in_=p1b.bitcast(BF16)[0:1, 0:128]), inc=VE)
            q.wait("tensor", VE)
            q.emit("tensor", lambda e: e.matmul(
                out=p1b[:, 0:512], lhsT=onesr_f[:], rhs=sxrow[0:1, 0:512],
                start=True, stop=True), inc=TE)
            q.wait("scalar", TE)
            q.emit("scalar", lambda e: e.copy(out=sxrep[:], in_=p1b[:, 0:512]),
                   inc=SC)
            for dc in range(8):
                q.wait("vector", SC)
                q.emit("vector", lambda e, dc=dc: e.tensor_mul(
                    out=lnt1[:, 0:512], in0=ln3f_c[dc], in1=sxrep[:]), inc=VE)
                q.emit("vector", lambda e, dc=dc: e.tensor_scalar(
                    out=xq[:, dc * 512:(dc + 1) * 512], in0=lnt1[:, 0:512],
                    scalar1=RND, scalar2=RND, op0=ALU.add, op1=ALU.subtract),
                    inc=VE)
            q.emit("vector", lambda e: e.reciprocal(
                out=s2row[0:1, 0:512], in_=sxrow[0:1, 0:512]), inc=VE)
            for j, dq in ((0, dqg), (1, dqu)):
                q.wait("vector", TE)
                q.emit("vector", lambda e, j=j: e.tensor_single_scalar(
                    out=lnt1[0:1, 0:512], in_=s2row[0:1, 0:512],
                    scalar=cmean_sb[0:1, j:j + 1], op=ALU.mult), inc=VE)
                q.wait("tensor", VE)
                q.wait("tensor", SC)
                q.emit("tensor", lambda e: e.matmul(
                    out=p1b[:, 0:512], lhsT=onesr_f[:], rhs=lnt1[0:1, 0:512],
                    start=True, stop=True), inc=TE)
                q.wait("scalar", TE)
                q.emit("scalar", lambda e, dq=dq: e.copy(out=dq[:], in_=p1b[:, 0:512]),
                       inc=SC)
            for mc in range(32):
                q.wait("gpsimd", TE)
                dma("gpsimd", wring[:, 0:1024], wqg_scr[mc])
                dma("gpsimd", wring[:, 1024:2048], wqu_scr[mc])
                q.wait("tensor", GP)
                q.wait("tensor", SC)
                q.wait("tensor", VE)
                for kc in range(8):
                    q.emit("tensor", lambda e, kc=kc: e.matmul(
                        out=p2[:, 0:512], lhsT=wring[:, kc * 128:(kc + 1) * 128],
                        rhs=xq[:, kc * 512:(kc + 1) * 512],
                        start=(kc == 0), stop=(kc == 7)), inc=None)
                    q.emit("tensor", lambda e, kc=kc: e.matmul(
                        out=p2[:, 512:1024],
                        lhsT=wring[:, 1024 + kc * 128:1024 + (kc + 1) * 128],
                        rhs=xq[:, kc * 512:(kc + 1) * 512],
                        start=(kc == 0), stop=(kc == 7)),
                        inc=(TE if kc == 7 else None))
                q.wait("vector", TE)
                q.emit("vector", lambda e: e.tensor_mul(
                    out=lnt1[:, 0:512], in0=p2[:, 0:512], in1=dqg[:]), inc=VE)
                q.wait("scalar", VE)
                q.emit("scalar", lambda e: e.activation(
                    out=gtmp[:], in_=lnt1[:, 0:512], func=AF.Silu), inc=SC)
                q.wait("vector", SC)
                q.emit("vector", lambda e: e.tensor_tensor(
                    out=lnt1[:, 0:512], in0=p2[:, 512:1024], in1=dqu[:],
                    op=ALU.mult), inc=VE)
                q.emit("vector", lambda e, mc=mc: e.tensor_mul(
                    out=hact[:, mc * 512:(mc + 1) * 512], in0=gtmp[:],
                    in1=lnt1[:, 0:512].bitcast(BF16)[:, 0:512] if False else
                    lnt1[:, 0:512]), inc=VE)
            for tb in range(4):
                for hhalf in range(2):
                    q.wait("tensor", VE)
                    q.wait("tensor", SC)
                    for ic in range(16):
                        q.emit("tensor", lambda e, tb=tb, hhalf=hhalf, ic=ic: e.transpose(
                            out=p4.bitcast(BF16)[:, ic * 128:(ic + 1) * 128],
                            in_=hact[:, (hhalf * 16 + ic) * 512 + tb * 128:
                                     (hhalf * 16 + ic) * 512 + (tb + 1) * 128],
                            identity=idm_bf[:]), inc=(TE if ic == 15 else None))
                    q.wait("vector", TE)
                    q.emit("vector", lambda e, tb=tb, hhalf=hhalf: e.reduce_max(
                        out=mxc[:, hhalf:hhalf + 1],
                        in_=p4.bitcast(BF16)[:, 0:2048],
                        axis=mybir.AxisListType.X, apply_absolute_value=True),
                        inc=VE)
                q.emit("vector", lambda e, tb=tb: e.tensor_max(
                    out=mxc[:, 0:1], in0=mxc[:, 0:1], in1=mxc[:, 1:2]), inc=VE)
                q.emit("vector", lambda e: e.tensor_scalar(
                    out=mxc[:, 0:1], in0=mxc[:, 0:1], scalar1=1e-5,
                    scalar2=None, op0=ALU.max), inc=VE)
                q.emit("vector", lambda e: e.reciprocal(
                    out=mxc[:, 0:1], in_=mxc[:, 0:1]), inc=VE)
                q.emit("vector", lambda e: e.tensor_scalar_mul(
                    out=mxc[:, 0:1], in0=mxc[:, 0:1], scalar1=127.0), inc=VE)
                q.wait("scalar", VE)
                q.emit("scalar", lambda e: e.copy(out=gtmp[:, 0:1], in_=mxc[:, 0:1]),
                       inc=SC)
                q.wait("tensor", SC)
                q.emit("tensor", lambda e: e.transpose(
                    out=p1b.bitcast(BF16)[0:1, 0:128], in_=gtmp[:, 0:1],
                    identity=idm_bf[:]), inc=TE)
                q.wait("vector", TE)
                q.emit("vector", lambda e, tb=tb: e.tensor_copy(
                    out=s2row[0:1, tb * 128:(tb + 1) * 128],
                    in_=p1b.bitcast(BF16)[0:1, 0:128]), inc=VE)
            q.wait("tensor", VE)
            q.emit("tensor", lambda e: e.matmul(
                out=p1b[:, 0:512], lhsT=onesr_f[:], rhs=s2row[0:1, 0:512],
                start=True, stop=True), inc=TE)
            q.wait("scalar", TE)
            q.emit("scalar", lambda e: e.copy(out=sxrep[:], in_=p1b[:, 0:512]),
                   inc=SC)
            for mc in range(32):
                q.wait("vector", SC)
                q.emit("vector", lambda e, mc=mc: e.tensor_mul(
                    out=lnt1[:, 0:512], in0=hact[:, mc * 512:(mc + 1) * 512],
                    in1=sxrep[:]), inc=VE)
                q.emit("vector", lambda e, mc=mc: e.tensor_scalar(
                    out=hact[:, mc * 512:(mc + 1) * 512], in0=lnt1[:, 0:512],
                    scalar1=RND, scalar2=RND, op0=ALU.add, op1=ALU.subtract),
                    inc=VE)
            q.emit("vector", lambda e: e.reciprocal(
                out=s2row[0:1, 0:512], in_=s2row[0:1, 0:512]), inc=VE)
            q.emit("vector", lambda e: e.tensor_single_scalar(
                out=s2row[0:1, 0:512], in_=s2row[0:1, 0:512],
                scalar=cmean_sb[0:1, 2:3], op=ALU.mult), inc=VE)
            q.wait("tensor", VE)
            q.wait("tensor", SC)
            q.emit("tensor", lambda e: e.matmul(
                out=p1b[:, 0:512], lhsT=onesr_f[:], rhs=s2row[0:1, 0:512],
                start=True, stop=True), inc=TE)
            q.wait("scalar", TE)
            q.emit("scalar", lambda e: e.copy(out=dqd[:], in_=p1b[:, 0:512]), inc=SC)
            for dc in range(8):
                q.wait("gpsimd", TE)
                dma("gpsimd", wdr[:], wqd_scr[dc])
                q.wait("tensor", GP)
                q.wait("tensor", SC)
                q.wait("tensor", VE)
                for ic in range(32):
                    q.emit("tensor", lambda e, ic=ic: e.matmul(
                        out=p1a[:, 0:512], lhsT=wdr[:, ic * 128:(ic + 1) * 128],
                        rhs=hact[:, ic * 512:(ic + 1) * 512],
                        start=(ic == 0), stop=(ic == 31)),
                        inc=(TE if ic == 31 else None))
                q.wait("vector", TE)
                q.emit("vector", lambda e, dc=dc: e.tensor_mul(
                    out=lnt1[:, 0:512], in0=p1a[:, 0:512], in1=dqd[:]), inc=VE)
                q.emit("vector", lambda e, dc=dc, t0=t0: e.tensor_add(
                    out=x_sb[dc][:, t0:t0 + 512], in0=x_sb[dc][:, t0:t0 + 512],
                    in1=lnt1[:, 0:512]), inc=VE)
        q.barrier(ALLS)

    def finalize():
        with nc.Block() as block:
            @block.sync
            def _(e):
                for th in q.q["sync"]:
                    th(e)
            @block.vector
            def _(e):
                for th in q.q["vector"]:
                    th(e)
            @block.scalar
            def _(e):
                for th in q.q["scalar"]:
                    th(e)
            @block.tensor
            def _(e):
                for th in q.q["tensor"]:
                    th(e)
            @block.gpsimd
            def _(e):
                for th in q.q["gpsimd"]:
                    th(e)
    return nc, q, dict(out_h=out_h, x_sb=x_sb, SY=SY, finalize=finalize,
                       sems=ALLS, estack=estack)


def add_store_and_finalize(nc, q, env):
    SY = env["SY"]
    q.barrier(env["sems"], ["sync"])
    for dc in range(8):
        q.emit("sync", lambda e, dc=dc: e.dma_start(
            out=env["out_h"][dc], in_=env["x_sb"][dc][:]), inc=SY, inck=16)
    q.wait("sync", SY)
    env["finalize"]()
    env["estack"].close()
    return nc


def build_full(stage_limit=8):
    nc, q, env = build_program(stage_limit)
    return add_store_and_finalize(nc, q, env)


def kernel(**inputs):
    maps = stage_inputs(inputs)
    nc = build_full()
    res = run_bass_kernel_spmd(nc, maps, core_ids=list(range(NC)))
    return unstage_output(res.results)


# version 06
# speedup vs baseline: 1.1320x; 1.1320x over previous
import contextlib
import numpy as np
import concourse.bass as bass
import concourse.mybir as mybir
from concourse.bass_utils import run_bass_kernel_spmd

F32 = mybir.dt.float32
BF16 = mybir.dt.bfloat16
AF = mybir.ActivationFunctionType
ALU = mybir.AluOpType

NB, S, D = 4, 4096, 1024
T = 2048
NC = 8
WIN = 512
KVT = T + WIN
E, I, IP = 8, 938, 1024
DI, DS, DTR = 2048, 16, 64
INNER = 4096
CH = 512
NCH = T // CH
RND = 12582912.0



def stage_inputs(inp):
    f32 = np.float32
    x = np.asarray(inp["L0_attn_x"], f32)

    wqkv = np.asarray(inp["L0_attn_Wqkv"], f32)
    wqkv_hp = np.zeros((8, 128, 3 * 8 * 128), f32)
    for hp in range(8):
        for j in range(3):
            blk = wqkv[:, j * 1024 + hp * 128: j * 1024 + (hp + 1) * 128]
            t = blk.reshape(8, 128, 128)
            wqkv_hp[hp, :, j * 1024:(j + 1) * 1024] = (
                t.transpose(1, 0, 2).reshape(128, 1024))
    wout = np.asarray(inp["L0_attn_Wout"], f32).reshape(8, 128, 1024)
    wgate_t = np.ascontiguousarray(
        np.asarray(inp["L0_ffn_gate_weight"], f32).T.reshape(8, 128, 8)
        .transpose(1, 0, 2).reshape(128, 64))
    sh_gu = np.asarray(inp["L0_ffn_shared_gate_up"], f32)
    sh_dn = np.asarray(inp["L0_ffn_shared_down"], f32)
    ex_gu = np.asarray(inp["L0_ffn_experts_gate_up"], f32)
    ex_dn = np.asarray(inp["L0_ffn_experts_down"], f32)

    def gu_tiles(gu):
        g = np.zeros((IP, D), f32); u = np.zeros((IP, D), f32)
        g[:I] = gu[:I]; u[:I] = gu[I:]
        out = np.zeros((8, 16, 128, 128), f32)
        for kc in range(8):
            for ic in range(8):
                out[kc, ic] = g[ic * 128:(ic + 1) * 128, kc * 128:(kc + 1) * 128].T
                out[kc, 8 + ic] = u[ic * 128:(ic + 1) * 128, kc * 128:(kc + 1) * 128].T
        return out

    def dn_tiles(dn):
        dpad = np.zeros((D, IP), f32); dpad[:, :I] = dn
        out = np.zeros((8, 8, 128, 128), f32)
        for ic in range(8):
            for dc in range(8):
                out[ic, dc] = dpad[dc * 128:(dc + 1) * 128, ic * 128:(ic + 1) * 128].T
        return out

    wgu = np.stack([gu_tiles(ex_gu[e]) for e in range(E)] + [gu_tiles(sh_gu)])
    wgu = np.ascontiguousarray(wgu.transpose(0, 3, 1, 2, 4).reshape(9, 128, 8 * 16 * 128))
    wdn = np.stack([dn_tiles(ex_dn[e]) for e in range(E)] + [dn_tiles(sh_dn)])
    wdn = np.ascontiguousarray(wdn.transpose(0, 3, 1, 2, 4).reshape(9, 128, 8 * 8 * 128))

    def mamba_pack(tag):
        g = lambda n: np.asarray(inp["L1_attn_" + n + "_" + tag], f32)
        in_w = g("in_w")
        win = np.zeros((8, 128, 2048), f32)
        for grp in range(8):
            for kc in range(8):
                for m2 in range(2):
                    mc = 2 * grp + m2
                    til = in_w[mc * 128:(mc + 1) * 128, kc * 128:(kc + 1) * 128].T
                    win[grp, :, (kc * 2 + m2) * 128:(kc * 2 + m2 + 1) * 128] = til
        wz = np.zeros((16, 128, 1024), f32)
        for dc in range(16):
            for kc in range(8):
                til = in_w[2048 + dc * 128:2048 + (dc + 1) * 128,
                           kc * 128:(kc + 1) * 128].T
                wz[dc, :, kc * 128:(kc + 1) * 128] = til
        xproj = g("xproj_w")
        xpg = np.zeros((128, 16 * 96), f32)
        for kc in range(16):
            xpg[:, kc * 96:(kc + 1) * 96] = xproj[:, kc * 128:(kc + 1) * 128].T
        mout = g("mout_w")
        mg = np.zeros((8, 128, 2048), f32)
        for mc in range(8):
            for ic in range(16):
                mg[mc, :, ic * 128:(ic + 1) * 128] = (
                    mout[mc * 128:(mc + 1) * 128, ic * 128:(ic + 1) * 128].T)
        return dict(
            win=win, wz=wz, conv=g("conv_w")[:, 0, :],
            convb=np.ascontiguousarray(g("conv_b").reshape(16, 128).T),
            xproj=xpg,
            dtw=np.ascontiguousarray(g("dt_w").T),
            dtb=np.ascontiguousarray(g("dt_b").reshape(16, 128).T),
            alog=np.ascontiguousarray(g("A_log")[0:1, :]),
            dv=np.ascontiguousarray(g("D").reshape(16, 128).T),
            wmout=mg,
        )

    mbd = {"f": mamba_pack("f"), "b": mamba_pack("b")}
    outw_t = np.ascontiguousarray(np.asarray(inp["L1_attn_out_w"], f32).T)
    outb = np.ascontiguousarray(np.asarray(inp["L1_attn_out_b"], f32).reshape(8, 128).T)
    gamma = np.asarray(inp["L1_attn_gamma"], f32)
    beta = np.asarray(inp["L1_attn_beta"], f32)
    def ffn_mc(w):
        o = np.zeros((32, 128, 1024), f32)
        for mc in range(32):
            for kc in range(8):
                o[mc, :, kc * 128:(kc + 1) * 128] = (
                    w[mc * 128:(mc + 1) * 128, kc * 128:(kc + 1) * 128].T)
        return o

    def ffn_dc(w):
        o = np.zeros((8, 128, 4096), f32)
        for dc in range(8):
            for ic in range(32):
                o[dc, :, ic * 128:(ic + 1) * 128] = (
                    w[dc * 128:(dc + 1) * 128, ic * 128:(ic + 1) * 128].T)
        return o
    wg2t = ffn_mc(np.asarray(inp["L1_ffn_w_gate"], f32))
    wu2t = ffn_mc(np.asarray(inp["L1_ffn_w_up"], f32))
    wd2t = ffn_dc(np.asarray(inp["L1_ffn_w_down"], f32))

    maps = []
    for c in range(NC):
        b, h = c // 2, c % 2
        if h == 0:
            idx = np.arange(T)
            xk = np.concatenate([np.zeros((WIN, D), f32), x[b, 0:T]], 0)
        else:
            idx = np.arange(S - 1, T - 1, -1)
            halo = x[b, T - 1:T - 1 - WIN:-1]
            xk = np.concatenate([x[b, idx], halo], 0)
        r = np.arange(128)[:, None]; cc = np.arange(640)[None, :]
        am = np.zeros((16, 128, 640), f32)
        for qb in range(16):
            if h == 0:
                am[qb] = ((cc - r >= 1) & (cc - r <= WIN)
                          & (qb * 128 + cc >= WIN)).astype(f32)
            else:
                am[qb] = ((cc - r >= 0) & (cc - r <= WIN - 1)).astype(f32)
        pselbig = np.zeros((128, 8, 304), f32)
        pselbig[:, c ^ 1, :] = 1.0
        L, R = ("f", "b") if h == 0 else ("b", "f")
        mbl, mbr = mbd[L], mbd[R]
        convd = np.zeros((2, 8, 128, 1024), f32)
        for dci in range(16):
            sl = slice(dci * 128, (dci + 1) * 128)
            for j in range(4):
                o = ((dci % 2) * 4 + j) * 128
                convd[0, dci // 2, :, o:o + 128] = np.diag(mbl["conv"][sl, j])
                convd[1, dci // 2, :, o:o + 128] = np.diag(mbr["conv"][sl, 3 - j])
        owLh = outw_t[0:1024] if h == 0 else outw_t[1024:2048]
        owRh = outw_t[1024:2048] if h == 0 else outw_t[0:1024]
        owL = np.zeros((8, 128, 1024), f32)
        owR = np.zeros((8, 128, 1024), f32)
        for dci in range(8):
            for kc in range(8):
                owL[dci, :, kc * 128:(kc + 1) * 128] = (
                    owLh[kc * 128:(kc + 1) * 128, dci * 128:(dci + 1) * 128])
                owR[dci, :, kc * 128:(kc + 1) * 128] = (
                    owRh[kc * 128:(kc + 1) * 128, dci * 128:(dci + 1) * 128])
        m = {
            "xin": np.ascontiguousarray(x[b, idx].T.reshape(8, 128, T)),
            "xkv": np.ascontiguousarray(xk.T.reshape(8, 128, KVT)),
            "amask": am, "idm": np.eye(128, dtype=f32),
            "ones_row": np.ones((1, 128), f32),
            "cbias": np.tile(np.array([[1e-5, 1e-30, np.log(0.125),
                                        12582912.0, -12582912.0, 0.0]], f32),
                             (128, 1)),
            "ones_col": np.ones((128, 1), f32),
            "nsel": np.ascontiguousarray(
                np.eye(16, dtype=f32)[:, :, None].repeat(128, 2).reshape(16, 16 * 128)),
            "pselbig": pselbig.reshape(128, 8 * 304),
            "qoff": np.array([[WIN if h == 0 else 0]], np.int32).view(f32),
            "wqkv": wqkv_hp, "wout": wout, "wgatelog": wgate_t,
            "wgu": wgu, "wdn": wdn,
            "winL": mbl["win"], "winR": mbr["win"], "convd": convd,
            "convbL": mbl["convb"], "convbR": mbr["convb"],
            "xprojL": mbl["xproj"], "xprojR": mbr["xproj"],
            "dtwL": mbl["dtw"], "dtwR": mbr["dtw"],
            "dtbL": mbl["dtb"], "dtbR": mbr["dtb"],
            "alogL": mbl["alog"], "alogR": mbr["alog"],
            "dvL": mbl["dv"], "dvR": mbr["dv"],
            "wmoutL": mbl["wmout"], "wmoutR": mbr["wmout"],
            "owL": owL, "owR": owR,
            "wzL": mbl["wz"], "wzR": mbr["wz"],
            "outb": outb,
            "gammarow": gamma.reshape(1, 1024),
            "betacol": np.ascontiguousarray(beta.reshape(8, 128).T),
            "wg2t": wg2t, "wu2t": wu2t, "wd2t": wd2t,
        }
        maps.append(m)
    return maps


def unstage_output(results):
    y = np.zeros((NB, S, D), np.float32)
    for c in range(NC):
        b, h = c // 2, c % 2
        o = results[c]["out"].reshape(D, T).T
        if h == 0:
            y[b, 0:T] = o
        else:
            y[b, S - 1:T - 1:-1] = o
    return y



class Sem:
    def __init__(self, h):
        self.h = h
        self.n = 0


class Q:
    ENGS = ("sync", "vector", "scalar", "tensor", "gpsimd")

    def __init__(self, self_sems=None):
        self.q = {e: [] for e in self.ENGS}
        self.self_sems = self_sems or {}

    def emit(self, eng, fn, inc=None, inck=1):
        ss = self.self_sems.get(eng)
        if ss is not None and ss.n > 0:
            self.wait(eng, ss)
        if inc is not None:
            inc.n += inck
        def thunk(e, fn=fn, inc=inc, inck=inck):
            ins = fn(e)
            if inc is not None:
                ins.then_inc(inc.h, inck)
        self.q[eng].append(thunk)

    def wait(self, eng, sem, val=None):
        v = sem.n if val is None else val
        self.q[eng].append(lambda e, s=sem, v=v: e.wait_ge(s.h, v))

    def barrier(self, sems, engs=None):
        for e in (engs or self.ENGS):
            for s in sems:
                self.wait(e, s)


def build_program(stage_limit=99):
    nc = bass.Bass("TRN2", target_bir_lowering=False)
    din = lambda n, s: nc.dram_tensor(n, s, F32, kind="ExternalInput").ap()

    xin_h = din("xin", [8, 128, T])
    xkv_h = din("xkv", [8, 128, KVT])
    am_h = din("amask", [16, 128, 640])
    idm_h = din("idm", [128, 128])
    onesr_h = din("ones_row", [1, 128])
    onesc_h = din("ones_col", [128, 1])
    cbias_h = din("cbias", [128, 6])
    nsel_h = din("nsel", [16, 16 * 128])
    pselbig_h = din("pselbig", [128, 8 * 304])
    qoff_h = din("qoff", [1, 1])
    wqkv_h = din("wqkv", [8, 128, 3 * 8 * 128])
    wout_h = din("wout", [8, 128, 1024])
    wgl_h = din("wgatelog", [128, 64])
    wgu_h = din("wgu", [9, 128, 8 * 16 * 128])
    wdn_h = din("wdn", [9, 128, 8 * 8 * 128])
    win_h = {"L": din("winL", [8, 128, 2048]), "R": din("winR", [8, 128, 2048])}
    wz_h = {"L": din("wzL", [16, 128, 1024]), "R": din("wzR", [16, 128, 1024])}
    convd_h = din("convd", [2, 8, 128, 1024])
    convb_h = {"L": din("convbL", [128, 16]), "R": din("convbR", [128, 16])}
    xproj_h = {"L": din("xprojL", [128, 16 * 96]), "R": din("xprojR", [128, 16 * 96])}
    dtw_h = {"L": din("dtwL", [64, DI]), "R": din("dtwR", [64, DI])}
    dtb_h = {"L": din("dtbL", [128, 16]), "R": din("dtbR", [128, 16])}
    alog_h = {"L": din("alogL", [1, 16]), "R": din("alogR", [1, 16])}
    dv_h = {"L": din("dvL", [128, 16]), "R": din("dvR", [128, 16])}
    wmout_h = {"L": din("wmoutL", [8, 128, 2048]), "R": din("wmoutR", [8, 128, 2048])}
    ow_h = {"L": din("owL", [8, 128, 1024]), "R": din("owR", [8, 128, 1024])}
    outb_h = din("outb", [128, 8])
    gr_h = din("gammarow", [1, 1024])
    bc_h = din("betacol", [128, 8])
    wg2_h = din("wg2t", [32, 128, 1024])
    wu2_h = din("wu2t", [32, 128, 1024])
    wd2_h = din("wd2t", [8, 128, 4096])
    out_h = nc.dram_tensor("out", [8, 128, T], F32, kind="ExternalOutput").ap()

    bncBC = nc.dram_tensor("bncBC", [32, CH], F32, kind="Internal").ap()
    fwdscr = nc.dram_tensor("fwdscr", [NCH, 128, 8 * CH], BF16, kind="Internal").ap()
    bncCW = nc.dram_tensor("bncCW", [8, T], F32, kind="Internal").ap()
    cc_in = nc.dram_tensor("cc_in", [128, 304], F32, kind="Internal").ap()
    cc_out = nc.dram_tensor("cc_out", [8, 128, 304], F32, kind="Internal").ap()
    wqg_scr = nc.dram_tensor("wqg_scr", [32, 128, 1024], BF16, kind="Internal").ap()
    wqu_scr = nc.dram_tensor("wqu_scr", [32, 128, 1024], BF16, kind="Internal").ap()
    wqd_scr = nc.dram_tensor("wqd_scr", [8, 128, 4096], BF16, kind="Internal").ap()

    sa = nc.alloc_sbuf_tensor_at
    off = [16544]

    def alloc(name, shape, dtype, cur=None):
        c = cur if cur is not None else off
        t = sa(name, shape, dtype, offset=c[0])
        nbytes = int(np.prod(shape[1:])) * (2 if dtype == BF16 else 4)
        c[0] = (c[0] + nbytes + 31) & ~31
        assert c[0] <= 224 * 1024, f"SBUF overflow at {name}: {c[0]}"
        return t

    x_sb = [alloc(f"x{i}", [128, T], F32) for i in range(8)]
    onesr_bf = alloc("onesr_bf", [1, 128], BF16)
    onesr_f = alloc("onesr_f", [1, 128], F32)
    onesc_bf = alloc("onesc_bf", [128, 1], BF16)
    onesc_f = alloc("onesc_f", [128, 1], F32)
    idm_bf = alloc("idm_bf", [128, 128], BF16)
    idm_f = alloc("idm_f", [128, 128], F32)
    qoff_sb = alloc("qoff_sb", [1, 1], F32)
    cbias_sb = alloc("cbias_sb", [128, 6], F32)
    outb_sb = alloc("outb_sb", [128, 8], F32)
    dtb_sb = {ch: alloc(f"dtb{ch}", [128, 16], F32) for ch in "LR"}
    dv_sb = {ch: alloc(f"dv{ch}", [128, 16], F32) for ch in "LR"}
    convb_sb = {ch: alloc(f"convb{ch}", [128, 16], F32) for ch in "LR"}
    avec_sb = {ch: alloc(f"avec{ch}", [128, 16], F32) for ch in "LR"}
    grow_sb = alloc("grow_sb", [1, 1024], F32)
    bcol_sb = alloc("bcol_sb", [128, 8], F32)
    stbuf = {"L": alloc("stL", [128, 256], BF16), "R": alloc("stR", [128, 256], BF16)}
    ximg = {"L": alloc("ximgL", [128, 48], BF16), "R": alloc("ximgR", [128, 48], BF16)}
    rowA_t = alloc("rowA", [1, 2560], F32)
    rowB_t = alloc("rowB", [1, 2560], F32)
    rowA = rowA_t[0:1, :]
    rowB = rowB_t[0:1, :]
    rowbf = alloc("rowbf", [1, 2048], BF16)
    lnt1 = alloc("lnt1", [128, 512], F32)
    casA = alloc("casA", [128, 512], BF16)
    casB = alloc("casB", [128, 512], BF16)
    wmean = alloc("wmean", [128, 4], F32)
    svec = alloc("svec", [128, 4], F32)
    smean = alloc("smean", [1, 8], F32)
    cmean_sb = alloc("cmean_sb", [1, 4], F32)

    ARENA0 = off[0]

    p4 = nc.alloc_psum_tensor("p4", [128, 2048], F32)
    p2 = nc.alloc_psum_tensor("p2", [128, 1024], F32)
    p1a = nc.alloc_psum_tensor("p1a", [128, 512], F32)
    p1b = nc.alloc_psum_tensor("p1b", [128, 512], F32)

    estack = contextlib.ExitStack()
    mk = lambda n: Sem(estack.enter_context(nc.semaphore(n)))
    SY = mk("SY"); GP = mk("GP"); VE = mk("VE"); SC = mk("SC"); TE = mk("TE")
    CC = mk("CC"); GC = mk("GC")
    ALLS = [SY, GP, VE, SC, TE, CC, GC]

    q = Q(self_sems={"vector": VE, "scalar": SC, "gpsimd": GC})

    def dma(eng, out, in_, sem=None):
        s = sem if sem is not None else (GP if eng == "gpsimd" else SY)
        q.emit(eng, lambda e, o=out, i=in_: e.dma_start(out=o, in_=i), inc=s, inck=16)
        q.wait(eng, s)

    def preload(fn):
        pass

    dma("sync", qoff_sb[:], qoff_h[:])
    dma("sync", cbias_sb[:], cbias_h[:])
    nc.const_aps.register(F32, float(RND), cbias_sb[:, 3:4])
    nc.const_aps.register(F32, float(-RND), cbias_sb[:, 4:5])
    nc.const_aps.register(F32, 1e-5, cbias_sb[:, 0:1])
    nc.const_aps.register(F32, 1e-30, cbias_sb[:, 1:2])
    nc.const_aps.register(F32, float(np.log(0.125)), cbias_sb[:, 2:3])
    dma("sync", onesr_f[:], onesr_h[:])
    dma("gpsimd", onesr_bf[:], onesr_h[:])
    dma("gpsimd", onesc_bf[:], onesc_h[:])
    dma("sync", onesc_f[:], onesc_h[:])
    dma("gpsimd", idm_bf[:], idm_h[:])
    dma("sync", idm_f[:], idm_h[:])
    dma("sync", outb_sb[:], outb_h[:])
    dma("sync", grow_sb[:], gr_h[:])
    dma("sync", bcol_sb[:], bc_h[:])
    for ch in "LR":
        dma("sync", dtb_sb[ch][:], dtb_h[ch][:])
        dma("sync", dv_sb[ch][:], dv_h[ch][:])
        dma("sync", convb_sb[ch][:], convb_h[ch][:])
    for i in range(8):
        dma("sync", x_sb[i][:], xin_h[i])
    q.barrier(ALLS)
    preload(AF.Exp)
    for ch in "LR":
        q.wait("sync", SC)
        dma("sync", rowA[0:1, 0:16], alog_h[ch][:])
        q.wait("scalar", SY)
        q.wait("scalar", TE)
        q.emit("scalar", lambda e: e.activation(
            out=rowB[0:1, 0:16], in_=rowA[0:1, 0:16], func=AF.Exp), inc=SC)
        q.wait("vector", SC)
        q.emit("vector", lambda e: e.tensor_scalar_mul(
            out=rowB[0:1, 0:16], in0=rowB[0:1, 0:16], scalar1=-1.0), inc=VE)
        q.wait("tensor", VE)
        q.wait("tensor", SC)
        q.emit("tensor", lambda e: e.matmul(
            out=p1a[0:128, 0:16], lhsT=onesr_f[:], rhs=rowB[0:1, 0:16],
            start=True, stop=True), inc=TE)
        q.wait("scalar", TE)
        q.emit("scalar", lambda e, ch=ch: e.copy(
            out=avec_sb[ch][:], in_=p1a[:, 0:16]), inc=SC)
    q.barrier(ALLS)

    def layer_norm(W, src_fn, outs, out32=None, affine=False, parts=3, maxiter=999, double=False):
        nw = (W + 511) // 512
        it = 0
        for w in range(nw):
            ws = w * 512
            wn = min(512, W - ws)
            q.wait("tensor", VE)
            for dc in range(8):
                it += 1
                if it > maxiter:
                    return
                sap = src_fn(dc, ws, wn)
                ca, cb = casA, casB
                q.wait("scalar", TE)
                q.emit("scalar", lambda e, sap=sap, wn=wn, ca=ca: e.copy(
                    out=ca[:, 0:wn], in_=sap), inc=SC)
                q.emit("scalar", lambda e, sap=sap, wn=wn, cb=cb: e.activation(
                    out=cb[:, 0:wn], in_=sap, func=AF.Square), inc=SC)
                q.wait("tensor", SC)
                q.emit("tensor", lambda e, dc=dc, wn=wn, ca=ca: e.matmul(
                    out=p1a[0:1, 0:wn], lhsT=onesc_bf[:], rhs=ca[:, 0:wn],
                    start=(dc == 0), stop=(dc == 7)), inc=TE)
                q.emit("tensor", lambda e, dc=dc, wn=wn, cb=cb: e.matmul(
                    out=p1b[0:1, 0:wn], lhsT=onesc_bf[:], rhs=cb[:, 0:wn],
                    start=(dc == 0), stop=(dc == 7)), inc=TE)
            q.wait("vector", TE)
            q.emit("vector", lambda e, ws=ws, wn=wn: e.tensor_scalar_mul(
                out=rowA[0:1, ws:ws + wn], in0=p1a[0:1, 0:wn], scalar1=1.0 / D),
                inc=VE)
            q.emit("vector", lambda e, ws=ws, wn=wn: e.tensor_scalar_mul(
                out=rowB[0:1, ws:ws + wn], in0=p1b[0:1, 0:wn], scalar1=1.0 / D),
                inc=VE)
        if parts < 2:
            return
        q.wait("scalar", VE)
        for w in range(nw):
            ws = w * 512
            wn = min(512, W - ws)
            q.emit("scalar", lambda e, ws=ws, wn=wn: e.activation(
                out=lnt1[0:1, 0:wn], in_=rowA[0:1, ws:ws + wn],
                func=AF.Square), inc=SC)
            q.wait("vector", SC)
            q.emit("vector", lambda e, ws=ws, wn=wn: e.tensor_sub(
                out=rowB[0:1, ws:ws + wn], in0=rowB[0:1, ws:ws + wn],
                in1=lnt1[0:1, 0:wn]), inc=VE)
            q.wait("scalar", VE)
        if not double:
            q.emit("scalar", lambda e: e.activation(
                out=rowB[0:1, 0:W], in_=rowB[0:1, 0:W], func=AF.Ln,
                bias=cbias_sb[0:1, 0:1]), inc=SC)
            q.wait("scalar", SC)
            q.emit("scalar", lambda e: e.activation(
                out=rowB[0:1, 0:W], in_=rowB[0:1, 0:W], func=AF.Exp, scale=-0.5),
                inc=SC)
        else:
            for w in range(nw):
                ws = w * 512
                wn = min(512, W - ws)
                rB = lambda: rowB[0:1, ws:ws + wn]
                q.emit("scalar", lambda e, ws=ws, wn=wn: e.activation(
                    out=lnt1[0:1, 0:wn], in_=rowB[0:1, ws:ws + wn], func=AF.Ln,
                    bias=cbias_sb[0:1, 0:1]), inc=SC)
                q.emit("scalar", lambda e, ws=ws, wn=wn: e.activation(
                    out=rowB[0:1, ws:ws + wn], in_=rowB[0:1, ws:ws + wn],
                    func=AF.Ln), inc=SC)
                q.wait("vector", SC)
                q.emit("vector", lambda e, ws=ws, wn=wn: e.tensor_sub(
                    out=rowB[0:1, ws:ws + wn], in0=rowB[0:1, ws:ws + wn],
                    in1=lnt1[0:1, 0:wn]), inc=VE)
                q.wait("scalar", VE)
                q.emit("scalar", lambda e, ws=ws, wn=wn: e.activation(
                    out=rowB[0:1, ws:ws + wn], in_=rowB[0:1, ws:ws + wn],
                    func=AF.Exp), inc=SC)
                q.emit("scalar", lambda e, ws=ws, wn=wn: e.activation(
                    out=rowB[0:1, ws:ws + wn], in_=rowB[0:1, ws:ws + wn],
                    func=AF.Ln, bias=cbias_sb[0:1, 0:1]), inc=SC)
                q.wait("vector", SC)
                q.emit("vector", lambda e, ws=ws, wn=wn: e.tensor_add(
                    out=rowB[0:1, ws:ws + wn], in0=rowB[0:1, ws:ws + wn],
                    in1=lnt1[0:1, 0:wn]), inc=VE)
                q.wait("scalar", VE)
                q.emit("scalar", lambda e, ws=ws, wn=wn: e.activation(
                    out=rowB[0:1, ws:ws + wn], in_=rowB[0:1, ws:ws + wn],
                    func=AF.Exp, scale=-0.5), inc=SC)
        q.wait("vector", SC)
        q.emit("vector", lambda e: e.scalar_tensor_tensor(
            out=rowA[0:1, 0:W], in0=rowA[0:1, 0:W], scalar=-1.0,
            in1=rowB[0:1, 0:W], op0=ALU.mult, op1=ALU.mult), inc=VE)
        if parts < 3:
            return
        for w in range(nw):
            ws = w * 512
            wn = min(512, W - ws)
            for dc in range(8):
                sap = src_fn(dc, ws, wn)
                q.wait("tensor", VE)
                lhs_r = (grow_sb[0:1, dc * 128:(dc + 1) * 128] if affine
                         else onesr_f[0:1, :])
                q.emit("tensor", lambda e, lhs_r=lhs_r, ws=ws, wn=wn: e.matmul(
                    out=p1a[:, 0:wn], lhsT=lhs_r, rhs=rowB[0:1, ws:ws + wn],
                    start=True, stop=True), inc=TE)
                q.emit("tensor", lambda e, lhs_r=lhs_r, ws=ws, wn=wn: e.matmul(
                    out=p1b[:, 0:wn], lhsT=lhs_r, rhs=rowA[0:1, ws:ws + wn],
                    start=True, stop=True), inc=TE)
                q.wait("vector", TE)
                q.emit("vector", lambda e, sap=sap, wn=wn: e.tensor_mul(
                    out=lnt1[:, 0:wn], in0=sap, in1=p1a[:, 0:wn]), inc=VE)
                if affine:
                    q.emit("vector", lambda e, dc=dc, ws=ws, wn=wn: e.scalar_tensor_tensor(
                        out=outs[dc][:, ws:ws + wn], in0=lnt1[:, 0:wn],
                        scalar=bcol_sb[:, dc:dc + 1], in1=p1b[:, 0:wn],
                        op0=ALU.add, op1=ALU.add), inc=VE)
                else:
                    q.emit("vector", lambda e, dc=dc, ws=ws, wn=wn: e.tensor_add(
                        out=outs[dc][:, ws:ws + wn], in0=lnt1[:, 0:wn],
                        in1=p1b[:, 0:wn]), inc=VE)
                if out32 is not None:
                    q.emit("vector", lambda e, dc=dc, ws=ws, wn=wn: e.tensor_add(
                        out=out32[dc][:, ws:ws + wn], in0=lnt1[:, 0:wn],
                        in1=p1b[:, 0:wn]), inc=VE)

    cur = [ARENA0]
    lnkv = alloc("lnkv", [128, 8 * KVT], BF16, cur)
    lnkv_c = [lnkv[:, i * KVT:(i + 1) * KVT] for i in range(8)]
    lnown = alloc("lnown", [128, 8 * T], BF16, cur)
    lnown_c = [lnown[:, i * T:(i + 1) * T] for i in range(8)]
    MOE_ARENA = list(cur)
    kvring = [alloc(f"kvr{i}", [128, 512], F32, cur) for i in range(4)]
    kv_cnt = [0]

    def kv_src(dc, ws, wn):
        i = kv_cnt[0] % 4
        kv_cnt[0] += 1
        q.wait("sync", SC)
        q.wait("sync", VE)
        dma("sync", kvring[i][:, 0:wn], xkv_h[dc][:, ws:ws + wn])
        q.wait("scalar", SY)
        q.wait("vector", SY)
        return kvring[i][:, 0:wn]

    if stage_limit == 18:
        q.barrier(ALLS)
    if stage_limit == 19:
        preload(AF.Exp)
    if stage_limit == 14 or 20 <= stage_limit <= 60:
        preload(AF.Exp)
        layer_norm(T, lambda dc, ws, wn: x_sb[dc][:, ws:ws + wn], lnown_c,
                   parts=1, maxiter=(999 if stage_limit == 14 else stage_limit - 20))
        q.barrier(ALLS)
    elif stage_limit >= 1:
        preload(AF.Exp)
        layer_norm(KVT, kv_src, lnkv_c,
                   parts=(1 if stage_limit == 11 else 2 if stage_limit == 12 else 3))
        q.barrier(ALLS)
        if stage_limit not in (11, 12, 13):
            layer_norm(T, lambda dc, ws, wn: x_sb[dc][:, ws:ws + wn], lnown_c)
            q.barrier(ALLS)

    if 2 <= stage_limit < 11:
        acur = list(MOE_ARENA)
        Y2 = alloc("Y2", [128, T], BF16, acur)
        nB = alloc("nB", [128, 2 * 16], F32, acur)
        expb = alloc("expb", [128, 640], BF16, acur)
        mexp = alloc("mexp", [128, 640], BF16, acur)
        attnT = alloc("attnT", [128, 640], BF16, acur)
        rsum = alloc("rsum", [128, 1], F32, acur)
        rrec = alloc("rrec", [128, 1], F32, acur)
        amsk = alloc("amsk", [128, 2 * 640], BF16, acur)
        wqp = alloc("wqp", [128, 3 * 8 * 128], BF16, acur)
        woutp = alloc("woutp", [128, 1024], BF16, acur)
        Q2 = alloc("Q2", [128, T], BF16, acur)
        K2 = alloc("K2", [128, KVT], BF16, acur)
        V2 = alloc("V2", [128, 20 * 128], BF16, acur)
        preload(AF.Exp)
        for hp in range(8):
            q.wait("gpsimd", TE)
            dma("gpsimd", wqp[:], wqkv_h[hp])
            dma("gpsimd", woutp[:], wout_h[hp])
            q.wait("tensor", GP)
            q.wait("tensor", SY)
            for tc in range(4):
                q.wait("tensor", SC)
                q.wait("tensor", VE)
                for kc in range(8):
                    q.emit("tensor", lambda e, kc=kc, tc=tc: e.matmul(
                        out=p1a[:], lhsT=wqp[:, kc * 128:(kc + 1) * 128],
                        rhs=lnown_c[kc][:, tc * 512:(tc + 1) * 512],
                        start=(kc == 0), stop=(kc == 7)),
                        inc=(TE if kc == 7 else None))
                q.wait("scalar", TE)
                q.emit("scalar", lambda e, tc=tc: e.copy(
                    out=Q2[:, tc * 512:(tc + 1) * 512], in_=p1a[:]), inc=SC)
            for tk in range(5):
                q.wait("tensor", SC)
                q.wait("tensor", VE)
                for kc in range(8):
                    q.emit("tensor", lambda e, kc=kc, tk=tk: e.matmul(
                        out=p1b[:], lhsT=wqp[:, 1024 + kc * 128:1024 + (kc + 1) * 128],
                        rhs=lnkv_c[kc][:, tk * 512:(tk + 1) * 512],
                        start=(kc == 0), stop=(kc == 7)),
                        inc=(TE if kc == 7 else None))
                q.wait("scalar", TE)
                q.emit("scalar", lambda e, tk=tk: e.copy(
                    out=K2[:, tk * 512:(tk + 1) * 512], in_=p1b[:]), inc=SC)
            for kb in range(20):
                q.wait("tensor", SC)
                q.wait("tensor", VE)
                for kc in range(8):
                    q.emit("tensor", lambda e, kb=kb, kc=kc: e.matmul(
                        out=p1a[:, 0:128],
                        lhsT=lnkv_c[kc][:, kb * 128:(kb + 1) * 128],
                        rhs=wqp[:, 2048 + kc * 128:2048 + (kc + 1) * 128],
                        start=(kc == 0), stop=(kc == 7)),
                        inc=(TE if kc == 7 else None))
                q.wait("scalar", TE)
                q.emit("scalar", lambda e, kb=kb: e.copy(
                    out=V2[:, kb * 128:(kb + 1) * 128], in_=p1a[:, 0:128]), inc=SC)
            for hd in range(2):
                dst = rowA if hd == 0 else rowB
                for tc in range(4):
                    q.wait("scalar", TE)
                    q.emit("scalar", lambda e, hd=hd, tc=tc: e.activation(
                        out=casA[hd * 64:hd * 64 + 64, :],
                        in_=Q2[hd * 64:hd * 64 + 64, tc * 512:(tc + 1) * 512],
                        func=AF.Square), inc=SC)
                    q.wait("tensor", SC)
                    q.wait("tensor", VE)
                    q.emit("tensor", lambda e, hd=hd, tc=tc: e.matmul(
                        out=p1a[0:1, 0:512],
                        lhsT=onesc_bf[hd * 64:hd * 64 + 64, :],
                        rhs=casA[hd * 64:hd * 64 + 64, :],
                        start=True, stop=True), inc=TE)
                    q.wait("vector", TE)
                    q.emit("vector", lambda e, dst=dst, tc=tc: e.tensor_copy(
                        out=dst[0:1, tc * 512:(tc + 1) * 512],
                        in_=p1a[0:1, 0:512]), inc=VE)
            for hd in range(2):
                nq = rowA if hd == 0 else rowB
                for tk in range(5):
                    q.wait("scalar", TE)
                    q.emit("scalar", lambda e, hd=hd, tk=tk: e.activation(
                        out=casB[hd * 64:hd * 64 + 64, :],
                        in_=K2[hd * 64:hd * 64 + 64, tk * 512:(tk + 1) * 512],
                        func=AF.Square), inc=SC)
                    q.wait("tensor", SC)
                    q.wait("tensor", VE)
                    q.emit("tensor", lambda e, hd=hd, tk=tk: e.matmul(
                        out=p1b[0:1, 0:512],
                        lhsT=onesc_bf[hd * 64:hd * 64 + 64, :],
                        rhs=casB[hd * 64:hd * 64 + 64, :],
                        start=True, stop=True), inc=TE)
                    q.wait("vector", TE)
                    q.emit("vector", lambda e, nq=nq, tk=tk: e.reduce_max(
                        out=nq[0:1, 2049 + tk:2050 + tk], in_=p1b[0:1, 0:512],
                        axis=mybir.AxisListType.X), inc=VE)
                q.emit("vector", lambda e, nq=nq: e.reduce_max(
                    out=nq[0:1, 2056:2057], in_=nq[0:1, 2049:2054],
                    axis=mybir.AxisListType.X), inc=VE)
                q.emit("vector", lambda e, nq=nq: e.tensor_single_scalar(
                    out=nq[0:1, 0:T], in_=nq[0:1, 0:T],
                    scalar=nq[0:1, 2056:2057], op=ALU.mult), inc=VE)
                q.wait("scalar", VE)
                q.emit("scalar", lambda e, nq=nq: e.activation(
                    out=nq[0:1, 0:T], in_=nq[0:1, 0:T], func=AF.Ln,
                    bias=cbias_sb[0:1, 1:2]), inc=SC)
                q.wait("scalar", SC)
                q.emit("scalar", lambda e, nq=nq: e.activation(
                    out=nq[0:1, 0:T], in_=nq[0:1, 0:T], func=AF.Exp,
                    scale=0.5, bias=cbias_sb[0:1, 2:3]), inc=SC)
                q.wait("vector", SC)
                q.emit("vector", lambda e, nq=nq: e.tensor_scalar_mul(
                    out=rowbf[0:1, 0:T], in0=nq[0:1, 0:T], scalar1=-1.0), inc=VE)
                q.wait("tensor", VE)
                for qb in range(16):
                    q.emit("tensor", lambda e, qb=qb: e.transpose(
                        out=p1a.bitcast(BF16)[0:128, 0:1],
                        in_=rowbf[0:1, qb * 128: qb * 128 + 128],
                        identity=idm_bf[0:1, 0:1]), inc=TE)
                    q.wait("scalar", TE)
                    q.emit("scalar", lambda e, hd=hd, qb=qb: e.copy(
                        out=nB[:, hd * 16 + qb:hd * 16 + qb + 1],
                        in_=p1a.bitcast(BF16)[0:128, 0:1]), inc=SC)
                    q.wait("tensor", SC)
            for qb in range(16):
                mtile = amsk[:, (qb % 2) * 640:(qb % 2) * 640 + 640]
                q.wait("gpsimd", VE)
                dma("gpsimd", mtile, am_h[qb])
                q.wait("vector", GP)
                for hd in range(2):
                    b0 = hd * 64
                    q.wait("tensor", SC)
                    q.emit("tensor", lambda e, b0=b0, qb=qb: e.matmul(
                        out=p2[:, 0:512],
                        lhsT=Q2[b0:b0 + 64, qb * 128:(qb + 1) * 128],
                        rhs=K2[b0:b0 + 64, qb * 128:qb * 128 + 512],
                        start=True, stop=True), inc=None)
                    q.emit("tensor", lambda e, b0=b0, qb=qb: e.matmul(
                        out=p2[:, 512:640],
                        lhsT=Q2[b0:b0 + 64, qb * 128:(qb + 1) * 128],
                        rhs=K2[b0:b0 + 64, qb * 128 + 512:qb * 128 + 640],
                        start=True, stop=True), inc=TE)
                    q.wait("scalar", TE)
                    q.emit("scalar", lambda e, hd=hd, qb=qb: e.activation(
                        out=expb[:], in_=p2[:, 0:640], func=AF.Exp, scale=0.125,
                        bias=nB[:, hd * 16 + qb:hd * 16 + qb + 1]), inc=SC)
                    q.wait("vector", SC)
                    q.emit("vector", lambda e, mtile=mtile: e.scalar_tensor_tensor(
                        out=mexp[:], in0=expb[:], scalar=1.0, in1=mtile,
                        op0=ALU.mult, op1=ALU.mult, accum_out=rsum[:]), inc=VE)
                    q.emit("vector", lambda e: e.reciprocal(
                        out=rrec[:], in_=rsum[:]), inc=VE)
                    q.emit("vector", lambda e: e.tensor_scalar_mul(
                        out=mexp[:], in0=mexp[:], scalar1=rrec[:]), inc=VE)
                    q.wait("tensor", VE)
                    for kb in range(5):
                        q.emit("tensor", lambda e, kb=kb: e.transpose(
                            out=p1a.bitcast(BF16)[:, kb * 128:(kb + 1) * 128],
                            in_=mexp[:, kb * 128:(kb + 1) * 128],
                            identity=idm_bf[:]), inc=(TE if kb == 4 else None))
                    q.wait("scalar", TE)
                    q.emit("scalar", lambda e: e.copy(
                        out=attnT[:], in_=p1a.bitcast(BF16)[:, 0:640]), inc=SC)
                    q.wait("tensor", SC)
                    for kb in range(5):
                        q.emit("tensor", lambda e, kb=kb, qb=qb, b0=b0: e.matmul(
                            out=p1b[b0:b0 + 64, 0:128],
                            lhsT=V2[:, (qb + kb) * 128 + b0:(qb + kb) * 128 + b0 + 64],
                            rhs=attnT[:, kb * 128:(kb + 1) * 128],
                            start=(kb == 0), stop=(kb == 4)),
                            inc=(TE if kb == 4 else None))
                    q.wait("scalar", TE)
                    q.emit("scalar", lambda e, qb=qb, b0=b0: e.copy(
                        out=Y2[b0:b0 + 64, qb * 128:(qb + 1) * 128],
                        in_=p1b[b0:b0 + 64, 0:128]), inc=SC)
            q.wait("tensor", SC)
            for tc in range(4):
                for dc in range(8):
                    q.wait("tensor", VE)
                    q.emit("tensor", lambda e, tc=tc, dc=dc: e.matmul(
                        out=p1a[:], lhsT=woutp[:, dc * 128:(dc + 1) * 128],
                        rhs=Y2[:, tc * 512:(tc + 1) * 512],
                        start=True, stop=True), inc=TE)
                    q.wait("vector", TE)
                    q.emit("vector", lambda e, tc=tc, dc=dc: e.tensor_add(
                        out=x_sb[dc][:, tc * 512:(tc + 1) * 512],
                        in0=x_sb[dc][:, tc * 512:(tc + 1) * 512],
                        in1=p1a[:]), inc=VE)
        q.barrier(ALLS)

    if 3 <= stage_limit < 11:
        mcur = [ARENA0]
        lnf = alloc("lnf", [128, 8 * T], BF16, mcur)
        lnf_c = [lnf[:, i * T:(i + 1) * T] for i in range(8)]
        MSNAP = list(mcur)
        lgbuf = alloc("lgbuf", [8, T], F32, mcur)
        lg_tm = alloc("lg_tm", [128, 8], F32, mcur)
        mcolA = alloc("mcolA", [128, 1], F32, mcur)
        mcolB = alloc("mcolB", [128, 1], F32, mcur)
        mcolC = alloc("mcolC", [128, 8], F32, mcur)
        cwf = alloc("cwf", [8, T], F32, mcur)
        cwb0 = alloc("cwb0", [128, 8], F32, mcur)
        ecur = list(MSNAP)
        cwrow = alloc("cwrow", [1, T], F32, ecur)
        cwb = alloc("cwb", [128, 256], BF16, ecur)
        wgl_sb = alloc("wgl_sb", [128, 64], BF16, mcur)
        wgu_sb = alloc("wgu_sb", [128, 8 * 16 * 128], BF16, ecur)
        wdn_sb = alloc("wdn_sb", [128, 8 * 8 * 128], BF16, ecur)
        gact = alloc("gact", [128, 256], BF16, ecur)
        uact = alloc("uact", [128, 256], BF16, ecur)
        act = [alloc(f"act{i}", [128, 256], BF16, ecur) for i in range(8)]

        layer_norm(T, lambda dc, ws, wn: x_sb[dc][:, ws:ws + wn], lnf_c)
        dma("gpsimd", wgl_sb[:], wgl_h[:])
        q.barrier(ALLS)
        for tc4 in range(4):
            q.wait("tensor", SC)
            for kc in range(8):
                q.emit("tensor", lambda e, tc4=tc4, kc=kc: e.matmul(
                    out=p1a[0:8, 0:512], lhsT=wgl_sb[:, kc * 8:(kc + 1) * 8],
                    rhs=lnf_c[kc][:, tc4 * 512:(tc4 + 1) * 512],
                    start=(kc == 0), stop=(kc == 7)),
                    inc=(TE if kc == 7 else None))
            q.wait("scalar", TE)
            q.emit("scalar", lambda e, tc4=tc4: e.copy(
                out=lgbuf[:, tc4 * 512:(tc4 + 1) * 512], in_=p1a[0:8, 0:512]),
                inc=SC)
        for tb in range(16):
            q.wait("tensor", SC)
            q.wait("tensor", VE)
            q.emit("tensor", lambda e, tb=tb: e.transpose(
                out=p1b[0:128, 0:8], in_=lgbuf[:, tb * 128:(tb + 1) * 128],
                identity=idm_f[0:8, 0:8]), inc=TE)
            q.wait("vector", TE)
            q.emit("vector", lambda e: e.tensor_copy(out=lg_tm[:], in_=p1b[:, 0:8]),
                   inc=VE)
            q.emit("vector", lambda e: e.reduce_max(
                out=mcolA[:], in_=lg_tm[:], axis=mybir.AxisListType.X,
                negate=True), inc=VE)
            q.wait("scalar", VE)
            q.emit("scalar", lambda e: e.activation(
                out=lg_tm[:], in_=lg_tm[:], func=AF.Exp, bias=mcolA[:],
                accum_out=mcolB[:]), inc=SC)
            q.wait("vector", SC)
            q.emit("vector", lambda e: e.reciprocal(out=mcolB[:], in_=mcolB[:]),
                   inc=VE)
            q.emit("vector", lambda e: e.tensor_scalar_mul(
                out=lg_tm[:], in0=lg_tm[:], scalar1=mcolB[:]), inc=VE)
            q.emit("vector", lambda e: e.reduce_max(
                out=mcolA[:], in_=lg_tm[:], axis=mybir.AxisListType.X), inc=VE)
            q.emit("vector", lambda e: e.tensor_scalar(
                out=mcolC[:, 0:8], in0=lg_tm[:], scalar1=mcolA[:], scalar2=None,
                op0=ALU.is_equal), inc=VE)
            q.emit("vector", lambda e: e.tensor_scalar(
                out=mcolC[:, 0:8], in0=mcolC[:, 0:8], scalar1=-1.0, scalar2=1.0,
                op0=ALU.mult, op1=ALU.add), inc=VE)
            q.emit("vector", lambda e: e.tensor_mul(
                out=mcolC[:, 0:8], in0=mcolC[:, 0:8], in1=lg_tm[:]), inc=VE)
            q.emit("vector", lambda e: e.reduce_max(
                out=mcolB[:], in_=mcolC[:, 0:8], axis=mybir.AxisListType.X),
                inc=VE)
            q.emit("vector", lambda e: e.scalar_tensor_tensor(
                out=cwb0[:, 1:2], in0=mcolA[:], scalar=1e-8, in1=mcolB[:],
                op0=ALU.add, op1=ALU.add), inc=VE)
            q.emit("vector", lambda e: e.reciprocal(
                out=cwb0[:, 1:2], in_=cwb0[:, 1:2]), inc=VE)
            q.emit("vector", lambda e: e.tensor_mul(
                out=cwb0[:, 2:3], in0=mcolA[:], in1=cwb0[:, 1:2]), inc=VE)
            q.emit("vector", lambda e: e.tensor_mul(
                out=cwb0[:, 3:4], in0=mcolB[:], in1=cwb0[:, 1:2]), inc=VE)
            q.emit("vector", lambda e: e.tensor_scalar(
                out=mcolC[:, 0:8], in0=mcolC[:, 0:8], scalar1=mcolB[:],
                scalar2=None, op0=ALU.is_equal), inc=VE)
            q.emit("vector", lambda e: e.tensor_scalar_mul(
                out=mcolC[:, 0:8], in0=mcolC[:, 0:8], scalar1=cwb0[:, 3:4]),
                inc=VE)
            q.emit("vector", lambda e: e.tensor_scalar(
                out=cwb0[:, 8:16] if False else lg_tm[:], in0=lg_tm[:],
                scalar1=mcolA[:], scalar2=None, op0=ALU.is_equal), inc=VE)
            q.emit("vector", lambda e: e.tensor_scalar_mul(
                out=lg_tm[:], in0=lg_tm[:], scalar1=cwb0[:, 2:3]), inc=VE)
            q.emit("vector", lambda e: e.tensor_add(
                out=mcolC[:, 0:8], in0=mcolC[:, 0:8], in1=lg_tm[:]), inc=VE)
            q.wait("tensor", VE)
            q.emit("tensor", lambda e, tb=tb: e.transpose(
                out=p1b[0:8, 0:128], in_=mcolC[:, 0:8],
                identity=idm_f[:]), inc=TE)
            q.wait("scalar", TE)
            q.emit("scalar", lambda e, tb=tb: e.copy(
                out=cwf[:, tb * 128:(tb + 1) * 128], in_=p1b[0:8, 0:128]), inc=SC)
        q.wait("sync", SC)
        dma("sync", bncCW[:], cwf[:])
        q.barrier(ALLS)
        for ex in range(9):
            q.wait("gpsimd", TE)
            dma("gpsimd", wgu_sb[:], wgu_h[ex])
            dma("gpsimd", wdn_sb[:], wdn_h[ex])
            if ex < 8:
                q.wait("sync", TE)
                dma("sync", cwrow[:], bncCW[ex:ex + 1, :])
            q.wait("tensor", GP)
            q.wait("tensor", SY)
            for tc in range(8):
                if ex < 8:
                    q.wait("tensor", SC)
                    q.emit("tensor", lambda e, tc=tc: e.matmul(
                        out=p1b[:, 0:256], lhsT=onesr_f[:],
                        rhs=cwrow[0:1, tc * 256:(tc + 1) * 256],
                        start=True, stop=True), inc=TE)
                    q.wait("scalar", TE)
                    q.emit("scalar", lambda e: e.copy(
                        out=cwb[:], in_=p1b[:, 0:256]), inc=SC)
                for ic in range(8):
                    q.wait("tensor", SC)
                    q.wait("tensor", VE)
                    for kc in range(8):
                        q.emit("tensor", lambda e, tc=tc, ic=ic, kc=kc: e.matmul(
                            out=p2[:, 0:256],
                            lhsT=wgu_sb[:, (kc * 16 + ic) * 128:(kc * 16 + ic + 1) * 128],
                            rhs=lnf_c[kc][:, tc * 256:(tc + 1) * 256],
                            start=(kc == 0), stop=(kc == 7)), inc=None)
                        q.emit("tensor", lambda e, tc=tc, ic=ic, kc=kc: e.matmul(
                            out=p2[:, 512:768],
                            lhsT=wgu_sb[:, (kc * 16 + 8 + ic) * 128:(kc * 16 + 9 + ic) * 128],
                            rhs=lnf_c[kc][:, tc * 256:(tc + 1) * 256],
                            start=(kc == 0), stop=(kc == 7)),
                            inc=(TE if kc == 7 else None))
                    q.wait("scalar", TE)
                    q.emit("scalar", lambda e: e.activation(
                        out=gact[:], in_=p2[:, 0:256], func=AF.Silu), inc=SC)
                    q.emit("scalar", lambda e: e.copy(
                        out=uact[:], in_=p2[:, 512:768]), inc=SC)
                    q.wait("vector", SC)
                    q.emit("vector", lambda e: e.tensor_mul(
                        out=gact[:], in0=gact[:], in1=uact[:]), inc=VE)
                    if ex < 8:
                        q.emit("vector", lambda e, ic=ic: e.tensor_mul(
                            out=act[ic][:], in0=gact[:], in1=cwb[:]), inc=VE)
                    else:
                        q.emit("vector", lambda e, ic=ic: e.tensor_copy(
                            out=act[ic][:], in_=gact[:]), inc=VE)
                for dc in range(8):
                    q.wait("tensor", VE)
                    for ic in range(8):
                        q.emit("tensor", lambda e, dc=dc, ic=ic: e.matmul(
                            out=p1a[:, 0:256],
                            lhsT=wdn_sb[:, (ic * 8 + dc) * 128:(ic * 8 + dc + 1) * 128],
                            rhs=act[ic][:], start=(ic == 0), stop=(ic == 7)),
                            inc=(TE if ic == 7 else None))
                    q.wait("vector", TE)
                    q.emit("vector", lambda e, dc=dc, tc=tc: e.tensor_add(
                        out=x_sb[dc][:, tc * 256:(tc + 1) * 256],
                        in0=x_sb[dc][:, tc * 256:(tc + 1) * 256],
                        in1=p1a[:, 0:256]), inc=VE)
        q.barrier(ALLS)


    if 4 <= stage_limit < 11:
        ccur = [ARENA0]
        hlnc = alloc("hlnc", [128, 8 * CH], BF16, ccur)
        hlnc_c = [hlnc[:, i * CH:(i + 1) * CH] for i in range(8)]
        xit = [alloc(f"xit{i}", [128, 515], BF16, ccur) for i in range(2)]
        xcb = alloc("xcb", [128, 16 * CH], BF16, ccur)
        dtrows = alloc("dtrows", [64, CH], BF16, ccur)
        Bsb = alloc("Bsb", [16, CH], BF16, ccur)
        Csb = alloc("Csb", [16, CH], BF16, ccur)
        nsel_sb = alloc("nsel_sb", [16, 8 * 128], BF16, ccur)
        Brep = alloc("Brep", [128, 16 * CH], BF16, ccur)
        Crep = alloc("Crep", [128, 16 * CH], BF16, ccur)
        dtile = alloc("dtile", [128, CH], BF16, ccur)
        dutile = alloc("dutile", [128, CH], BF16, ccur)
        atile = alloc("atile", [128, CH], BF16, ccur)
        wtile = alloc("wtile", [128, CH], BF16, ccur)
        htile = alloc("htile", [128, CH], BF16, ccur)
        hctmp = alloc("hctmp", [128, CH], BF16, ccur)
        atile2 = alloc("atile2", [128, CH], BF16, ccur)
        htile2 = alloc("htile2", [128, CH], BF16, ccur)
        hctmp2 = alloc("hctmp2", [128, CH], BF16, ccur)
        yacc = alloc("yacc", [128, 8 * CH], BF16, ccur)
        ytile = lnt1[:, 0:256].bitcast(BF16)[:, 0:CH]
        zst = alloc("zst", [128, CH], BF16, ccur)
        ygated = xcb
        winr = alloc("winr", [128, 2048], BF16, ccur)
        zwr = alloc("zwr", [128, 1024], BF16, ccur)
        moutr = winr
        xprojw = alloc("xprojw", [128, 16 * 96], BF16, ccur)
        dtwsb = alloc("dtwsb", [64, DI], BF16, ccur)
        convr = alloc("convr", [128, 1024], BF16, ccur)
        fwdt = alloc("fwdt", [128, CH], BF16, ccur)
        fwdin = alloc("fwdin", [128, 8 * CH], BF16, ccur)
        owr = alloc("owr", [128, 8 * 128], BF16, ccur)
        kcur = [ARENA0]
        gath = alloc("gath", [128, 8 * 304], F32, kcur)
        pselbig_sb = alloc("pselbig_sb", [128, 8 * 304], F32, kcur)
        ccbuf = alloc("ccbuf", [128, 304], F32, kcur)
        ext_sb = alloc("ext_sb", [128, 304], F32, kcur)

        def chain(ch, phase):
            chunks = range(NCH) if phase == 0 else range(NCH - 1, -1, -1)
            cdb = 0 if ch == "L" else 1
            pl = {"scanm": [VE.n, VE.n], "expm": [SC.n, SC.n],
                  "hcm": [GC.n, GC.n], "savem": [SC.n, SC.n],
                  "pam": [VE.n, VE.n]}
            q.wait("gpsimd", TE)
            dma("gpsimd", xprojw[:], xproj_h[ch][:])
            dma("gpsimd", dtwsb[:], dtw_h[ch][:])
            rv = slice(None, None, -1) if phase == 1 else slice(None, None, None)
            for cc in chunks:
                t0 = cc * CH
                layer_norm(CH, lambda dc, ws, wn, t0=t0:
                           x_sb[dc][:, t0 + ws:t0 + ws + wn],
                           hlnc_c, affine=True, double=True)
                q.wait("tensor", VE)
                for dc in range(16):
                    g = dc % 2
                    if g == 0:
                        q.wait("gpsimd", TE)
                        dma("gpsimd", winr[:], win_h[ch][dc // 2])
                        dma("gpsimd", convr[:], convd_h[cdb, dc // 2])
                        q.wait("tensor", GP)
                    xi = xit[g]
                    q.wait("scalar", TE)
                    if phase == 0:
                        q.emit("scalar", lambda e, dc=dc, ch=ch, xi=xi: e.copy(
                            out=xi[:, 0:3], in_=ximg[ch][:, dc * 3:dc * 3 + 3]),
                            inc=SC)
                    else:
                        q.emit("scalar", lambda e, dc=dc, ch=ch, xi=xi: e.copy(
                            out=xi[:, 512:515], in_=ximg[ch][:, dc * 3:dc * 3 + 3]),
                            inc=SC)
                    q.wait("tensor", SC)
                    for kc in range(8):
                        q.emit("tensor", lambda e, g=g, kc=kc: e.matmul(
                            out=p2[:, 0:512],
                            lhsT=winr[:, (kc * 2 + g) * 128:(kc * 2 + g + 1) * 128],
                            rhs=hlnc_c[kc], start=(kc == 0), stop=(kc == 7)),
                            inc=(TE if kc == 7 else None))
                    q.wait("scalar", TE)
                    xoff = 3 if phase == 0 else 0
                    q.emit("scalar", lambda e, xi=xi, xoff=xoff: e.copy(
                        out=xi[:, xoff:xoff + 512], in_=p2[:, 0:512]), inc=SC)
                    moff = 512 if phase == 0 else 0
                    q.emit("scalar", lambda e, xi=xi, dc=dc, ch=ch, moff=moff: e.copy(
                        out=ximg[ch][:, dc * 3:dc * 3 + 3], in_=xi[:, moff:moff + 3]),
                        inc=SC)
                    q.wait("tensor", SC)
                    for j in range(4):
                        q.emit("tensor", lambda e, g=g, j=j, xi=xi: e.matmul(
                            out=p1a[:, 0:512],
                            lhsT=convr[:, (g * 4 + j) * 128:(g * 4 + j + 1) * 128],
                            rhs=xi[:, j:j + 512], start=(j == 0), stop=(j == 3)),
                            inc=(TE if j == 3 else None))
                    q.wait("scalar", TE)
                    q.emit("scalar", lambda e, dc=dc, ch=ch: e.activation(
                        out=xcb[:, dc * CH:(dc + 1) * CH], in_=p1a[:, 0:512],
                        func=AF.Silu, bias=convb_sb[ch][:, dc:dc + 1]), inc=SC)
                q.wait("tensor", SC)
                q.wait("tensor", VE)
                q.wait("tensor", GC)
                for kc in range(16):
                    q.emit("tensor", lambda e, kc=kc: e.matmul(
                        out=p4[0:64, 0:512], lhsT=xprojw[:, kc * 96:kc * 96 + 64],
                        rhs=xcb[:, kc * CH:(kc + 1) * CH],
                        start=(kc == 0), stop=(kc == 15)), inc=None)
                    q.emit("tensor", lambda e, kc=kc: e.matmul(
                        out=p4[0:16, 512:1024],
                        lhsT=xprojw[:, kc * 96 + 64:kc * 96 + 80],
                        rhs=xcb[:, kc * CH:(kc + 1) * CH],
                        start=(kc == 0), stop=(kc == 15)), inc=None)
                    q.emit("tensor", lambda e, kc=kc: e.matmul(
                        out=p4[0:16, 1024:1536],
                        lhsT=xprojw[:, kc * 96 + 80:kc * 96 + 96],
                        rhs=xcb[:, kc * CH:(kc + 1) * CH],
                        start=(kc == 0), stop=(kc == 15)),
                        inc=(TE if kc == 15 else None))
                q.wait("scalar", TE)
                q.emit("scalar", lambda e: e.copy(out=dtrows[:], in_=p4[0:64, 0:512]),
                       inc=SC)
                q.emit("scalar", lambda e: e.copy(out=Bsb[:], in_=p4[0:16, 512:1024]),
                       inc=SC)
                q.emit("scalar", lambda e: e.copy(out=Csb[:], in_=p4[0:16, 1024:1536]),
                       inc=SC)
                for n in range(16):
                    if n % 8 == 0:
                        q.wait("gpsimd", TE)
                        dma("gpsimd", nsel_sb[:], nsel_h[:, n * 128:(n + 8) * 128])
                        q.wait("tensor", GP)
                    q.wait("tensor", SC)
                    q.emit("tensor", lambda e, n=n: e.matmul(
                        out=p1b[:, 0:512], lhsT=nsel_sb[:, (n % 8) * 128:(n % 8 + 1) * 128],
                        rhs=Bsb[:], start=True, stop=True), inc=TE)
                    q.wait("scalar", TE)
                    q.emit("scalar", lambda e, n=n: e.copy(
                        out=Brep[:, n * CH:(n + 1) * CH], in_=p1b[:, 0:512]), inc=SC)
                    q.wait("tensor", SC)
                    q.emit("tensor", lambda e, n=n: e.matmul(
                        out=p1b[:, 0:512], lhsT=nsel_sb[:, (n % 8) * 128:(n % 8 + 1) * 128],
                        rhs=Csb[:], start=True, stop=True), inc=TE)
                    q.wait("scalar", TE)
                    q.emit("scalar", lambda e, n=n: e.copy(
                        out=Crep[:, n * CH:(n + 1) * CH], in_=p1b[:, 0:512]), inc=SC)
                for dc in range(16):
                    q.wait("tensor", SC)
                    q.wait("tensor", VE)
                    q.emit("tensor", lambda e, dc=dc: e.matmul(
                        out=p1b[:, 0:512], lhsT=dtwsb[:, dc * 128:(dc + 1) * 128],
                        rhs=dtrows[:], start=True, stop=True), inc=TE)
                    q.wait("scalar", TE)
                    q.emit("scalar", lambda e, dc=dc, ch=ch: e.activation(
                        out=lnt1[:, 0:512], in_=p1b[:, 0:512], func=AF.Exp,
                        bias=dtb_sb[ch][:, dc:dc + 1]), inc=SC)
                    q.wait("scalar", SC)
                    q.emit("scalar", lambda e: e.activation(
                        out=dtile[:], in_=lnt1[:, 0:512], func=AF.Ln, bias=1.0),
                        inc=SC)
                    q.wait("vector", SC)
                    q.emit("vector", lambda e, dc=dc: e.tensor_mul(
                        out=dutile[:], in0=dtile[:],
                        in1=xcb[:, dc * CH:(dc + 1) * CH]), inc=VE)
                    at2 = (atile, atile2)
                    ht2 = (htile, htile2)
                    hc2 = (hctmp, hctmp2)
                    sv = 511 if phase == 0 else 0
                    for n in range(17):
                        if n < 16:
                            q.wait("scalar", VE, pl["scanm"][n % 2])
                            q.emit("scalar", lambda e, n=n, ch=ch: e.activation(
                                out=at2[n % 2][:], in_=dtile[:], func=AF.Exp,
                                scale=avec_sb[ch][:, n:n + 1]), inc=SC)
                            pl["expm"][n % 2] = SC.n
                            q.emit("vector", lambda e, n=n: e.tensor_mul(
                                out=wtile[:], in0=dutile[:],
                                in1=Brep[:, n * CH:(n + 1) * CH]), inc=VE)
                            q.wait("vector", SC, pl["expm"][n % 2])
                            q.wait("vector", SC, pl["savem"][n % 2])
                            q.emit("vector", lambda e, n=n, dc=dc, ch=ch, rv=rv:
                                   e.tensor_tensor_scan(
                                out=ht2[n % 2][:, rv], data0=at2[n % 2][:, rv],
                                data1=wtile[:, rv],
                                initial=stbuf[ch][:, dc * 16 + n:dc * 16 + n + 1],
                                op0=ALU.mult, op1=ALU.add), inc=VE)
                            pl["scanm"][n % 2] = VE.n
                        if n >= 1:
                            m = n - 1
                            q.wait("scalar", VE, pl["scanm"][m % 2])
                            q.emit("scalar", lambda e, m=m, dc=dc, ch=ch, sv=sv:
                                   e.copy(
                                out=stbuf[ch][:, dc * 16 + m:dc * 16 + m + 1],
                                in_=ht2[m % 2][:, sv:sv + 1]), inc=SC)
                            pl["savem"][m % 2] = SC.n
                            q.emit("vector", lambda e, m=m: e.tensor_mul(
                                out=hc2[m % 2][:], in0=ht2[m % 2][:],
                                in1=Crep[:, m * CH:(m + 1) * CH]), inc=VE)
                        if n >= 2 and n % 2 == 0:
                            p = (n - 2) // 2
                            q.emit("vector", lambda e, p=p: e.tensor_add(
                                out=yacc[:, p * CH:(p + 1) * CH],
                                in0=hc2[0][:], in1=hc2[1][:]), inc=VE)
                    q.emit("vector", lambda e: e.tensor_add(
                        out=yacc[:, 0:CH], in0=yacc[:, 0:CH], in1=yacc[:, CH:2 * CH]),
                        inc=VE)
                    q.emit("vector", lambda e: e.tensor_add(
                        out=yacc[:, 2 * CH:3 * CH], in0=yacc[:, 2 * CH:3 * CH],
                        in1=yacc[:, 3 * CH:4 * CH]), inc=VE)
                    q.emit("vector", lambda e: e.tensor_add(
                        out=yacc[:, 4 * CH:5 * CH], in0=yacc[:, 4 * CH:5 * CH],
                        in1=yacc[:, 5 * CH:6 * CH]), inc=VE)
                    q.emit("vector", lambda e: e.tensor_add(
                        out=yacc[:, 6 * CH:7 * CH], in0=yacc[:, 6 * CH:7 * CH],
                        in1=yacc[:, 7 * CH:8 * CH]), inc=VE)
                    q.emit("vector", lambda e: e.tensor_add(
                        out=yacc[:, 0:CH], in0=yacc[:, 0:CH],
                        in1=yacc[:, 2 * CH:3 * CH]), inc=VE)
                    q.emit("vector", lambda e: e.tensor_add(
                        out=yacc[:, 4 * CH:5 * CH], in0=yacc[:, 4 * CH:5 * CH],
                        in1=yacc[:, 6 * CH:7 * CH]), inc=VE)
                    q.emit("vector", lambda e: e.tensor_add(
                        out=ytile[:], in0=yacc[:, 0:CH], in1=yacc[:, 4 * CH:5 * CH]),
                        inc=VE)
                    q.wait("gpsimd", TE)
                    dma("gpsimd", zwr[:], wz_h[ch][dc])
                    q.wait("tensor", GP)
                    q.wait("tensor", SC)
                    for kc in range(8):
                        q.emit("tensor", lambda e, kc=kc: e.matmul(
                            out=p2[:, 512:1024],
                            lhsT=zwr[:, kc * 128:(kc + 1) * 128],
                            rhs=hlnc_c[kc], start=(kc == 0), stop=(kc == 7)),
                            inc=(TE if kc == 7 else None))
                    q.wait("scalar", TE)
                    q.emit("scalar", lambda e: e.activation(
                        out=zst[:], in_=p2[:, 512:1024], func=AF.Silu), inc=SC)
                    q.wait("vector", SC)
                    q.emit("vector", lambda e, dc=dc, ch=ch: e.scalar_tensor_tensor(
                        out=wtile[:], in0=xcb[:, dc * CH:(dc + 1) * CH],
                        scalar=dv_sb[ch][:, dc:dc + 1], in1=ytile[:],
                        op0=ALU.mult, op1=ALU.add), inc=VE)
                    q.emit("vector", lambda e, dc=dc: e.tensor_mul(
                        out=ygated[:, dc * CH:(dc + 1) * CH], in0=wtile[:],
                        in1=zst[:]), inc=VE)
                for mc in range(8):
                    q.wait("gpsimd", TE)
                    q.wait("gpsimd", SY)
                    dma("gpsimd", moutr[:], wmout_h[ch][mc])
                    q.wait("tensor", GP)
                    q.wait("tensor", SC)
                    q.wait("tensor", VE)
                    for ic in range(16):
                        q.emit("tensor", lambda e, ic=ic: e.matmul(
                            out=p2[:, 0:512], lhsT=moutr[:, ic * 128:(ic + 1) * 128],
                            rhs=ygated[:, ic * CH:(ic + 1) * CH],
                            start=(ic == 0), stop=(ic == 15)),
                            inc=(TE if ic == 15 else None))
                    q.wait("scalar", TE)
                    if phase == 0:
                        q.wait("scalar", SY)
                        q.emit("scalar", lambda e, mc=mc: e.copy(
                            out=fwdt[:], in_=p2[:, 0:512]), inc=SC)
                        q.wait("sync", SC)
                        q.emit("sync", lambda e, mc=mc, cc=cc: e.dma_start(
                            out=fwdscr[cc][:, mc * CH:(mc + 1) * CH],
                            in_=fwdt[:]), inc=SY, inck=16)
                    else:
                        q.emit("scalar", lambda e, mc=mc: e.copy(
                            out=yacc[:, mc * CH:(mc + 1) * CH], in_=p2[:, 0:512]),
                            inc=SC)
                if phase == 1:
                    layer_norm(CH, lambda dc, ws, wn, t0=t0:
                               x_sb[dc][:, t0 + ws:t0 + ws + wn], hlnc_c)
                    q.wait("gpsimd", TE)
                    dma("gpsimd", fwdin[:], fwdscr[cc])
                    q.wait("tensor", GP)
                    for dc in range(8):
                        q.wait("gpsimd", TE)
                        dma("gpsimd", owr[:], ow_h["L"][dc])
                        q.wait("tensor", GP)
                        q.wait("tensor", SC)
                        q.wait("tensor", VE)
                        for kc in range(8):
                            q.emit("tensor", lambda e, kc=kc: e.matmul(
                                out=p1a[:, 0:512],
                                lhsT=owr[:, kc * 128:(kc + 1) * 128],
                                rhs=fwdin[:, kc * CH:(kc + 1) * CH],
                                start=(kc == 0), stop=False), inc=TE)
                        q.wait("gpsimd", TE)
                        dma("gpsimd", owr[:], ow_h["R"][dc])
                        q.wait("tensor", GP)
                        for kc in range(8):
                            q.emit("tensor", lambda e, kc=kc: e.matmul(
                                out=p1a[:, 0:512],
                                lhsT=owr[:, kc * 128:(kc + 1) * 128],
                                rhs=yacc[:, kc * CH:(kc + 1) * CH],
                                start=False, stop=(kc == 7)),
                                inc=(TE if kc == 7 else None))
                        q.wait("vector", TE)
                        q.emit("vector", lambda e, dc=dc: e.scalar_tensor_tensor(
                            out=lnt1[:, 0:512], in0=p1a[:, 0:512],
                            scalar=outb_sb[:, dc:dc + 1], in1=hlnc_c[dc],
                            op0=ALU.add, op1=ALU.add), inc=VE)
                        q.emit("vector", lambda e, dc=dc, t0=t0: e.tensor_add(
                            out=x_sb[dc][:, t0:t0 + CH], in0=x_sb[dc][:, t0:t0 + CH],
                            in1=lnt1[:, 0:512]), inc=VE)

        q.emit("vector", lambda e: e.memset(stbuf["L"][:], 0.0), inc=VE)
        q.emit("vector", lambda e: e.memset(stbuf["R"][:], 0.0), inc=VE)
        q.emit("vector", lambda e: e.memset(ximg["L"][:], 0.0), inc=VE)
        q.emit("vector", lambda e: e.memset(ximg["R"][:], 0.0), inc=VE)
        q.barrier([VE])
        chain("L", 0)
        q.barrier(ALLS)
        if stage_limit == 6:
            pass
        if stage_limit != 6:
            dma("sync", pselbig_sb[:], pselbig_h[:])
            q.wait("vector", SY)
            q.emit("vector", lambda e: e.tensor_copy(out=ccbuf[:, 0:256],
                                                     in_=stbuf["L"][:]), inc=VE)
            q.emit("vector", lambda e: e.tensor_copy(out=ccbuf[:, 256:304],
                                                     in_=ximg["L"][:]), inc=VE)
            q.wait("sync", VE)
            dma("sync", cc_in[:], ccbuf[:])
            q.wait("gpsimd", SY)
            q.emit("gpsimd", lambda e: e.collective_compute(
                "AllGather", ALU.bypass, replica_groups=[list(range(8))],
                ins=[cc_in[:]], outs=[cc_out[:]]), inc=CC)
            q.wait("sync", CC)
            dma("sync", gath[:].rearrange("p (a b) -> p a b", a=8),
                cc_out.transpose([1, 0, 2]))
            q.wait("vector", SY)
            q.emit("vector", lambda e: e.tensor_mul(
                out=gath[:], in0=gath[:], in1=pselbig_sb[:]), inc=VE)
            for pr in ((0, 1), (2, 3), (4, 5), (6, 7), (0, 2), (4, 6), (0, 4)):
                q.emit("vector", lambda e, pr=pr: e.tensor_add(
                    out=gath[:, pr[0] * 304:pr[0] * 304 + 304],
                    in0=gath[:, pr[0] * 304:pr[0] * 304 + 304],
                    in1=gath[:, pr[1] * 304:pr[1] * 304 + 304]), inc=VE)
            q.wait("scalar", VE)
            q.emit("scalar", lambda e: e.copy(out=stbuf["R"][:], in_=gath[:, 0:256]),
                   inc=SC)
            for dc in range(16):
                q.emit("scalar", lambda e, dc=dc: e.copy(
                    out=ximg["R"][:, dc * 3:dc * 3 + 3],
                    in_=gath[:, 256 + dc * 3:256 + dc * 3 + 3][:, ::-1]), inc=SC)
            q.barrier(ALLS)
            chain("R", 1)
            q.barrier(ALLS)


    if 8 <= stage_limit < 11:
        fcur = [ARENA0]
        wring = alloc("wring", [128, 2 * 1024], BF16, fcur)
        wfr = [alloc(f"wfr{i}", [128, 1024], F32, fcur) for i in range(2)]
        ln3b = alloc("ln3b", [128, 8 * 512], BF16, fcur)
        ln3b_c = [ln3b[:, i * 512:(i + 1) * 512] for i in range(8)]
        ln3f = alloc("ln3f", [128, 8 * 512], F32, fcur)
        ln3f_c = [ln3f[:, i * 512:(i + 1) * 512] for i in range(8)]
        xq = alloc("xq", [128, 8 * 512], BF16, fcur)
        hact = alloc("hact", [128, 32 * 512], BF16, fcur)
        mxc = alloc("mxc", [128, 8], F32, fcur)
        sxrow = alloc("sxrow", [1, 512], F32, fcur)
        s2row = alloc("s2row", [1, 512], F32, fcur)
        dqg = alloc("dqg", [128, 512], F32, fcur)
        dqu = alloc("dqu", [128, 512], F32, fcur)
        dqd = alloc("dqd", [128, 512], F32, fcur)
        sxrep = alloc("sxrep", [128, 512], F32, fcur)
        gtmp = alloc("gtmp", [128, 512], BF16, fcur)
        wdr = alloc("wdr", [128, 4096], BF16, fcur)

        q.wait("vector", VE)
        q.emit("vector", lambda e: e.memset(wmean[:], 0.0), inc=VE)
        for j, wh in enumerate((wg2_h, wu2_h)):
            for mc in range(32):
                q.wait("sync", VE)
                dma("sync", wfr[mc % 2][:], wh[mc])
                q.wait("vector", SY)
                q.emit("vector", lambda e, mc=mc: e.reduce_sum(
                    out=lnt1[:, 0:1], in_=wfr[mc % 2][:],
                    axis=mybir.AxisListType.X, apply_absolute_value=True), inc=VE)
                q.emit("vector", lambda e, j=j: e.tensor_add(
                    out=wmean[:, j:j + 1], in0=wmean[:, j:j + 1],
                    in1=lnt1[:, 0:1]), inc=VE)
        for dc in range(8):
            for half in range(2):
                q.wait("sync", VE)
                dma("sync", wdr[:].bitcast(F32)[:, 0:2048],
                    wd2_h[dc][:, half * 2048:(half + 1) * 2048])
                q.wait("vector", SY)
                q.emit("vector", lambda e: e.reduce_sum(
                    out=lnt1[:, 0:1], in_=wdr[:].bitcast(F32)[:, 0:2048],
                    axis=mybir.AxisListType.X, apply_absolute_value=True), inc=VE)
                q.emit("vector", lambda e: e.tensor_add(
                    out=wmean[:, 2:3], in0=wmean[:, 2:3], in1=lnt1[:, 0:1]), inc=VE)
        q.wait("tensor", VE)
        q.emit("tensor", lambda e: e.matmul(
            out=p1a[0:1, 0:4], lhsT=onesc_f[:], rhs=wmean[:, 0:4],
            start=True, stop=True), inc=TE)
        q.wait("vector", TE)
        q.emit("vector", lambda e: e.tensor_scalar(
            out=cmean_sb[0:1, 0:4], in0=p1a[0:1, 0:4],
            scalar1=1.0 / (INNER * D), scalar2=1e-5, op0=ALU.mult,
            op1=ALU.max), inc=VE)
        q.emit("vector", lambda e: e.reciprocal(
            out=smean[0:1, 0:4], in_=cmean_sb[0:1, 0:4]), inc=VE)
        q.wait("tensor", VE)
        q.emit("tensor", lambda e: e.matmul(
            out=p1a[:, 0:4], lhsT=onesr_f[:], rhs=smean[0:1, 0:4],
            start=True, stop=True), inc=TE)
        q.wait("scalar", TE)
        q.emit("scalar", lambda e: e.copy(out=svec[:, 0:4], in_=p1a[:, 0:4]), inc=SC)
        for j, (wh, scr, nt) in enumerate(
                ((wg2_h, wqg_scr, 32), (wu2_h, wqu_scr, 32), (wd2_h, wqd_scr, 8))):
            sj = 2 if j == 2 else j
            for ti in range(nt):
                wf = wfr[ti % 2]
                wide = 4096 if j == 2 else 1024
                if j == 2:
                    q.wait("sync", SY)
                    q.wait("sync", SC)
                    dma("sync", wfr[0][:], wh[ti][:, 0:1024])
                    dma("sync", wfr[1][:], wh[ti][:, 1024:2048])
                    for half in range(2):
                        q.wait("scalar", SY)
                        q.wait("scalar", VE)
                        q.emit("scalar", lambda e, half=half, sj=sj: e.activation(
                            out=lnt1[:, 0:512], in_=wfr[half][:, 0:512],
                            func=AF.Copy, scale=svec[:, sj:sj + 1],
                            bias=float(RND)), inc=SC)
                        q.emit("scalar", lambda e, half=half, sj=sj: e.activation(
                            out=wfr[half][:, 0:512], in_=lnt1[:, 0:512],
                            func=AF.Copy, bias=float(-RND)), inc=SC)
                        q.emit("scalar", lambda e, half=half, sj=sj: e.activation(
                            out=lnt1[:, 0:512], in_=wfr[half][:, 512:1024],
                            func=AF.Copy, scale=svec[:, sj:sj + 1],
                            bias=float(RND)), inc=SC)
                        q.emit("scalar", lambda e, half=half, sj=sj: e.activation(
                            out=wfr[half][:, 512:1024], in_=lnt1[:, 0:512],
                            func=AF.Copy, bias=float(-RND)), inc=SC)
                        q.wait("vector", SC)
                        q.emit("vector", lambda e, half=half: e.tensor_scalar(
                            out=wring[:, half * 1024:(half + 1) * 1024],
                            in0=wfr[half][:], scalar1=1.0, scalar2=-1.0,
                            op0=ALU.min, op1=ALU.max), inc=VE)
                    q.wait("sync", VE)
                    q.emit("sync", lambda e, ti=ti: e.dma_start(
                        out=scr[ti][:, 0:2048], in_=wring[:]), inc=SY, inck=16)
                    q.wait("sync", SY)
                    dma("sync", wfr[0][:], wh[ti][:, 2048:3072])
                    dma("sync", wfr[1][:], wh[ti][:, 3072:4096])
                    for half in range(2):
                        q.wait("scalar", SY)
                        q.wait("scalar", VE)
                        q.emit("scalar", lambda e, half=half, sj=sj: e.activation(
                            out=lnt1[:, 0:512], in_=wfr[half][:, 0:512],
                            func=AF.Copy, scale=svec[:, sj:sj + 1],
                            bias=float(RND)), inc=SC)
                        q.emit("scalar", lambda e, half=half, sj=sj: e.activation(
                            out=wfr[half][:, 0:512], in_=lnt1[:, 0:512],
                            func=AF.Copy, bias=float(-RND)), inc=SC)
                        q.emit("scalar", lambda e, half=half, sj=sj: e.activation(
                            out=lnt1[:, 0:512], in_=wfr[half][:, 512:1024],
                            func=AF.Copy, scale=svec[:, sj:sj + 1],
                            bias=float(RND)), inc=SC)
                        q.emit("scalar", lambda e, half=half, sj=sj: e.activation(
                            out=wfr[half][:, 512:1024], in_=lnt1[:, 0:512],
                            func=AF.Copy, bias=float(-RND)), inc=SC)
                        q.wait("vector", SC)
                        q.emit("vector", lambda e, half=half: e.tensor_scalar(
                            out=wring[:, half * 1024:(half + 1) * 1024],
                            in0=wfr[half][:], scalar1=1.0, scalar2=-1.0,
                            op0=ALU.min, op1=ALU.max), inc=VE)
                    q.wait("sync", VE)
                    q.emit("sync", lambda e, ti=ti: e.dma_start(
                        out=scr[ti][:, 2048:4096], in_=wring[:]), inc=SY, inck=16)
                    q.wait("sync", SY)
                else:
                    q.wait("sync", SC)
                    q.wait("sync", VE)
                    dma("sync", wf[:], wh[ti])
                    q.wait("scalar", SY)
                    q.wait("scalar", VE)
                    q.emit("scalar", lambda e, wf=wf, sj=sj: e.activation(
                        out=lnt1[:, 0:512], in_=wf[:, 0:512], func=AF.Copy,
                        scale=svec[:, sj:sj + 1], bias=float(RND)), inc=SC)
                    q.emit("scalar", lambda e, wf=wf: e.activation(
                        out=wf[:, 0:512], in_=lnt1[:, 0:512], func=AF.Copy,
                        bias=float(-RND)), inc=SC)
                    q.emit("scalar", lambda e, wf=wf, sj=sj: e.activation(
                        out=lnt1[:, 0:512], in_=wf[:, 512:1024], func=AF.Copy,
                        scale=svec[:, sj:sj + 1], bias=float(RND)), inc=SC)
                    q.emit("scalar", lambda e, wf=wf: e.activation(
                        out=wf[:, 512:1024], in_=lnt1[:, 0:512], func=AF.Copy,
                        bias=float(-RND)), inc=SC)
                    q.wait("vector", SC)
                    q.emit("vector", lambda e, wf=wf, ti=ti: e.tensor_scalar(
                        out=wring[:, (ti % 2) * 1024:(ti % 2 + 1) * 1024],
                        in0=wf[:], scalar1=1.0, scalar2=-1.0,
                        op0=ALU.min, op1=ALU.max), inc=VE)
                    q.wait("sync", VE)
                    q.emit("sync", lambda e, ti=ti, scr=scr: e.dma_start(
                        out=scr[ti], in_=wring[:, (ti % 2) * 1024:(ti % 2 + 1) * 1024]),
                        inc=SY, inck=16)
                    q.wait("sync", SY)
        q.barrier(ALLS)
        for tc in range(4):
            t0 = tc * 512
            layer_norm(512, lambda dc, ws, wn, t0=t0:
                       x_sb[dc][:, t0 + ws:t0 + ws + wn],
                       ln3b_c, out32=ln3f_c)
            for tb in range(4):
                q.wait("tensor", VE)
                q.wait("tensor", SC)
                for dc in range(8):
                    q.emit("tensor", lambda e, tb=tb, dc=dc: e.transpose(
                        out=p2.bitcast(BF16)[:, dc * 128:(dc + 1) * 128],
                        in_=ln3b_c[dc][:, tb * 128:(tb + 1) * 128],
                        identity=idm_bf[:]), inc=(TE if dc == 7 else None))
                q.wait("vector", TE)
                q.emit("vector", lambda e, tb=tb: e.reduce_max(
                    out=mxc[:, tb:tb + 1], in_=p2.bitcast(BF16)[:, 0:1024],
                    axis=mybir.AxisListType.X, apply_absolute_value=True), inc=VE)
                q.emit("vector", lambda e, tb=tb: e.tensor_scalar(
                    out=mxc[:, 4 + tb:5 + tb], in0=mxc[:, tb:tb + 1],
                    scalar1=1e-5, scalar2=None, op0=ALU.max), inc=VE)
                q.emit("vector", lambda e, tb=tb: e.reciprocal(
                    out=mxc[:, 4 + tb:5 + tb], in_=mxc[:, 4 + tb:5 + tb]), inc=VE)
                q.emit("vector", lambda e, tb=tb: e.tensor_scalar_mul(
                    out=mxc[:, 4 + tb:5 + tb], in0=mxc[:, 4 + tb:5 + tb],
                    scalar1=127.0), inc=VE)
                q.wait("scalar", VE)
                q.emit("scalar", lambda e, tb=tb: e.copy(
                    out=gtmp[:, 0:1], in_=mxc[:, 4 + tb:5 + tb]), inc=SC)
                q.wait("tensor", SC)
                q.emit("tensor", lambda e, tb=tb: e.transpose(
                    out=p1b.bitcast(BF16)[0:1, 0:128],
                    in_=gtmp[:, 0:1], identity=idm_bf[:]), inc=TE)
                q.wait("vector", TE)
                q.emit("vector", lambda e, tb=tb: e.tensor_copy(
                    out=sxrow[0:1, tb * 128:(tb + 1) * 128],
                    in_=p1b.bitcast(BF16)[0:1, 0:128]), inc=VE)
            q.wait("tensor", VE)
            q.emit("tensor", lambda e: e.matmul(
                out=p1b[:, 0:512], lhsT=onesr_f[:], rhs=sxrow[0:1, 0:512],
                start=True, stop=True), inc=TE)
            q.wait("scalar", TE)
            q.emit("scalar", lambda e: e.copy(out=sxrep[:], in_=p1b[:, 0:512]),
                   inc=SC)
            for dc in range(8):
                q.wait("vector", SC)
                q.emit("vector", lambda e, dc=dc: e.tensor_mul(
                    out=lnt1[:, 0:512], in0=ln3f_c[dc], in1=sxrep[:]), inc=VE)
                q.emit("vector", lambda e, dc=dc: e.tensor_scalar(
                    out=xq[:, dc * 512:(dc + 1) * 512], in0=lnt1[:, 0:512],
                    scalar1=RND, scalar2=RND, op0=ALU.add, op1=ALU.subtract),
                    inc=VE)
            q.emit("vector", lambda e: e.reciprocal(
                out=s2row[0:1, 0:512], in_=sxrow[0:1, 0:512]), inc=VE)
            for j, dq in ((0, dqg), (1, dqu)):
                q.wait("vector", TE)
                q.emit("vector", lambda e, j=j: e.tensor_single_scalar(
                    out=lnt1[0:1, 0:512], in_=s2row[0:1, 0:512],
                    scalar=cmean_sb[0:1, j:j + 1], op=ALU.mult), inc=VE)
                q.wait("tensor", VE)
                q.wait("tensor", SC)
                q.emit("tensor", lambda e: e.matmul(
                    out=p1b[:, 0:512], lhsT=onesr_f[:], rhs=lnt1[0:1, 0:512],
                    start=True, stop=True), inc=TE)
                q.wait("scalar", TE)
                q.emit("scalar", lambda e, dq=dq: e.copy(out=dq[:], in_=p1b[:, 0:512]),
                       inc=SC)
            for mc in range(32):
                q.wait("gpsimd", TE)
                dma("gpsimd", wring[:, 0:1024], wqg_scr[mc])
                dma("gpsimd", wring[:, 1024:2048], wqu_scr[mc])
                q.wait("tensor", GP)
                q.wait("tensor", SC)
                q.wait("tensor", VE)
                for kc in range(8):
                    q.emit("tensor", lambda e, kc=kc: e.matmul(
                        out=p2[:, 0:512], lhsT=wring[:, kc * 128:(kc + 1) * 128],
                        rhs=xq[:, kc * 512:(kc + 1) * 512],
                        start=(kc == 0), stop=(kc == 7)), inc=None)
                    q.emit("tensor", lambda e, kc=kc: e.matmul(
                        out=p2[:, 512:1024],
                        lhsT=wring[:, 1024 + kc * 128:1024 + (kc + 1) * 128],
                        rhs=xq[:, kc * 512:(kc + 1) * 512],
                        start=(kc == 0), stop=(kc == 7)),
                        inc=(TE if kc == 7 else None))
                q.wait("vector", TE)
                q.emit("vector", lambda e: e.tensor_mul(
                    out=lnt1[:, 0:512], in0=p2[:, 0:512], in1=dqg[:]), inc=VE)
                q.wait("scalar", VE)
                q.emit("scalar", lambda e: e.activation(
                    out=gtmp[:], in_=lnt1[:, 0:512], func=AF.Silu), inc=SC)
                q.wait("vector", SC)
                q.emit("vector", lambda e: e.tensor_tensor(
                    out=lnt1[:, 0:512], in0=p2[:, 512:1024], in1=dqu[:],
                    op=ALU.mult), inc=VE)
                q.emit("vector", lambda e, mc=mc: e.tensor_mul(
                    out=hact[:, mc * 512:(mc + 1) * 512], in0=gtmp[:],
                    in1=lnt1[:, 0:512].bitcast(BF16)[:, 0:512] if False else
                    lnt1[:, 0:512]), inc=VE)
            for tb in range(4):
                for hhalf in range(2):
                    q.wait("tensor", VE)
                    q.wait("tensor", SC)
                    for ic in range(16):
                        q.emit("tensor", lambda e, tb=tb, hhalf=hhalf, ic=ic: e.transpose(
                            out=p4.bitcast(BF16)[:, ic * 128:(ic + 1) * 128],
                            in_=hact[:, (hhalf * 16 + ic) * 512 + tb * 128:
                                     (hhalf * 16 + ic) * 512 + (tb + 1) * 128],
                            identity=idm_bf[:]), inc=(TE if ic == 15 else None))
                    q.wait("vector", TE)
                    q.emit("vector", lambda e, tb=tb, hhalf=hhalf: e.reduce_max(
                        out=mxc[:, hhalf:hhalf + 1],
                        in_=p4.bitcast(BF16)[:, 0:2048],
                        axis=mybir.AxisListType.X, apply_absolute_value=True),
                        inc=VE)
                q.emit("vector", lambda e, tb=tb: e.tensor_max(
                    out=mxc[:, 0:1], in0=mxc[:, 0:1], in1=mxc[:, 1:2]), inc=VE)
                q.emit("vector", lambda e: e.tensor_scalar(
                    out=mxc[:, 0:1], in0=mxc[:, 0:1], scalar1=1e-5,
                    scalar2=None, op0=ALU.max), inc=VE)
                q.emit("vector", lambda e: e.reciprocal(
                    out=mxc[:, 0:1], in_=mxc[:, 0:1]), inc=VE)
                q.emit("vector", lambda e: e.tensor_scalar_mul(
                    out=mxc[:, 0:1], in0=mxc[:, 0:1], scalar1=127.0), inc=VE)
                q.wait("scalar", VE)
                q.emit("scalar", lambda e: e.copy(out=gtmp[:, 0:1], in_=mxc[:, 0:1]),
                       inc=SC)
                q.wait("tensor", SC)
                q.emit("tensor", lambda e: e.transpose(
                    out=p1b.bitcast(BF16)[0:1, 0:128], in_=gtmp[:, 0:1],
                    identity=idm_bf[:]), inc=TE)
                q.wait("vector", TE)
                q.emit("vector", lambda e, tb=tb: e.tensor_copy(
                    out=s2row[0:1, tb * 128:(tb + 1) * 128],
                    in_=p1b.bitcast(BF16)[0:1, 0:128]), inc=VE)
            q.wait("tensor", VE)
            q.emit("tensor", lambda e: e.matmul(
                out=p1b[:, 0:512], lhsT=onesr_f[:], rhs=s2row[0:1, 0:512],
                start=True, stop=True), inc=TE)
            q.wait("scalar", TE)
            q.emit("scalar", lambda e: e.copy(out=sxrep[:], in_=p1b[:, 0:512]),
                   inc=SC)
            for mc in range(32):
                q.wait("vector", SC)
                q.emit("vector", lambda e, mc=mc: e.tensor_mul(
                    out=lnt1[:, 0:512], in0=hact[:, mc * 512:(mc + 1) * 512],
                    in1=sxrep[:]), inc=VE)
                q.emit("vector", lambda e, mc=mc: e.tensor_scalar(
                    out=hact[:, mc * 512:(mc + 1) * 512], in0=lnt1[:, 0:512],
                    scalar1=RND, scalar2=RND, op0=ALU.add, op1=ALU.subtract),
                    inc=VE)
            q.emit("vector", lambda e: e.reciprocal(
                out=s2row[0:1, 0:512], in_=s2row[0:1, 0:512]), inc=VE)
            q.emit("vector", lambda e: e.tensor_single_scalar(
                out=s2row[0:1, 0:512], in_=s2row[0:1, 0:512],
                scalar=cmean_sb[0:1, 2:3], op=ALU.mult), inc=VE)
            q.wait("tensor", VE)
            q.wait("tensor", SC)
            q.emit("tensor", lambda e: e.matmul(
                out=p1b[:, 0:512], lhsT=onesr_f[:], rhs=s2row[0:1, 0:512],
                start=True, stop=True), inc=TE)
            q.wait("scalar", TE)
            q.emit("scalar", lambda e: e.copy(out=dqd[:], in_=p1b[:, 0:512]), inc=SC)
            for dc in range(8):
                q.wait("gpsimd", TE)
                dma("gpsimd", wdr[:], wqd_scr[dc])
                q.wait("tensor", GP)
                q.wait("tensor", SC)
                q.wait("tensor", VE)
                for ic in range(32):
                    q.emit("tensor", lambda e, ic=ic: e.matmul(
                        out=p1a[:, 0:512], lhsT=wdr[:, ic * 128:(ic + 1) * 128],
                        rhs=hact[:, ic * 512:(ic + 1) * 512],
                        start=(ic == 0), stop=(ic == 31)),
                        inc=(TE if ic == 31 else None))
                q.wait("vector", TE)
                q.emit("vector", lambda e, dc=dc: e.tensor_mul(
                    out=lnt1[:, 0:512], in0=p1a[:, 0:512], in1=dqd[:]), inc=VE)
                q.emit("vector", lambda e, dc=dc, t0=t0: e.tensor_add(
                    out=x_sb[dc][:, t0:t0 + 512], in0=x_sb[dc][:, t0:t0 + 512],
                    in1=lnt1[:, 0:512]), inc=VE)
        q.barrier(ALLS)

    def finalize():
        with nc.Block() as block:
            @block.sync
            def _(e):
                for th in q.q["sync"]:
                    th(e)
            @block.vector
            def _(e):
                for th in q.q["vector"]:
                    th(e)
            @block.scalar
            def _(e):
                for th in q.q["scalar"]:
                    th(e)
            @block.tensor
            def _(e):
                for th in q.q["tensor"]:
                    th(e)
            @block.gpsimd
            def _(e):
                for th in q.q["gpsimd"]:
                    th(e)
    return nc, q, dict(out_h=out_h, x_sb=x_sb, SY=SY, finalize=finalize,
                       sems=ALLS, estack=estack)


def add_store_and_finalize(nc, q, env):
    SY = env["SY"]
    q.barrier(env["sems"], ["sync"])
    for dc in range(8):
        q.emit("sync", lambda e, dc=dc: e.dma_start(
            out=env["out_h"][dc], in_=env["x_sb"][dc][:]), inc=SY, inck=16)
    q.wait("sync", SY)
    env["finalize"]()
    env["estack"].close()
    return nc


def build_full(stage_limit=8):
    nc, q, env = build_program(stage_limit)
    return add_store_and_finalize(nc, q, env)


def kernel(**inputs):
    maps = stage_inputs(inputs)
    nc = build_full()
    res = run_bass_kernel_spmd(nc, maps, core_ids=list(range(NC)))
    return unstage_output(res.results)


# version 07
# speedup vs baseline: 1.1557x; 1.0209x over previous
import contextlib
import numpy as np
import concourse.bass as bass
import concourse.mybir as mybir
from concourse.bass_utils import run_bass_kernel_spmd

F32 = mybir.dt.float32
BF16 = mybir.dt.bfloat16
AF = mybir.ActivationFunctionType
ALU = mybir.AluOpType

NB, S, D = 4, 4096, 1024
T = 2048
NC = 8
WIN = 512
KVT = T + WIN
E, I, IP = 8, 938, 1024
DI, DS, DTR = 2048, 16, 64
INNER = 4096
CH = 512
NCH = T // CH
RND = 12582912.0



def stage_inputs(inp):
    f32 = np.float32
    x = np.asarray(inp["L0_attn_x"], f32)

    wqkv = np.asarray(inp["L0_attn_Wqkv"], f32)
    wqkv_hp = np.zeros((8, 128, 3 * 8 * 128), f32)
    for hp in range(8):
        for j in range(3):
            blk = wqkv[:, j * 1024 + hp * 128: j * 1024 + (hp + 1) * 128]
            t = blk.reshape(8, 128, 128)
            wqkv_hp[hp, :, j * 1024:(j + 1) * 1024] = (
                t.transpose(1, 0, 2).reshape(128, 1024))
    wout = np.asarray(inp["L0_attn_Wout"], f32).reshape(8, 128, 1024)
    wgate_t = np.ascontiguousarray(
        np.asarray(inp["L0_ffn_gate_weight"], f32).T.reshape(8, 128, 8)
        .transpose(1, 0, 2).reshape(128, 64))
    sh_gu = np.asarray(inp["L0_ffn_shared_gate_up"], f32)
    sh_dn = np.asarray(inp["L0_ffn_shared_down"], f32)
    ex_gu = np.asarray(inp["L0_ffn_experts_gate_up"], f32)
    ex_dn = np.asarray(inp["L0_ffn_experts_down"], f32)

    def gu_tiles(gu):
        g = np.zeros((IP, D), f32); u = np.zeros((IP, D), f32)
        g[:I] = gu[:I]; u[:I] = gu[I:]
        out = np.zeros((8, 16, 128, 128), f32)
        for kc in range(8):
            for ic in range(8):
                out[kc, ic] = g[ic * 128:(ic + 1) * 128, kc * 128:(kc + 1) * 128].T
                out[kc, 8 + ic] = u[ic * 128:(ic + 1) * 128, kc * 128:(kc + 1) * 128].T
        return out

    def dn_tiles(dn):
        dpad = np.zeros((D, IP), f32); dpad[:, :I] = dn
        out = np.zeros((8, 8, 128, 128), f32)
        for ic in range(8):
            for dc in range(8):
                out[ic, dc] = dpad[dc * 128:(dc + 1) * 128, ic * 128:(ic + 1) * 128].T
        return out

    wgu = np.stack([gu_tiles(ex_gu[e]) for e in range(E)] + [gu_tiles(sh_gu)])
    wgu = np.ascontiguousarray(wgu.transpose(0, 3, 1, 2, 4).reshape(9, 128, 8 * 16 * 128))
    wdn = np.stack([dn_tiles(ex_dn[e]) for e in range(E)] + [dn_tiles(sh_dn)])
    wdn = np.ascontiguousarray(wdn.transpose(0, 3, 1, 2, 4).reshape(9, 128, 8 * 8 * 128))

    def mamba_pack(tag):
        g = lambda n: np.asarray(inp["L1_attn_" + n + "_" + tag], f32)
        in_w = g("in_w")
        win = np.zeros((8, 128, 2048), f32)
        for grp in range(8):
            for kc in range(8):
                for m2 in range(2):
                    mc = 2 * grp + m2
                    til = in_w[mc * 128:(mc + 1) * 128, kc * 128:(kc + 1) * 128].T
                    win[grp, :, (kc * 2 + m2) * 128:(kc * 2 + m2 + 1) * 128] = til
        wz = np.zeros((16, 128, 1024), f32)
        for dc in range(16):
            for kc in range(8):
                til = in_w[2048 + dc * 128:2048 + (dc + 1) * 128,
                           kc * 128:(kc + 1) * 128].T
                wz[dc, :, kc * 128:(kc + 1) * 128] = til
        xproj = g("xproj_w")
        xpg = np.zeros((128, 16 * 96), f32)
        for kc in range(16):
            xpg[:, kc * 96:(kc + 1) * 96] = xproj[:, kc * 128:(kc + 1) * 128].T
        mout = g("mout_w")
        mg = np.zeros((8, 128, 2048), f32)
        for mc in range(8):
            for ic in range(16):
                mg[mc, :, ic * 128:(ic + 1) * 128] = (
                    mout[mc * 128:(mc + 1) * 128, ic * 128:(ic + 1) * 128].T)
        return dict(
            win=win, wz=wz, conv=g("conv_w")[:, 0, :],
            convb=np.ascontiguousarray(g("conv_b").reshape(16, 128).T),
            xproj=xpg,
            dtw=np.ascontiguousarray(g("dt_w").T),
            dtb=np.ascontiguousarray(g("dt_b").reshape(16, 128).T),
            alog=np.ascontiguousarray(g("A_log")[0:1, :]),
            dv=np.ascontiguousarray(g("D").reshape(16, 128).T),
            wmout=mg,
        )

    mbd = {"f": mamba_pack("f"), "b": mamba_pack("b")}
    outw_t = np.ascontiguousarray(np.asarray(inp["L1_attn_out_w"], f32).T)
    outb = np.ascontiguousarray(np.asarray(inp["L1_attn_out_b"], f32).reshape(8, 128).T)
    gamma = np.asarray(inp["L1_attn_gamma"], f32)
    beta = np.asarray(inp["L1_attn_beta"], f32)
    def ffn_mc(w):
        o = np.zeros((32, 128, 1024), f32)
        for mc in range(32):
            for kc in range(8):
                o[mc, :, kc * 128:(kc + 1) * 128] = (
                    w[mc * 128:(mc + 1) * 128, kc * 128:(kc + 1) * 128].T)
        return o

    def ffn_dc(w):
        o = np.zeros((8, 128, 4096), f32)
        for dc in range(8):
            for ic in range(32):
                o[dc, :, ic * 128:(ic + 1) * 128] = (
                    w[dc * 128:(dc + 1) * 128, ic * 128:(ic + 1) * 128].T)
        return o
    wg2t = ffn_mc(np.asarray(inp["L1_ffn_w_gate"], f32))
    wu2t = ffn_mc(np.asarray(inp["L1_ffn_w_up"], f32))
    wd2t = ffn_dc(np.asarray(inp["L1_ffn_w_down"], f32))

    maps = []
    for c in range(NC):
        b, h = c // 2, c % 2
        if h == 0:
            idx = np.arange(T)
            xk = np.concatenate([np.zeros((WIN, D), f32), x[b, 0:T]], 0)
        else:
            idx = np.arange(S - 1, T - 1, -1)
            halo = x[b, T - 1:T - 1 - WIN:-1]
            xk = np.concatenate([x[b, idx], halo], 0)
        r = np.arange(128)[:, None]; cc = np.arange(640)[None, :]
        am = np.zeros((16, 128, 640), f32)
        for qb in range(16):
            if h == 0:
                am[qb] = ((cc - r >= 1) & (cc - r <= WIN)
                          & (qb * 128 + cc >= WIN)).astype(f32)
            else:
                am[qb] = ((cc - r >= 0) & (cc - r <= WIN - 1)).astype(f32)
        pselbig = np.zeros((128, 8, 304), f32)
        pselbig[:, c ^ 1, :] = 1.0
        L, R = ("f", "b") if h == 0 else ("b", "f")
        mbl, mbr = mbd[L], mbd[R]
        convd = np.zeros((2, 8, 128, 1024), f32)
        for dci in range(16):
            sl = slice(dci * 128, (dci + 1) * 128)
            for j in range(4):
                o = ((dci % 2) * 4 + j) * 128
                convd[0, dci // 2, :, o:o + 128] = np.diag(mbl["conv"][sl, j])
                convd[1, dci // 2, :, o:o + 128] = np.diag(mbr["conv"][sl, 3 - j])
        owLh = outw_t[0:1024] if h == 0 else outw_t[1024:2048]
        owRh = outw_t[1024:2048] if h == 0 else outw_t[0:1024]
        owL = np.zeros((8, 128, 1024), f32)
        owR = np.zeros((8, 128, 1024), f32)
        for dci in range(8):
            for kc in range(8):
                owL[dci, :, kc * 128:(kc + 1) * 128] = (
                    owLh[kc * 128:(kc + 1) * 128, dci * 128:(dci + 1) * 128])
                owR[dci, :, kc * 128:(kc + 1) * 128] = (
                    owRh[kc * 128:(kc + 1) * 128, dci * 128:(dci + 1) * 128])
        m = {
            "xin": np.ascontiguousarray(x[b, idx].T.reshape(8, 128, T)),
            "xkv": np.ascontiguousarray(xk.T.reshape(8, 128, KVT)),
            "amask": am, "idm": np.eye(128, dtype=f32),
            "ones_row": np.ones((1, 128), f32),
            "cbias": np.tile(np.array([[1e-5, 1e-30, np.log(0.125),
                                        12582912.0, -12582912.0, 0.0]], f32),
                             (128, 1)),
            "ones_col": np.ones((128, 1), f32),
            "nsel": np.ascontiguousarray(
                np.eye(16, dtype=f32)[:, :, None].repeat(128, 2).reshape(16, 16 * 128)),
            "pselbig": pselbig.reshape(128, 8 * 304),
            "qoff": np.array([[WIN if h == 0 else 0]], np.int32).view(f32),
            "wqkv": wqkv_hp, "wout": wout, "wgatelog": wgate_t,
            "wgu": wgu, "wdn": wdn,
            "winL": mbl["win"], "winR": mbr["win"], "convd": convd,
            "convbL": mbl["convb"], "convbR": mbr["convb"],
            "xprojL": mbl["xproj"], "xprojR": mbr["xproj"],
            "dtwL": mbl["dtw"], "dtwR": mbr["dtw"],
            "dtbL": mbl["dtb"], "dtbR": mbr["dtb"],
            "alogL": mbl["alog"], "alogR": mbr["alog"],
            "dvL": mbl["dv"], "dvR": mbr["dv"],
            "wmoutL": mbl["wmout"], "wmoutR": mbr["wmout"],
            "owL": owL, "owR": owR,
            "wzL": mbl["wz"], "wzR": mbr["wz"],
            "outb": outb,
            "gammarow": gamma.reshape(1, 1024),
            "betacol": np.ascontiguousarray(beta.reshape(8, 128).T),
            "wg2t": wg2t, "wu2t": wu2t, "wd2t": wd2t,
        }
        maps.append(m)
    return maps


def unstage_output(results):
    y = np.zeros((NB, S, D), np.float32)
    for c in range(NC):
        b, h = c // 2, c % 2
        o = results[c]["out"].reshape(D, T).T
        if h == 0:
            y[b, 0:T] = o
        else:
            y[b, S - 1:T - 1:-1] = o
    return y



class Sem:
    def __init__(self, h):
        self.h = h
        self.n = 0


class Q:
    ENGS = ("sync", "vector", "scalar", "tensor", "gpsimd")

    def __init__(self, self_sems=None):
        self.q = {e: [] for e in self.ENGS}
        self.self_sems = self_sems or {}

    def emit(self, eng, fn, inc=None, inck=1):
        ss = self.self_sems.get(eng)
        if ss is not None and ss.n > 0:
            self.wait(eng, ss)
        if inc is not None:
            inc.n += inck
        def thunk(e, fn=fn, inc=inc, inck=inck):
            ins = fn(e)
            if inc is not None:
                ins.then_inc(inc.h, inck)
        self.q[eng].append(thunk)

    def wait(self, eng, sem, val=None):
        v = sem.n if val is None else val
        self.q[eng].append(lambda e, s=sem, v=v: e.wait_ge(s.h, v))

    def barrier(self, sems, engs=None):
        for e in (engs or self.ENGS):
            for s in sems:
                self.wait(e, s)


def build_program(stage_limit=99):
    nc = bass.Bass("TRN2", target_bir_lowering=False)
    din = lambda n, s: nc.dram_tensor(n, s, F32, kind="ExternalInput").ap()

    xin_h = din("xin", [8, 128, T])
    xkv_h = din("xkv", [8, 128, KVT])
    am_h = din("amask", [16, 128, 640])
    idm_h = din("idm", [128, 128])
    onesr_h = din("ones_row", [1, 128])
    onesc_h = din("ones_col", [128, 1])
    cbias_h = din("cbias", [128, 6])
    nsel_h = din("nsel", [16, 16 * 128])
    pselbig_h = din("pselbig", [128, 8 * 304])
    qoff_h = din("qoff", [1, 1])
    wqkv_h = din("wqkv", [8, 128, 3 * 8 * 128])
    wout_h = din("wout", [8, 128, 1024])
    wgl_h = din("wgatelog", [128, 64])
    wgu_h = din("wgu", [9, 128, 8 * 16 * 128])
    wdn_h = din("wdn", [9, 128, 8 * 8 * 128])
    win_h = {"L": din("winL", [8, 128, 2048]), "R": din("winR", [8, 128, 2048])}
    wz_h = {"L": din("wzL", [16, 128, 1024]), "R": din("wzR", [16, 128, 1024])}
    convd_h = din("convd", [2, 8, 128, 1024])
    convb_h = {"L": din("convbL", [128, 16]), "R": din("convbR", [128, 16])}
    xproj_h = {"L": din("xprojL", [128, 16 * 96]), "R": din("xprojR", [128, 16 * 96])}
    dtw_h = {"L": din("dtwL", [64, DI]), "R": din("dtwR", [64, DI])}
    dtb_h = {"L": din("dtbL", [128, 16]), "R": din("dtbR", [128, 16])}
    alog_h = {"L": din("alogL", [1, 16]), "R": din("alogR", [1, 16])}
    dv_h = {"L": din("dvL", [128, 16]), "R": din("dvR", [128, 16])}
    wmout_h = {"L": din("wmoutL", [8, 128, 2048]), "R": din("wmoutR", [8, 128, 2048])}
    ow_h = {"L": din("owL", [8, 128, 1024]), "R": din("owR", [8, 128, 1024])}
    outb_h = din("outb", [128, 8])
    gr_h = din("gammarow", [1, 1024])
    bc_h = din("betacol", [128, 8])
    wg2_h = din("wg2t", [32, 128, 1024])
    wu2_h = din("wu2t", [32, 128, 1024])
    wd2_h = din("wd2t", [8, 128, 4096])
    out_h = nc.dram_tensor("out", [8, 128, T], F32, kind="ExternalOutput").ap()

    bncBC = nc.dram_tensor("bncBC", [32, CH], F32, kind="Internal").ap()
    fwdscr = nc.dram_tensor("fwdscr", [NCH, 128, 8 * CH], BF16, kind="Internal").ap()
    bncCW = nc.dram_tensor("bncCW", [8, T], F32, kind="Internal").ap()
    cc_in = nc.dram_tensor("cc_in", [128, 304], F32, kind="Internal").ap()
    cc_out = nc.dram_tensor("cc_out", [8, 128, 304], F32, kind="Internal").ap()
    wqg_scr = nc.dram_tensor("wqg_scr", [32, 128, 1024], BF16, kind="Internal").ap()
    wqu_scr = nc.dram_tensor("wqu_scr", [32, 128, 1024], BF16, kind="Internal").ap()
    wqd_scr = nc.dram_tensor("wqd_scr", [8, 128, 4096], BF16, kind="Internal").ap()

    sa = nc.alloc_sbuf_tensor_at
    off = [16544]

    def alloc(name, shape, dtype, cur=None):
        c = cur if cur is not None else off
        t = sa(name, shape, dtype, offset=c[0])
        nbytes = int(np.prod(shape[1:])) * (2 if dtype == BF16 else 4)
        c[0] = (c[0] + nbytes + 31) & ~31
        assert c[0] <= 224 * 1024, f"SBUF overflow at {name}: {c[0]}"
        return t

    x_sb = [alloc(f"x{i}", [128, T], F32) for i in range(8)]
    onesr_bf = alloc("onesr_bf", [1, 128], BF16)
    onesr_f = alloc("onesr_f", [1, 128], F32)
    onesc_bf = alloc("onesc_bf", [128, 1], BF16)
    onesc_f = alloc("onesc_f", [128, 1], F32)
    idm_bf = alloc("idm_bf", [128, 128], BF16)
    idm_f = alloc("idm_f", [128, 128], F32)
    qoff_sb = alloc("qoff_sb", [1, 1], F32)
    cbias_sb = alloc("cbias_sb", [128, 6], F32)
    outb_sb = alloc("outb_sb", [128, 8], F32)
    dtb_sb = {ch: alloc(f"dtb{ch}", [128, 16], F32) for ch in "LR"}
    dv_sb = {ch: alloc(f"dv{ch}", [128, 16], F32) for ch in "LR"}
    convb_sb = {ch: alloc(f"convb{ch}", [128, 16], F32) for ch in "LR"}
    avec_sb = {ch: alloc(f"avec{ch}", [128, 16], F32) for ch in "LR"}
    grow_sb = alloc("grow_sb", [1, 1024], F32)
    bcol_sb = alloc("bcol_sb", [128, 8], F32)
    stbuf = {"L": alloc("stL", [128, 256], BF16), "R": alloc("stR", [128, 256], BF16)}
    ximg = {"L": alloc("ximgL", [128, 48], BF16), "R": alloc("ximgR", [128, 48], BF16)}
    rowA_t = alloc("rowA", [1, 2560], F32)
    rowB_t = alloc("rowB", [1, 2560], F32)
    rowA = rowA_t[0:1, :]
    rowB = rowB_t[0:1, :]
    rowbf = alloc("rowbf", [1, 2048], BF16)
    lnt1 = alloc("lnt1", [128, 512], F32)
    casA = alloc("casA", [128, 512], BF16)
    casB = alloc("casB", [128, 512], BF16)
    wmean = alloc("wmean", [128, 4], F32)
    svec = alloc("svec", [128, 4], F32)
    smean = alloc("smean", [1, 8], F32)
    cmean_sb = alloc("cmean_sb", [1, 4], F32)

    ARENA0 = off[0]

    p4 = nc.alloc_psum_tensor("p4", [128, 2048], F32)
    p2 = nc.alloc_psum_tensor("p2", [128, 1024], F32)
    p1a = nc.alloc_psum_tensor("p1a", [128, 512], F32)
    p1b = nc.alloc_psum_tensor("p1b", [128, 512], F32)

    estack = contextlib.ExitStack()
    mk = lambda n: Sem(estack.enter_context(nc.semaphore(n)))
    SY = mk("SY"); GP = mk("GP"); VE = mk("VE"); SC = mk("SC"); TE = mk("TE")
    CC = mk("CC"); GC = mk("GC")
    ALLS = [SY, GP, VE, SC, TE, CC, GC]

    q = Q(self_sems={"vector": VE, "scalar": SC, "gpsimd": GC})

    def dma(eng, out, in_, sem=None):
        s = sem if sem is not None else (GP if eng == "gpsimd" else SY)
        q.emit(eng, lambda e, o=out, i=in_: e.dma_start(out=o, in_=i), inc=s, inck=16)
        q.wait(eng, s)

    def preload(fn):
        pass

    dma("sync", qoff_sb[:], qoff_h[:])
    dma("sync", cbias_sb[:], cbias_h[:])
    nc.const_aps.register(F32, float(RND), cbias_sb[:, 3:4])
    nc.const_aps.register(F32, float(-RND), cbias_sb[:, 4:5])
    nc.const_aps.register(F32, 1e-5, cbias_sb[:, 0:1])
    nc.const_aps.register(F32, 1e-30, cbias_sb[:, 1:2])
    nc.const_aps.register(F32, float(np.log(0.125)), cbias_sb[:, 2:3])
    dma("sync", onesr_f[:], onesr_h[:])
    dma("gpsimd", onesr_bf[:], onesr_h[:])
    dma("gpsimd", onesc_bf[:], onesc_h[:])
    dma("sync", onesc_f[:], onesc_h[:])
    dma("gpsimd", idm_bf[:], idm_h[:])
    dma("sync", idm_f[:], idm_h[:])
    dma("sync", outb_sb[:], outb_h[:])
    dma("sync", grow_sb[:], gr_h[:])
    dma("sync", bcol_sb[:], bc_h[:])
    for ch in "LR":
        dma("sync", dtb_sb[ch][:], dtb_h[ch][:])
        dma("sync", dv_sb[ch][:], dv_h[ch][:])
        dma("sync", convb_sb[ch][:], convb_h[ch][:])
    for i in range(8):
        dma("sync", x_sb[i][:], xin_h[i])
    q.barrier(ALLS)
    preload(AF.Exp)
    for ch in "LR":
        q.wait("sync", SC)
        dma("sync", rowA[0:1, 0:16], alog_h[ch][:])
        q.wait("scalar", SY)
        q.wait("scalar", TE)
        q.emit("scalar", lambda e: e.activation(
            out=rowB[0:1, 0:16], in_=rowA[0:1, 0:16], func=AF.Exp), inc=SC)
        q.wait("vector", SC)
        q.emit("vector", lambda e: e.tensor_scalar_mul(
            out=rowB[0:1, 0:16], in0=rowB[0:1, 0:16], scalar1=-1.0), inc=VE)
        q.wait("tensor", VE)
        q.wait("tensor", SC)
        q.emit("tensor", lambda e: e.matmul(
            out=p1a[0:128, 0:16], lhsT=onesr_f[:], rhs=rowB[0:1, 0:16],
            start=True, stop=True), inc=TE)
        q.wait("scalar", TE)
        q.emit("scalar", lambda e, ch=ch: e.copy(
            out=avec_sb[ch][:], in_=p1a[:, 0:16]), inc=SC)
    q.barrier(ALLS)

    def layer_norm(W, src_fn, outs, out32=None, affine=False, parts=3, maxiter=999, double=False):
        nw = (W + 511) // 512
        it = 0
        for w in range(nw):
            ws = w * 512
            wn = min(512, W - ws)
            q.wait("tensor", VE)
            for dc in range(8):
                it += 1
                if it > maxiter:
                    return
                sap = src_fn(dc, ws, wn)
                ca, cb = casA, casB
                q.wait("scalar", TE)
                q.emit("scalar", lambda e, sap=sap, wn=wn, ca=ca: e.copy(
                    out=ca[:, 0:wn], in_=sap), inc=SC)
                q.emit("scalar", lambda e, sap=sap, wn=wn, cb=cb: e.activation(
                    out=cb[:, 0:wn], in_=sap, func=AF.Square), inc=SC)
                q.wait("tensor", SC)
                q.emit("tensor", lambda e, dc=dc, wn=wn, ca=ca: e.matmul(
                    out=p1a[0:1, 0:wn], lhsT=onesc_bf[:], rhs=ca[:, 0:wn],
                    start=(dc == 0), stop=(dc == 7)), inc=TE)
                q.emit("tensor", lambda e, dc=dc, wn=wn, cb=cb: e.matmul(
                    out=p1b[0:1, 0:wn], lhsT=onesc_bf[:], rhs=cb[:, 0:wn],
                    start=(dc == 0), stop=(dc == 7)), inc=TE)
            q.wait("vector", TE)
            q.emit("vector", lambda e, ws=ws, wn=wn: e.tensor_scalar_mul(
                out=rowA[0:1, ws:ws + wn], in0=p1a[0:1, 0:wn], scalar1=1.0 / D),
                inc=VE)
            q.emit("vector", lambda e, ws=ws, wn=wn: e.tensor_scalar_mul(
                out=rowB[0:1, ws:ws + wn], in0=p1b[0:1, 0:wn], scalar1=1.0 / D),
                inc=VE)
        if parts < 2:
            return
        q.wait("scalar", VE)
        for w in range(nw):
            ws = w * 512
            wn = min(512, W - ws)
            q.emit("scalar", lambda e, ws=ws, wn=wn: e.activation(
                out=lnt1[0:1, 0:wn], in_=rowA[0:1, ws:ws + wn],
                func=AF.Square), inc=SC)
            q.wait("vector", SC)
            q.emit("vector", lambda e, ws=ws, wn=wn: e.tensor_sub(
                out=rowB[0:1, ws:ws + wn], in0=rowB[0:1, ws:ws + wn],
                in1=lnt1[0:1, 0:wn]), inc=VE)
            q.wait("scalar", VE)
        if not double:
            q.emit("scalar", lambda e: e.activation(
                out=rowB[0:1, 0:W], in_=rowB[0:1, 0:W], func=AF.Ln,
                bias=cbias_sb[0:1, 0:1]), inc=SC)
            q.wait("scalar", SC)
            q.emit("scalar", lambda e: e.activation(
                out=rowB[0:1, 0:W], in_=rowB[0:1, 0:W], func=AF.Exp, scale=-0.5),
                inc=SC)
        else:
            for w in range(nw):
                ws = w * 512
                wn = min(512, W - ws)
                rB = lambda: rowB[0:1, ws:ws + wn]
                q.emit("scalar", lambda e, ws=ws, wn=wn: e.activation(
                    out=lnt1[0:1, 0:wn], in_=rowB[0:1, ws:ws + wn], func=AF.Ln,
                    bias=cbias_sb[0:1, 0:1]), inc=SC)
                q.emit("scalar", lambda e, ws=ws, wn=wn: e.activation(
                    out=rowB[0:1, ws:ws + wn], in_=rowB[0:1, ws:ws + wn],
                    func=AF.Ln), inc=SC)
                q.wait("vector", SC)
                q.emit("vector", lambda e, ws=ws, wn=wn: e.tensor_sub(
                    out=rowB[0:1, ws:ws + wn], in0=rowB[0:1, ws:ws + wn],
                    in1=lnt1[0:1, 0:wn]), inc=VE)
                q.wait("scalar", VE)
                q.emit("scalar", lambda e, ws=ws, wn=wn: e.activation(
                    out=rowB[0:1, ws:ws + wn], in_=rowB[0:1, ws:ws + wn],
                    func=AF.Exp), inc=SC)
                q.emit("scalar", lambda e, ws=ws, wn=wn: e.activation(
                    out=rowB[0:1, ws:ws + wn], in_=rowB[0:1, ws:ws + wn],
                    func=AF.Ln, bias=cbias_sb[0:1, 0:1]), inc=SC)
                q.wait("vector", SC)
                q.emit("vector", lambda e, ws=ws, wn=wn: e.tensor_add(
                    out=rowB[0:1, ws:ws + wn], in0=rowB[0:1, ws:ws + wn],
                    in1=lnt1[0:1, 0:wn]), inc=VE)
                q.wait("scalar", VE)
                q.emit("scalar", lambda e, ws=ws, wn=wn: e.activation(
                    out=rowB[0:1, ws:ws + wn], in_=rowB[0:1, ws:ws + wn],
                    func=AF.Exp, scale=-0.5), inc=SC)
        q.wait("vector", SC)
        q.emit("vector", lambda e: e.scalar_tensor_tensor(
            out=rowA[0:1, 0:W], in0=rowA[0:1, 0:W], scalar=-1.0,
            in1=rowB[0:1, 0:W], op0=ALU.mult, op1=ALU.mult), inc=VE)
        if parts < 3:
            return
        for w in range(nw):
            ws = w * 512
            wn = min(512, W - ws)
            for dc in range(8):
                sap = src_fn(dc, ws, wn)
                q.wait("tensor", VE)
                lhs_r = (grow_sb[0:1, dc * 128:(dc + 1) * 128] if affine
                         else onesr_f[0:1, :])
                q.emit("tensor", lambda e, lhs_r=lhs_r, ws=ws, wn=wn: e.matmul(
                    out=p1a[:, 0:wn], lhsT=lhs_r, rhs=rowB[0:1, ws:ws + wn],
                    start=True, stop=True), inc=TE)
                q.emit("tensor", lambda e, lhs_r=lhs_r, ws=ws, wn=wn: e.matmul(
                    out=p1b[:, 0:wn], lhsT=lhs_r, rhs=rowA[0:1, ws:ws + wn],
                    start=True, stop=True), inc=TE)
                q.wait("vector", TE)
                q.emit("vector", lambda e, sap=sap, wn=wn: e.tensor_mul(
                    out=lnt1[:, 0:wn], in0=sap, in1=p1a[:, 0:wn]), inc=VE)
                if affine:
                    q.emit("vector", lambda e, dc=dc, ws=ws, wn=wn: e.scalar_tensor_tensor(
                        out=outs[dc][:, ws:ws + wn], in0=lnt1[:, 0:wn],
                        scalar=bcol_sb[:, dc:dc + 1], in1=p1b[:, 0:wn],
                        op0=ALU.add, op1=ALU.add), inc=VE)
                else:
                    q.emit("vector", lambda e, dc=dc, ws=ws, wn=wn: e.tensor_add(
                        out=outs[dc][:, ws:ws + wn], in0=lnt1[:, 0:wn],
                        in1=p1b[:, 0:wn]), inc=VE)
                if out32 is not None:
                    q.emit("vector", lambda e, dc=dc, ws=ws, wn=wn: e.tensor_add(
                        out=out32[dc][:, ws:ws + wn], in0=lnt1[:, 0:wn],
                        in1=p1b[:, 0:wn]), inc=VE)

    cur = [ARENA0]
    lnkv = alloc("lnkv", [128, 8 * KVT], BF16, cur)
    lnkv_c = [lnkv[:, i * KVT:(i + 1) * KVT] for i in range(8)]
    lnown = alloc("lnown", [128, 8 * T], BF16, cur)
    lnown_c = [lnown[:, i * T:(i + 1) * T] for i in range(8)]
    MOE_ARENA = list(cur)
    kvring = [alloc(f"kvr{i}", [128, 512], F32, cur) for i in range(4)]
    kv_cnt = [0]

    def kv_src(dc, ws, wn):
        i = kv_cnt[0] % 4
        kv_cnt[0] += 1
        q.wait("sync", SC)
        q.wait("sync", VE)
        dma("sync", kvring[i][:, 0:wn], xkv_h[dc][:, ws:ws + wn])
        q.wait("scalar", SY)
        q.wait("vector", SY)
        return kvring[i][:, 0:wn]

    if stage_limit == 18:
        q.barrier(ALLS)
    if stage_limit == 19:
        preload(AF.Exp)
    if stage_limit == 14 or 20 <= stage_limit <= 60:
        preload(AF.Exp)
        layer_norm(T, lambda dc, ws, wn: x_sb[dc][:, ws:ws + wn], lnown_c,
                   parts=1, maxiter=(999 if stage_limit == 14 else stage_limit - 20))
        q.barrier(ALLS)
    elif stage_limit >= 1:
        preload(AF.Exp)
        layer_norm(KVT, kv_src, lnkv_c,
                   parts=(1 if stage_limit == 11 else 2 if stage_limit == 12 else 3))
        q.barrier(ALLS)
        if stage_limit not in (11, 12, 13):
            layer_norm(T, lambda dc, ws, wn: x_sb[dc][:, ws:ws + wn], lnown_c)
            q.barrier(ALLS)

    if 2 <= stage_limit < 11:
        acur = list(MOE_ARENA)
        Y2 = alloc("Y2", [128, T], BF16, acur)
        nB = alloc("nB", [128, 2 * 16], F32, acur)
        expb = alloc("expb", [128, 640], BF16, acur)
        mexp = alloc("mexp", [128, 640], BF16, acur)
        attnT = alloc("attnT", [128, 640], BF16, acur)
        rsum = alloc("rsum", [128, 1], F32, acur)
        rrec = alloc("rrec", [128, 1], F32, acur)
        amsk = alloc("amsk", [128, 2 * 640], BF16, acur)
        wqp = alloc("wqp", [128, 3 * 8 * 128], BF16, acur)
        woutp = alloc("woutp", [128, 1024], BF16, acur)
        Q2 = alloc("Q2", [128, T], BF16, acur)
        K2 = alloc("K2", [128, KVT], BF16, acur)
        V2 = alloc("V2", [128, 20 * 128], BF16, acur)
        preload(AF.Exp)
        for hp in range(8):
            q.wait("gpsimd", TE)
            dma("gpsimd", wqp[:], wqkv_h[hp])
            dma("gpsimd", woutp[:], wout_h[hp])
            q.wait("tensor", GP)
            q.wait("tensor", SY)
            for tc in range(4):
                q.wait("tensor", SC)
                q.wait("tensor", VE)
                for kc in range(8):
                    q.emit("tensor", lambda e, kc=kc, tc=tc: e.matmul(
                        out=p1a[:], lhsT=wqp[:, kc * 128:(kc + 1) * 128],
                        rhs=lnown_c[kc][:, tc * 512:(tc + 1) * 512],
                        start=(kc == 0), stop=(kc == 7)),
                        inc=(TE if kc == 7 else None))
                q.wait("scalar", TE)
                q.emit("scalar", lambda e, tc=tc: e.copy(
                    out=Q2[:, tc * 512:(tc + 1) * 512], in_=p1a[:]), inc=SC)
            for tk in range(5):
                q.wait("tensor", SC)
                q.wait("tensor", VE)
                for kc in range(8):
                    q.emit("tensor", lambda e, kc=kc, tk=tk: e.matmul(
                        out=p1b[:], lhsT=wqp[:, 1024 + kc * 128:1024 + (kc + 1) * 128],
                        rhs=lnkv_c[kc][:, tk * 512:(tk + 1) * 512],
                        start=(kc == 0), stop=(kc == 7)),
                        inc=(TE if kc == 7 else None))
                q.wait("scalar", TE)
                q.emit("scalar", lambda e, tk=tk: e.copy(
                    out=K2[:, tk * 512:(tk + 1) * 512], in_=p1b[:]), inc=SC)
            for kb in range(20):
                q.wait("tensor", SC)
                q.wait("tensor", VE)
                for kc in range(8):
                    q.emit("tensor", lambda e, kb=kb, kc=kc: e.matmul(
                        out=p1a[:, 0:128],
                        lhsT=lnkv_c[kc][:, kb * 128:(kb + 1) * 128],
                        rhs=wqp[:, 2048 + kc * 128:2048 + (kc + 1) * 128],
                        start=(kc == 0), stop=(kc == 7)),
                        inc=(TE if kc == 7 else None))
                q.wait("scalar", TE)
                q.emit("scalar", lambda e, kb=kb: e.copy(
                    out=V2[:, kb * 128:(kb + 1) * 128], in_=p1a[:, 0:128]), inc=SC)
            for hd in range(2):
                dst = rowA if hd == 0 else rowB
                for tc in range(4):
                    q.wait("scalar", TE)
                    q.emit("scalar", lambda e, hd=hd, tc=tc: e.activation(
                        out=casA[hd * 64:hd * 64 + 64, :],
                        in_=Q2[hd * 64:hd * 64 + 64, tc * 512:(tc + 1) * 512],
                        func=AF.Square), inc=SC)
                    q.wait("tensor", SC)
                    q.wait("tensor", VE)
                    q.emit("tensor", lambda e, hd=hd, tc=tc: e.matmul(
                        out=p1a[0:1, 0:512],
                        lhsT=onesc_bf[hd * 64:hd * 64 + 64, :],
                        rhs=casA[hd * 64:hd * 64 + 64, :],
                        start=True, stop=True), inc=TE)
                    q.wait("vector", TE)
                    q.emit("vector", lambda e, dst=dst, tc=tc: e.tensor_copy(
                        out=dst[0:1, tc * 512:(tc + 1) * 512],
                        in_=p1a[0:1, 0:512]), inc=VE)
            for hd in range(2):
                nq = rowA if hd == 0 else rowB
                for tk in range(5):
                    q.wait("scalar", TE)
                    q.emit("scalar", lambda e, hd=hd, tk=tk: e.activation(
                        out=casB[hd * 64:hd * 64 + 64, :],
                        in_=K2[hd * 64:hd * 64 + 64, tk * 512:(tk + 1) * 512],
                        func=AF.Square), inc=SC)
                    q.wait("tensor", SC)
                    q.wait("tensor", VE)
                    q.emit("tensor", lambda e, hd=hd, tk=tk: e.matmul(
                        out=p1b[0:1, 0:512],
                        lhsT=onesc_bf[hd * 64:hd * 64 + 64, :],
                        rhs=casB[hd * 64:hd * 64 + 64, :],
                        start=True, stop=True), inc=TE)
                    q.wait("vector", TE)
                    q.emit("vector", lambda e, nq=nq, tk=tk: e.reduce_max(
                        out=nq[0:1, 2049 + tk:2050 + tk], in_=p1b[0:1, 0:512],
                        axis=mybir.AxisListType.X), inc=VE)
                q.emit("vector", lambda e, nq=nq: e.reduce_max(
                    out=nq[0:1, 2056:2057], in_=nq[0:1, 2049:2054],
                    axis=mybir.AxisListType.X), inc=VE)
                q.emit("vector", lambda e, nq=nq: e.tensor_single_scalar(
                    out=nq[0:1, 0:T], in_=nq[0:1, 0:T],
                    scalar=nq[0:1, 2056:2057], op=ALU.mult), inc=VE)
                q.wait("scalar", VE)
                q.emit("scalar", lambda e, nq=nq: e.activation(
                    out=nq[0:1, 0:T], in_=nq[0:1, 0:T], func=AF.Ln,
                    bias=cbias_sb[0:1, 1:2]), inc=SC)
                q.wait("scalar", SC)
                q.emit("scalar", lambda e, nq=nq: e.activation(
                    out=nq[0:1, 0:T], in_=nq[0:1, 0:T], func=AF.Exp,
                    scale=0.5, bias=cbias_sb[0:1, 2:3]), inc=SC)
                q.wait("vector", SC)
                q.emit("vector", lambda e, nq=nq: e.tensor_scalar_mul(
                    out=rowbf[0:1, 0:T], in0=nq[0:1, 0:T], scalar1=-1.0), inc=VE)
                q.wait("tensor", VE)
                for qb in range(16):
                    q.emit("tensor", lambda e, qb=qb: e.transpose(
                        out=p1a.bitcast(BF16)[0:128, 0:1],
                        in_=rowbf[0:1, qb * 128: qb * 128 + 128],
                        identity=idm_bf[0:1, 0:1]), inc=TE)
                    q.wait("scalar", TE)
                    q.emit("scalar", lambda e, hd=hd, qb=qb: e.copy(
                        out=nB[:, hd * 16 + qb:hd * 16 + qb + 1],
                        in_=p1a.bitcast(BF16)[0:128, 0:1]), inc=SC)
                    q.wait("tensor", SC)
            for qb in range(16):
                mtile = amsk[:, (qb % 2) * 640:(qb % 2) * 640 + 640]
                q.wait("gpsimd", VE)
                dma("gpsimd", mtile, am_h[qb])
                q.wait("vector", GP)
                for hd in range(2):
                    b0 = hd * 64
                    q.wait("tensor", SC)
                    q.emit("tensor", lambda e, b0=b0, qb=qb: e.matmul(
                        out=p2[:, 0:512],
                        lhsT=Q2[b0:b0 + 64, qb * 128:(qb + 1) * 128],
                        rhs=K2[b0:b0 + 64, qb * 128:qb * 128 + 512],
                        start=True, stop=True), inc=None)
                    q.emit("tensor", lambda e, b0=b0, qb=qb: e.matmul(
                        out=p2[:, 512:640],
                        lhsT=Q2[b0:b0 + 64, qb * 128:(qb + 1) * 128],
                        rhs=K2[b0:b0 + 64, qb * 128 + 512:qb * 128 + 640],
                        start=True, stop=True), inc=TE)
                    q.wait("scalar", TE)
                    q.emit("scalar", lambda e, hd=hd, qb=qb: e.activation(
                        out=expb[:], in_=p2[:, 0:640], func=AF.Exp, scale=0.125,
                        bias=nB[:, hd * 16 + qb:hd * 16 + qb + 1]), inc=SC)
                    q.wait("vector", SC)
                    q.emit("vector", lambda e, mtile=mtile: e.scalar_tensor_tensor(
                        out=mexp[:], in0=expb[:], scalar=1.0, in1=mtile,
                        op0=ALU.mult, op1=ALU.mult, accum_out=rsum[:]), inc=VE)
                    q.emit("vector", lambda e: e.reciprocal(
                        out=rrec[:], in_=rsum[:]), inc=VE)
                    q.emit("vector", lambda e: e.tensor_scalar_mul(
                        out=mexp[:], in0=mexp[:], scalar1=rrec[:]), inc=VE)
                    q.wait("tensor", VE)
                    for kb in range(5):
                        q.emit("tensor", lambda e, kb=kb: e.transpose(
                            out=p1a.bitcast(BF16)[:, kb * 128:(kb + 1) * 128],
                            in_=mexp[:, kb * 128:(kb + 1) * 128],
                            identity=idm_bf[:]), inc=(TE if kb == 4 else None))
                    q.wait("scalar", TE)
                    q.emit("scalar", lambda e: e.copy(
                        out=attnT[:], in_=p1a.bitcast(BF16)[:, 0:640]), inc=SC)
                    q.wait("tensor", SC)
                    for kb in range(5):
                        q.emit("tensor", lambda e, kb=kb, qb=qb, b0=b0: e.matmul(
                            out=p1b[b0:b0 + 64, 0:128],
                            lhsT=V2[:, (qb + kb) * 128 + b0:(qb + kb) * 128 + b0 + 64],
                            rhs=attnT[:, kb * 128:(kb + 1) * 128],
                            start=(kb == 0), stop=(kb == 4)),
                            inc=(TE if kb == 4 else None))
                    q.wait("scalar", TE)
                    q.emit("scalar", lambda e, qb=qb, b0=b0: e.copy(
                        out=Y2[b0:b0 + 64, qb * 128:(qb + 1) * 128],
                        in_=p1b[b0:b0 + 64, 0:128]), inc=SC)
            q.wait("tensor", SC)
            for tc in range(4):
                for dc in range(8):
                    q.wait("tensor", VE)
                    q.emit("tensor", lambda e, tc=tc, dc=dc: e.matmul(
                        out=p1a[:], lhsT=woutp[:, dc * 128:(dc + 1) * 128],
                        rhs=Y2[:, tc * 512:(tc + 1) * 512],
                        start=True, stop=True), inc=TE)
                    q.wait("vector", TE)
                    q.emit("vector", lambda e, tc=tc, dc=dc: e.tensor_add(
                        out=x_sb[dc][:, tc * 512:(tc + 1) * 512],
                        in0=x_sb[dc][:, tc * 512:(tc + 1) * 512],
                        in1=p1a[:]), inc=VE)
        q.barrier(ALLS)

    if 3 <= stage_limit < 11:
        mcur = [ARENA0]
        lnf = alloc("lnf", [128, 8 * T], BF16, mcur)
        lnf_c = [lnf[:, i * T:(i + 1) * T] for i in range(8)]
        MSNAP = list(mcur)
        lgbuf = alloc("lgbuf", [8, T], F32, mcur)
        lg_tm = alloc("lg_tm", [128, 8], F32, mcur)
        mcolA = alloc("mcolA", [128, 1], F32, mcur)
        mcolB = alloc("mcolB", [128, 1], F32, mcur)
        mcolC = alloc("mcolC", [128, 8], F32, mcur)
        cwf = alloc("cwf", [8, T], F32, mcur)
        cwb0 = alloc("cwb0", [128, 8], F32, mcur)
        ecur = list(MSNAP)
        cwrow = alloc("cwrow", [1, T], F32, ecur)
        cwb = alloc("cwb", [128, 256], BF16, ecur)
        wgl_sb = alloc("wgl_sb", [128, 64], BF16, mcur)
        wgu_sb = alloc("wgu_sb", [128, 8 * 16 * 128], BF16, ecur)
        wdn_sb = alloc("wdn_sb", [128, 8 * 8 * 128], BF16, ecur)
        gact = alloc("gact", [128, 256], BF16, ecur)
        uact = alloc("uact", [128, 256], BF16, ecur)
        act = [alloc(f"act{i}", [128, 256], BF16, ecur) for i in range(8)]

        layer_norm(T, lambda dc, ws, wn: x_sb[dc][:, ws:ws + wn], lnf_c)
        dma("gpsimd", wgl_sb[:], wgl_h[:])
        q.barrier(ALLS)
        for tc4 in range(4):
            q.wait("tensor", SC)
            for kc in range(8):
                q.emit("tensor", lambda e, tc4=tc4, kc=kc: e.matmul(
                    out=p1a[0:8, 0:512], lhsT=wgl_sb[:, kc * 8:(kc + 1) * 8],
                    rhs=lnf_c[kc][:, tc4 * 512:(tc4 + 1) * 512],
                    start=(kc == 0), stop=(kc == 7)),
                    inc=(TE if kc == 7 else None))
            q.wait("scalar", TE)
            q.emit("scalar", lambda e, tc4=tc4: e.copy(
                out=lgbuf[:, tc4 * 512:(tc4 + 1) * 512], in_=p1a[0:8, 0:512]),
                inc=SC)
        for tb in range(16):
            q.wait("tensor", SC)
            q.wait("tensor", VE)
            q.emit("tensor", lambda e, tb=tb: e.transpose(
                out=p1b[0:128, 0:8], in_=lgbuf[:, tb * 128:(tb + 1) * 128],
                identity=idm_f[0:8, 0:8]), inc=TE)
            q.wait("vector", TE)
            q.emit("vector", lambda e: e.tensor_copy(out=lg_tm[:], in_=p1b[:, 0:8]),
                   inc=VE)
            q.emit("vector", lambda e: e.reduce_max(
                out=mcolA[:], in_=lg_tm[:], axis=mybir.AxisListType.X,
                negate=True), inc=VE)
            q.wait("scalar", VE)
            q.emit("scalar", lambda e: e.activation(
                out=lg_tm[:], in_=lg_tm[:], func=AF.Exp, bias=mcolA[:],
                accum_out=mcolB[:]), inc=SC)
            q.wait("vector", SC)
            q.emit("vector", lambda e: e.reciprocal(out=mcolB[:], in_=mcolB[:]),
                   inc=VE)
            q.emit("vector", lambda e: e.tensor_scalar_mul(
                out=lg_tm[:], in0=lg_tm[:], scalar1=mcolB[:]), inc=VE)
            q.emit("vector", lambda e: e.reduce_max(
                out=mcolA[:], in_=lg_tm[:], axis=mybir.AxisListType.X), inc=VE)
            q.emit("vector", lambda e: e.tensor_scalar(
                out=mcolC[:, 0:8], in0=lg_tm[:], scalar1=mcolA[:], scalar2=None,
                op0=ALU.is_equal), inc=VE)
            q.emit("vector", lambda e: e.tensor_scalar(
                out=mcolC[:, 0:8], in0=mcolC[:, 0:8], scalar1=-1.0, scalar2=1.0,
                op0=ALU.mult, op1=ALU.add), inc=VE)
            q.emit("vector", lambda e: e.tensor_mul(
                out=mcolC[:, 0:8], in0=mcolC[:, 0:8], in1=lg_tm[:]), inc=VE)
            q.emit("vector", lambda e: e.reduce_max(
                out=mcolB[:], in_=mcolC[:, 0:8], axis=mybir.AxisListType.X),
                inc=VE)
            q.emit("vector", lambda e: e.scalar_tensor_tensor(
                out=cwb0[:, 1:2], in0=mcolA[:], scalar=1e-8, in1=mcolB[:],
                op0=ALU.add, op1=ALU.add), inc=VE)
            q.emit("vector", lambda e: e.reciprocal(
                out=cwb0[:, 1:2], in_=cwb0[:, 1:2]), inc=VE)
            q.emit("vector", lambda e: e.tensor_mul(
                out=cwb0[:, 2:3], in0=mcolA[:], in1=cwb0[:, 1:2]), inc=VE)
            q.emit("vector", lambda e: e.tensor_mul(
                out=cwb0[:, 3:4], in0=mcolB[:], in1=cwb0[:, 1:2]), inc=VE)
            q.emit("vector", lambda e: e.tensor_scalar(
                out=mcolC[:, 0:8], in0=mcolC[:, 0:8], scalar1=mcolB[:],
                scalar2=None, op0=ALU.is_equal), inc=VE)
            q.emit("vector", lambda e: e.tensor_scalar_mul(
                out=mcolC[:, 0:8], in0=mcolC[:, 0:8], scalar1=cwb0[:, 3:4]),
                inc=VE)
            q.emit("vector", lambda e: e.tensor_scalar(
                out=cwb0[:, 8:16] if False else lg_tm[:], in0=lg_tm[:],
                scalar1=mcolA[:], scalar2=None, op0=ALU.is_equal), inc=VE)
            q.emit("vector", lambda e: e.tensor_scalar_mul(
                out=lg_tm[:], in0=lg_tm[:], scalar1=cwb0[:, 2:3]), inc=VE)
            q.emit("vector", lambda e: e.tensor_add(
                out=mcolC[:, 0:8], in0=mcolC[:, 0:8], in1=lg_tm[:]), inc=VE)
            q.wait("tensor", VE)
            q.emit("tensor", lambda e, tb=tb: e.transpose(
                out=p1b[0:8, 0:128], in_=mcolC[:, 0:8],
                identity=idm_f[:]), inc=TE)
            q.wait("scalar", TE)
            q.emit("scalar", lambda e, tb=tb: e.copy(
                out=cwf[:, tb * 128:(tb + 1) * 128], in_=p1b[0:8, 0:128]), inc=SC)
        q.wait("sync", SC)
        dma("sync", bncCW[:], cwf[:])
        q.barrier(ALLS)
        for ex in range(9):
            q.wait("gpsimd", TE)
            dma("gpsimd", wgu_sb[:], wgu_h[ex])
            dma("gpsimd", wdn_sb[:], wdn_h[ex])
            if ex < 8:
                q.wait("sync", TE)
                dma("sync", cwrow[:], bncCW[ex:ex + 1, :])
            q.wait("tensor", GP)
            q.wait("tensor", SY)
            for tc in range(8):
                if ex < 8:
                    q.wait("tensor", SC)
                    q.emit("tensor", lambda e, tc=tc: e.matmul(
                        out=p1b[:, 0:256], lhsT=onesr_f[:],
                        rhs=cwrow[0:1, tc * 256:(tc + 1) * 256],
                        start=True, stop=True), inc=TE)
                    q.wait("scalar", TE)
                    q.emit("scalar", lambda e: e.copy(
                        out=cwb[:], in_=p1b[:, 0:256]), inc=SC)
                for ic in range(8):
                    q.wait("tensor", SC)
                    q.wait("tensor", VE)
                    for kc in range(8):
                        q.emit("tensor", lambda e, tc=tc, ic=ic, kc=kc: e.matmul(
                            out=p2[:, 0:256],
                            lhsT=wgu_sb[:, (kc * 16 + ic) * 128:(kc * 16 + ic + 1) * 128],
                            rhs=lnf_c[kc][:, tc * 256:(tc + 1) * 256],
                            start=(kc == 0), stop=(kc == 7)), inc=None)
                        q.emit("tensor", lambda e, tc=tc, ic=ic, kc=kc: e.matmul(
                            out=p2[:, 512:768],
                            lhsT=wgu_sb[:, (kc * 16 + 8 + ic) * 128:(kc * 16 + 9 + ic) * 128],
                            rhs=lnf_c[kc][:, tc * 256:(tc + 1) * 256],
                            start=(kc == 0), stop=(kc == 7)),
                            inc=(TE if kc == 7 else None))
                    q.wait("scalar", TE)
                    q.emit("scalar", lambda e: e.activation(
                        out=gact[:], in_=p2[:, 0:256], func=AF.Silu), inc=SC)
                    q.emit("scalar", lambda e: e.copy(
                        out=uact[:], in_=p2[:, 512:768]), inc=SC)
                    q.wait("vector", SC)
                    q.emit("vector", lambda e: e.tensor_mul(
                        out=gact[:], in0=gact[:], in1=uact[:]), inc=VE)
                    if ex < 8:
                        q.emit("vector", lambda e, ic=ic: e.tensor_mul(
                            out=act[ic][:], in0=gact[:], in1=cwb[:]), inc=VE)
                    else:
                        q.emit("vector", lambda e, ic=ic: e.tensor_copy(
                            out=act[ic][:], in_=gact[:]), inc=VE)
                for dc in range(8):
                    q.wait("tensor", VE)
                    for ic in range(8):
                        q.emit("tensor", lambda e, dc=dc, ic=ic: e.matmul(
                            out=p1a[:, 0:256],
                            lhsT=wdn_sb[:, (ic * 8 + dc) * 128:(ic * 8 + dc + 1) * 128],
                            rhs=act[ic][:], start=(ic == 0), stop=(ic == 7)),
                            inc=(TE if ic == 7 else None))
                    q.wait("vector", TE)
                    q.emit("vector", lambda e, dc=dc, tc=tc: e.tensor_add(
                        out=x_sb[dc][:, tc * 256:(tc + 1) * 256],
                        in0=x_sb[dc][:, tc * 256:(tc + 1) * 256],
                        in1=p1a[:, 0:256]), inc=VE)
        q.barrier(ALLS)


    if 4 <= stage_limit < 11:
        ccur = [ARENA0]
        hlnc = alloc("hlnc", [128, 8 * CH], BF16, ccur)
        hlnc_c = [hlnc[:, i * CH:(i + 1) * CH] for i in range(8)]
        xit = [alloc(f"xit{i}", [128, 515], BF16, ccur) for i in range(2)]
        xcb = alloc("xcb", [128, 16 * CH], BF16, ccur)
        dtrows = alloc("dtrows", [64, CH], BF16, ccur)
        Bsb = alloc("Bsb", [16, CH], BF16, ccur)
        Csb = alloc("Csb", [16, CH], BF16, ccur)
        nsel_sb = alloc("nsel_sb", [16, 8 * 128], BF16, ccur)
        Brep = alloc("Brep", [128, 16 * CH], BF16, ccur)
        Crep = alloc("Crep", [128, 16 * CH], BF16, ccur)
        dtile = alloc("dtile", [128, CH], BF16, ccur)
        dutile = alloc("dutile", [128, CH], BF16, ccur)
        atile = alloc("atile", [128, CH], BF16, ccur)
        wtile = alloc("wtile", [128, CH], BF16, ccur)
        htile = alloc("htile", [128, CH], BF16, ccur)
        hctmp = alloc("hctmp", [128, CH], BF16, ccur)
        atile2 = alloc("atile2", [128, CH], BF16, ccur)
        htile2 = alloc("htile2", [128, CH], BF16, ccur)
        hctmp2 = alloc("hctmp2", [128, CH], BF16, ccur)
        dtile2 = alloc("dtile2", [128, CH], BF16, ccur)
        dutile2 = alloc("dutile2", [128, CH], BF16, ccur)
        zst2 = alloc("zst2", [128, CH], BF16, ccur)
        yacc = alloc("yacc", [128, 8 * CH], BF16, ccur)
        zst = alloc("zst", [128, CH], BF16, ccur)
        ygated = xcb
        winr = alloc("winr", [128, 2048], BF16, ccur)
        moutr = winr
        xprojw = alloc("xprojw", [128, 16 * 96], BF16, ccur)
        dtwsb = alloc("dtwsb", [64, DI], BF16, ccur)
        convr = alloc("convr", [128, 1024], BF16, ccur)
        zwr = convr
        fwdt = alloc("fwdt", [128, CH], BF16, ccur)
        fwdin = alloc("fwdin", [128, 8 * CH], BF16, ccur)
        owr = alloc("owr", [128, 8 * 128], BF16, ccur)
        kcur = [ARENA0]
        gath = alloc("gath", [128, 8 * 304], F32, kcur)
        pselbig_sb = alloc("pselbig_sb", [128, 8 * 304], F32, kcur)
        ccbuf = alloc("ccbuf", [128, 304], F32, kcur)
        ext_sb = alloc("ext_sb", [128, 304], F32, kcur)

        def chain(ch, phase):
            chunks = range(NCH) if phase == 0 else range(NCH - 1, -1, -1)
            cdb = 0 if ch == "L" else 1
            pl = {"scanm": [VE.n, VE.n], "expm": [SC.n, SC.n],
                  "hcm": [GC.n, GC.n], "savem": [SC.n, SC.n],
                  "pam": [VE.n, VE.n], "dtm": TE.n, "expbm": SC.n,
                  "spm": SC.n, "silum": SC.n, "zmmm": TE.n,
                  "ygm": [VE.n, VE.n]}
            q.wait("gpsimd", TE)
            dma("gpsimd", xprojw[:], xproj_h[ch][:])
            dma("gpsimd", dtwsb[:], dtw_h[ch][:])
            rv = slice(None, None, -1) if phase == 1 else slice(None, None, None)
            for cc in chunks:
                t0 = cc * CH
                layer_norm(CH, lambda dc, ws, wn, t0=t0:
                           x_sb[dc][:, t0 + ws:t0 + ws + wn],
                           hlnc_c, affine=True, double=True)
                q.wait("tensor", VE)
                for dc in range(16):
                    g = dc % 2
                    if g == 0:
                        q.wait("gpsimd", TE)
                        dma("gpsimd", winr[:], win_h[ch][dc // 2])
                        dma("gpsimd", convr[:], convd_h[cdb, dc // 2])
                        q.wait("tensor", GP)
                    xi = xit[g]
                    q.wait("scalar", TE)
                    if phase == 0:
                        q.emit("scalar", lambda e, dc=dc, ch=ch, xi=xi: e.copy(
                            out=xi[:, 0:3], in_=ximg[ch][:, dc * 3:dc * 3 + 3]),
                            inc=SC)
                    else:
                        q.emit("scalar", lambda e, dc=dc, ch=ch, xi=xi: e.copy(
                            out=xi[:, 512:515], in_=ximg[ch][:, dc * 3:dc * 3 + 3]),
                            inc=SC)
                    q.wait("tensor", SC)
                    for kc in range(8):
                        q.emit("tensor", lambda e, g=g, kc=kc: e.matmul(
                            out=p2[:, 0:512],
                            lhsT=winr[:, (kc * 2 + g) * 128:(kc * 2 + g + 1) * 128],
                            rhs=hlnc_c[kc], start=(kc == 0), stop=(kc == 7)),
                            inc=(TE if kc == 7 else None))
                    q.wait("scalar", TE)
                    xoff = 3 if phase == 0 else 0
                    q.emit("scalar", lambda e, xi=xi, xoff=xoff: e.copy(
                        out=xi[:, xoff:xoff + 512], in_=p2[:, 0:512]), inc=SC)
                    moff = 512 if phase == 0 else 0
                    q.emit("scalar", lambda e, xi=xi, dc=dc, ch=ch, moff=moff: e.copy(
                        out=ximg[ch][:, dc * 3:dc * 3 + 3], in_=xi[:, moff:moff + 3]),
                        inc=SC)
                    q.wait("tensor", SC)
                    for j in range(4):
                        q.emit("tensor", lambda e, g=g, j=j, xi=xi: e.matmul(
                            out=p1a[:, 0:512],
                            lhsT=convr[:, (g * 4 + j) * 128:(g * 4 + j + 1) * 128],
                            rhs=xi[:, j:j + 512], start=(j == 0), stop=(j == 3)),
                            inc=(TE if j == 3 else None))
                    q.wait("scalar", TE)
                    q.emit("scalar", lambda e, dc=dc, ch=ch: e.activation(
                        out=xcb[:, dc * CH:(dc + 1) * CH], in_=p1a[:, 0:512],
                        func=AF.Silu, bias=convb_sb[ch][:, dc:dc + 1]), inc=SC)
                q.wait("tensor", SC)
                q.wait("tensor", VE)
                q.wait("tensor", GC)
                for kc in range(16):
                    q.emit("tensor", lambda e, kc=kc: e.matmul(
                        out=p4[0:64, 0:512], lhsT=xprojw[:, kc * 96:kc * 96 + 64],
                        rhs=xcb[:, kc * CH:(kc + 1) * CH],
                        start=(kc == 0), stop=(kc == 15)), inc=None)
                    q.emit("tensor", lambda e, kc=kc: e.matmul(
                        out=p4[0:16, 512:1024],
                        lhsT=xprojw[:, kc * 96 + 64:kc * 96 + 80],
                        rhs=xcb[:, kc * CH:(kc + 1) * CH],
                        start=(kc == 0), stop=(kc == 15)), inc=None)
                    q.emit("tensor", lambda e, kc=kc: e.matmul(
                        out=p4[0:16, 1024:1536],
                        lhsT=xprojw[:, kc * 96 + 80:kc * 96 + 96],
                        rhs=xcb[:, kc * CH:(kc + 1) * CH],
                        start=(kc == 0), stop=(kc == 15)),
                        inc=(TE if kc == 15 else None))
                q.wait("scalar", TE)
                q.emit("scalar", lambda e: e.copy(out=dtrows[:], in_=p4[0:64, 0:512]),
                       inc=SC)
                q.emit("scalar", lambda e: e.copy(out=Bsb[:], in_=p4[0:16, 512:1024]),
                       inc=SC)
                q.emit("scalar", lambda e: e.copy(out=Csb[:], in_=p4[0:16, 1024:1536]),
                       inc=SC)
                for n in range(16):
                    if n % 8 == 0:
                        q.wait("gpsimd", TE)
                        dma("gpsimd", nsel_sb[:], nsel_h[:, n * 128:(n + 8) * 128])
                        q.wait("tensor", GP)
                    q.wait("tensor", SC)
                    q.emit("tensor", lambda e, n=n: e.matmul(
                        out=p1b[:, 0:512], lhsT=nsel_sb[:, (n % 8) * 128:(n % 8 + 1) * 128],
                        rhs=Bsb[:], start=True, stop=True), inc=TE)
                    q.wait("scalar", TE)
                    q.emit("scalar", lambda e, n=n: e.copy(
                        out=Brep[:, n * CH:(n + 1) * CH], in_=p1b[:, 0:512]), inc=SC)
                    q.wait("tensor", SC)
                    q.emit("tensor", lambda e, n=n: e.matmul(
                        out=p1b[:, 0:512], lhsT=nsel_sb[:, (n % 8) * 128:(n % 8 + 1) * 128],
                        rhs=Csb[:], start=True, stop=True), inc=TE)
                    q.wait("scalar", TE)
                    q.emit("scalar", lambda e, n=n: e.copy(
                        out=Crep[:, n * CH:(n + 1) * CH], in_=p1b[:, 0:512]), inc=SC)
                at2 = (atile, atile2)
                ht2 = (htile, htile2)
                hc2 = (hctmp, hctmp2)
                dt2 = (dtile, dtile2)
                du2 = (dutile, dutile2)
                zs2 = (zst, zst2)
                sv = 511 if phase == 0 else 0
                pl["expbm"] = SC.n

                def dchead_te(dc):
                    q.wait("tensor", SC, pl["expbm"])
                    q.emit("tensor", lambda e, dc=dc: e.matmul(
                        out=p1b[:, 0:512],
                        lhsT=dtwsb[:, dc * 128:(dc + 1) * 128],
                        rhs=dtrows[:], start=True, stop=True), inc=TE)
                    pl["dtm"] = TE.n

                def dchead_sc(dc):
                    q.wait("scalar", TE, pl["dtm"])
                    q.emit("scalar", lambda e, dc=dc, ch=ch: e.activation(
                        out=lnt1[:, 0:512], in_=p1b[:, 0:512], func=AF.Exp,
                        bias=dtb_sb[ch][:, dc:dc + 1]), inc=SC)
                    pl["expbm"] = SC.n
                    q.emit("scalar", lambda e, dc=dc: e.activation(
                        out=dt2[dc % 2][:], in_=lnt1[:, 0:512], func=AF.Ln,
                        bias=1.0), inc=SC)
                    pl["spm"] = SC.n

                def dchead_ve(dc):
                    q.wait("vector", SC, pl["spm"])
                    q.emit("vector", lambda e, dc=dc: e.tensor_mul(
                        out=du2[dc % 2][:], in0=dt2[dc % 2][:],
                        in1=xcb[:, dc * CH:(dc + 1) * CH]), inc=VE)

                dchead_te(0)
                dchead_sc(0)
                dchead_ve(0)
                for dc in range(16):
                    for n in range(17):
                        if n < 16:
                            q.wait("scalar", VE, pl["scanm"][n % 2])
                            q.emit("scalar", lambda e, n=n, dc=dc, ch=ch:
                                   e.activation(
                                out=at2[n % 2][:], in_=dt2[dc % 2][:],
                                func=AF.Exp,
                                scale=avec_sb[ch][:, n:n + 1]), inc=SC)
                            pl["expm"][n % 2] = SC.n
                            q.emit("vector", lambda e, n=n, dc=dc: e.tensor_mul(
                                out=wtile[:], in0=du2[dc % 2][:],
                                in1=Brep[:, n * CH:(n + 1) * CH]), inc=VE)
                            q.wait("vector", SC, pl["expm"][n % 2])
                            q.wait("vector", SC, pl["savem"][n % 2])
                            q.emit("vector", lambda e, n=n, dc=dc, ch=ch, rv=rv:
                                   e.tensor_tensor_scan(
                                out=ht2[n % 2][:, rv], data0=at2[n % 2][:, rv],
                                data1=wtile[:, rv],
                                initial=stbuf[ch][:, dc * 16 + n:dc * 16 + n + 1],
                                op0=ALU.mult, op1=ALU.add), inc=VE)
                            pl["scanm"][n % 2] = VE.n
                        if n >= 1:
                            m = n - 1
                            q.wait("scalar", VE, pl["scanm"][m % 2])
                            q.emit("scalar", lambda e, m=m, dc=dc, ch=ch, sv=sv:
                                   e.copy(
                                out=stbuf[ch][:, dc * 16 + m:dc * 16 + m + 1],
                                in_=ht2[m % 2][:, sv:sv + 1]), inc=SC)
                            pl["savem"][m % 2] = SC.n
                            q.emit("vector", lambda e, m=m: e.tensor_mul(
                                out=hc2[m % 2][:], in0=ht2[m % 2][:],
                                in1=Crep[:, m * CH:(m + 1) * CH]), inc=VE)
                        if n >= 2 and n % 2 == 0:
                            p = (n - 2) // 2
                            q.emit("vector", lambda e, p=p: e.tensor_add(
                                out=yacc[:, p * CH:(p + 1) * CH],
                                in0=hc2[0][:], in1=hc2[1][:]), inc=VE)
                        if n == 2:
                            q.wait("gpsimd", TE)
                            dma("gpsimd", zwr[:], wz_h[ch][dc])
                        if n == 4:
                            q.wait("tensor", GP)
                            q.wait("tensor", SC, pl["silum"])
                            for kc in range(8):
                                q.emit("tensor", lambda e, kc=kc: e.matmul(
                                    out=p2[:, 512:1024],
                                    lhsT=zwr[:, kc * 128:(kc + 1) * 128],
                                    rhs=hlnc_c[kc], start=(kc == 0),
                                    stop=(kc == 7)),
                                    inc=(TE if kc == 7 else None))
                            pl["zmmm"] = TE.n
                        if n == 6 and dc < 15:
                            dchead_te(dc + 1)
                        if n == 8:
                            q.wait("scalar", TE, pl["zmmm"])
                            q.wait("scalar", VE, pl["ygm"][dc % 2])
                            q.emit("scalar", lambda e, dc=dc: e.activation(
                                out=zs2[dc % 2][:], in_=p2[:, 512:1024],
                                func=AF.Silu), inc=SC)
                            pl["silum"] = SC.n
                        if n == 10 and dc < 15:
                            dchead_sc(dc + 1)
                        if n == 12 and dc < 15:
                            dchead_ve(dc + 1)
                    q.emit("vector", lambda e: e.tensor_add(
                        out=yacc[:, 0:CH], in0=yacc[:, 0:CH], in1=yacc[:, CH:2 * CH]),
                        inc=VE)
                    q.emit("vector", lambda e: e.tensor_add(
                        out=yacc[:, 2 * CH:3 * CH], in0=yacc[:, 2 * CH:3 * CH],
                        in1=yacc[:, 3 * CH:4 * CH]), inc=VE)
                    q.emit("vector", lambda e: e.tensor_add(
                        out=yacc[:, 4 * CH:5 * CH], in0=yacc[:, 4 * CH:5 * CH],
                        in1=yacc[:, 5 * CH:6 * CH]), inc=VE)
                    q.emit("vector", lambda e: e.tensor_add(
                        out=yacc[:, 6 * CH:7 * CH], in0=yacc[:, 6 * CH:7 * CH],
                        in1=yacc[:, 7 * CH:8 * CH]), inc=VE)
                    q.emit("vector", lambda e: e.tensor_add(
                        out=yacc[:, 0:CH], in0=yacc[:, 0:CH],
                        in1=yacc[:, 2 * CH:3 * CH]), inc=VE)
                    q.emit("vector", lambda e: e.tensor_add(
                        out=yacc[:, 4 * CH:5 * CH], in0=yacc[:, 4 * CH:5 * CH],
                        in1=yacc[:, 6 * CH:7 * CH]), inc=VE)
                    q.emit("vector", lambda e: e.tensor_add(
                        out=yacc[:, 0:CH], in0=yacc[:, 0:CH],
                        in1=yacc[:, 4 * CH:5 * CH]), inc=VE)
                    q.emit("vector", lambda e, dc=dc, ch=ch: e.scalar_tensor_tensor(
                        out=wtile[:], in0=xcb[:, dc * CH:(dc + 1) * CH],
                        scalar=dv_sb[ch][:, dc:dc + 1], in1=yacc[:, 0:CH],
                        op0=ALU.mult, op1=ALU.add), inc=VE)
                    q.wait("vector", SC, pl["silum"])
                    q.emit("vector", lambda e, dc=dc: e.tensor_mul(
                        out=ygated[:, dc * CH:(dc + 1) * CH], in0=wtile[:],
                        in1=zs2[dc % 2][:]), inc=VE)
                    pl["ygm"][dc % 2] = VE.n
                for mc in range(8):
                    q.wait("gpsimd", TE)
                    q.wait("gpsimd", SY)
                    dma("gpsimd", moutr[:], wmout_h[ch][mc])
                    q.wait("tensor", GP)
                    q.wait("tensor", SC)
                    q.wait("tensor", VE)
                    for ic in range(16):
                        q.emit("tensor", lambda e, ic=ic: e.matmul(
                            out=p2[:, 0:512], lhsT=moutr[:, ic * 128:(ic + 1) * 128],
                            rhs=ygated[:, ic * CH:(ic + 1) * CH],
                            start=(ic == 0), stop=(ic == 15)),
                            inc=(TE if ic == 15 else None))
                    q.wait("scalar", TE)
                    if phase == 0:
                        q.wait("scalar", SY)
                        q.emit("scalar", lambda e, mc=mc: e.copy(
                            out=fwdt[:], in_=p2[:, 0:512]), inc=SC)
                        q.wait("sync", SC)
                        q.emit("sync", lambda e, mc=mc, cc=cc: e.dma_start(
                            out=fwdscr[cc][:, mc * CH:(mc + 1) * CH],
                            in_=fwdt[:]), inc=SY, inck=16)
                    else:
                        q.emit("scalar", lambda e, mc=mc: e.copy(
                            out=yacc[:, mc * CH:(mc + 1) * CH], in_=p2[:, 0:512]),
                            inc=SC)
                if phase == 1:
                    layer_norm(CH, lambda dc, ws, wn, t0=t0:
                               x_sb[dc][:, t0 + ws:t0 + ws + wn], hlnc_c)
                    q.wait("gpsimd", TE)
                    dma("gpsimd", fwdin[:], fwdscr[cc])
                    q.wait("tensor", GP)
                    for dc in range(8):
                        q.wait("gpsimd", TE)
                        dma("gpsimd", owr[:], ow_h["L"][dc])
                        q.wait("tensor", GP)
                        q.wait("tensor", SC)
                        q.wait("tensor", VE)
                        for kc in range(8):
                            q.emit("tensor", lambda e, kc=kc: e.matmul(
                                out=p1a[:, 0:512],
                                lhsT=owr[:, kc * 128:(kc + 1) * 128],
                                rhs=fwdin[:, kc * CH:(kc + 1) * CH],
                                start=(kc == 0), stop=False), inc=TE)
                        q.wait("gpsimd", TE)
                        dma("gpsimd", owr[:], ow_h["R"][dc])
                        q.wait("tensor", GP)
                        for kc in range(8):
                            q.emit("tensor", lambda e, kc=kc: e.matmul(
                                out=p1a[:, 0:512],
                                lhsT=owr[:, kc * 128:(kc + 1) * 128],
                                rhs=yacc[:, kc * CH:(kc + 1) * CH],
                                start=False, stop=(kc == 7)),
                                inc=(TE if kc == 7 else None))
                        q.wait("vector", TE)
                        q.emit("vector", lambda e, dc=dc: e.scalar_tensor_tensor(
                            out=lnt1[:, 0:512], in0=p1a[:, 0:512],
                            scalar=outb_sb[:, dc:dc + 1], in1=hlnc_c[dc],
                            op0=ALU.add, op1=ALU.add), inc=VE)
                        q.emit("vector", lambda e, dc=dc, t0=t0: e.tensor_add(
                            out=x_sb[dc][:, t0:t0 + CH], in0=x_sb[dc][:, t0:t0 + CH],
                            in1=lnt1[:, 0:512]), inc=VE)

        q.emit("vector", lambda e: e.memset(stbuf["L"][:], 0.0), inc=VE)
        q.emit("vector", lambda e: e.memset(stbuf["R"][:], 0.0), inc=VE)
        q.emit("vector", lambda e: e.memset(ximg["L"][:], 0.0), inc=VE)
        q.emit("vector", lambda e: e.memset(ximg["R"][:], 0.0), inc=VE)
        q.barrier([VE])
        chain("L", 0)
        q.barrier(ALLS)
        if stage_limit == 6:
            pass
        if stage_limit != 6:
            dma("sync", pselbig_sb[:], pselbig_h[:])
            q.wait("vector", SY)
            q.emit("vector", lambda e: e.tensor_copy(out=ccbuf[:, 0:256],
                                                     in_=stbuf["L"][:]), inc=VE)
            q.emit("vector", lambda e: e.tensor_copy(out=ccbuf[:, 256:304],
                                                     in_=ximg["L"][:]), inc=VE)
            q.wait("sync", VE)
            dma("sync", cc_in[:], ccbuf[:])
            q.wait("gpsimd", SY)
            q.emit("gpsimd", lambda e: e.collective_compute(
                "AllGather", ALU.bypass, replica_groups=[list(range(8))],
                ins=[cc_in[:]], outs=[cc_out[:]]), inc=CC)
            q.wait("sync", CC)
            dma("sync", gath[:].rearrange("p (a b) -> p a b", a=8),
                cc_out.transpose([1, 0, 2]))
            q.wait("vector", SY)
            q.emit("vector", lambda e: e.tensor_mul(
                out=gath[:], in0=gath[:], in1=pselbig_sb[:]), inc=VE)
            for pr in ((0, 1), (2, 3), (4, 5), (6, 7), (0, 2), (4, 6), (0, 4)):
                q.emit("vector", lambda e, pr=pr: e.tensor_add(
                    out=gath[:, pr[0] * 304:pr[0] * 304 + 304],
                    in0=gath[:, pr[0] * 304:pr[0] * 304 + 304],
                    in1=gath[:, pr[1] * 304:pr[1] * 304 + 304]), inc=VE)
            q.wait("scalar", VE)
            q.emit("scalar", lambda e: e.copy(out=stbuf["R"][:], in_=gath[:, 0:256]),
                   inc=SC)
            for dc in range(16):
                q.emit("scalar", lambda e, dc=dc: e.copy(
                    out=ximg["R"][:, dc * 3:dc * 3 + 3],
                    in_=gath[:, 256 + dc * 3:256 + dc * 3 + 3][:, ::-1]), inc=SC)
            q.barrier(ALLS)
            chain("R", 1)
            q.barrier(ALLS)


    if 8 <= stage_limit < 11:
        fcur = [ARENA0]
        wring = alloc("wring", [128, 2 * 1024], BF16, fcur)
        wfr = [alloc(f"wfr{i}", [128, 1024], F32, fcur) for i in range(2)]
        ln3b = alloc("ln3b", [128, 8 * 512], BF16, fcur)
        ln3b_c = [ln3b[:, i * 512:(i + 1) * 512] for i in range(8)]
        ln3f = alloc("ln3f", [128, 8 * 512], F32, fcur)
        ln3f_c = [ln3f[:, i * 512:(i + 1) * 512] for i in range(8)]
        xq = alloc("xq", [128, 8 * 512], BF16, fcur)
        hact = alloc("hact", [128, 32 * 512], BF16, fcur)
        mxc = alloc("mxc", [128, 8], F32, fcur)
        sxrow = alloc("sxrow", [1, 512], F32, fcur)
        s2row = alloc("s2row", [1, 512], F32, fcur)
        dqg = alloc("dqg", [128, 512], F32, fcur)
        dqu = alloc("dqu", [128, 512], F32, fcur)
        dqd = alloc("dqd", [128, 512], F32, fcur)
        sxrep = alloc("sxrep", [128, 512], F32, fcur)
        gtmp = alloc("gtmp", [128, 512], BF16, fcur)
        wdr = alloc("wdr", [128, 4096], BF16, fcur)

        q.wait("vector", VE)
        q.emit("vector", lambda e: e.memset(wmean[:], 0.0), inc=VE)
        for j, wh in enumerate((wg2_h, wu2_h)):
            for mc in range(32):
                q.wait("sync", VE)
                dma("sync", wfr[mc % 2][:], wh[mc])
                q.wait("vector", SY)
                q.emit("vector", lambda e, mc=mc: e.reduce_sum(
                    out=lnt1[:, 0:1], in_=wfr[mc % 2][:],
                    axis=mybir.AxisListType.X, apply_absolute_value=True), inc=VE)
                q.emit("vector", lambda e, j=j: e.tensor_add(
                    out=wmean[:, j:j + 1], in0=wmean[:, j:j + 1],
                    in1=lnt1[:, 0:1]), inc=VE)
        for dc in range(8):
            for half in range(2):
                q.wait("sync", VE)
                dma("sync", wdr[:].bitcast(F32)[:, 0:2048],
                    wd2_h[dc][:, half * 2048:(half + 1) * 2048])
                q.wait("vector", SY)
                q.emit("vector", lambda e: e.reduce_sum(
                    out=lnt1[:, 0:1], in_=wdr[:].bitcast(F32)[:, 0:2048],
                    axis=mybir.AxisListType.X, apply_absolute_value=True), inc=VE)
                q.emit("vector", lambda e: e.tensor_add(
                    out=wmean[:, 2:3], in0=wmean[:, 2:3], in1=lnt1[:, 0:1]), inc=VE)
        q.wait("tensor", VE)
        q.emit("tensor", lambda e: e.matmul(
            out=p1a[0:1, 0:4], lhsT=onesc_f[:], rhs=wmean[:, 0:4],
            start=True, stop=True), inc=TE)
        q.wait("vector", TE)
        q.emit("vector", lambda e: e.tensor_scalar(
            out=cmean_sb[0:1, 0:4], in0=p1a[0:1, 0:4],
            scalar1=1.0 / (INNER * D), scalar2=1e-5, op0=ALU.mult,
            op1=ALU.max), inc=VE)
        q.emit("vector", lambda e: e.reciprocal(
            out=smean[0:1, 0:4], in_=cmean_sb[0:1, 0:4]), inc=VE)
        q.wait("tensor", VE)
        q.emit("tensor", lambda e: e.matmul(
            out=p1a[:, 0:4], lhsT=onesr_f[:], rhs=smean[0:1, 0:4],
            start=True, stop=True), inc=TE)
        q.wait("scalar", TE)
        q.emit("scalar", lambda e: e.copy(out=svec[:, 0:4], in_=p1a[:, 0:4]), inc=SC)
        for j, (wh, scr, nt) in enumerate(
                ((wg2_h, wqg_scr, 32), (wu2_h, wqu_scr, 32), (wd2_h, wqd_scr, 8))):
            sj = 2 if j == 2 else j
            for ti in range(nt):
                wf = wfr[ti % 2]
                wide = 4096 if j == 2 else 1024
                if j == 2:
                    q.wait("sync", SY)
                    q.wait("sync", SC)
                    dma("sync", wfr[0][:], wh[ti][:, 0:1024])
                    dma("sync", wfr[1][:], wh[ti][:, 1024:2048])
                    for half in range(2):
                        q.wait("scalar", SY)
                        q.wait("scalar", VE)
                        q.emit("scalar", lambda e, half=half, sj=sj: e.activation(
                            out=lnt1[:, 0:512], in_=wfr[half][:, 0:512],
                            func=AF.Copy, scale=svec[:, sj:sj + 1],
                            bias=float(RND)), inc=SC)
                        q.emit("scalar", lambda e, half=half, sj=sj: e.activation(
                            out=wfr[half][:, 0:512], in_=lnt1[:, 0:512],
                            func=AF.Copy, bias=float(-RND)), inc=SC)
                        q.emit("scalar", lambda e, half=half, sj=sj: e.activation(
                            out=lnt1[:, 0:512], in_=wfr[half][:, 512:1024],
                            func=AF.Copy, scale=svec[:, sj:sj + 1],
                            bias=float(RND)), inc=SC)
                        q.emit("scalar", lambda e, half=half, sj=sj: e.activation(
                            out=wfr[half][:, 512:1024], in_=lnt1[:, 0:512],
                            func=AF.Copy, bias=float(-RND)), inc=SC)
                        q.wait("vector", SC)
                        q.emit("vector", lambda e, half=half: e.tensor_scalar(
                            out=wring[:, half * 1024:(half + 1) * 1024],
                            in0=wfr[half][:], scalar1=1.0, scalar2=-1.0,
                            op0=ALU.min, op1=ALU.max), inc=VE)
                    q.wait("sync", VE)
                    q.emit("sync", lambda e, ti=ti: e.dma_start(
                        out=scr[ti][:, 0:2048], in_=wring[:]), inc=SY, inck=16)
                    q.wait("sync", SY)
                    dma("sync", wfr[0][:], wh[ti][:, 2048:3072])
                    dma("sync", wfr[1][:], wh[ti][:, 3072:4096])
                    for half in range(2):
                        q.wait("scalar", SY)
                        q.wait("scalar", VE)
                        q.emit("scalar", lambda e, half=half, sj=sj: e.activation(
                            out=lnt1[:, 0:512], in_=wfr[half][:, 0:512],
                            func=AF.Copy, scale=svec[:, sj:sj + 1],
                            bias=float(RND)), inc=SC)
                        q.emit("scalar", lambda e, half=half, sj=sj: e.activation(
                            out=wfr[half][:, 0:512], in_=lnt1[:, 0:512],
                            func=AF.Copy, bias=float(-RND)), inc=SC)
                        q.emit("scalar", lambda e, half=half, sj=sj: e.activation(
                            out=lnt1[:, 0:512], in_=wfr[half][:, 512:1024],
                            func=AF.Copy, scale=svec[:, sj:sj + 1],
                            bias=float(RND)), inc=SC)
                        q.emit("scalar", lambda e, half=half, sj=sj: e.activation(
                            out=wfr[half][:, 512:1024], in_=lnt1[:, 0:512],
                            func=AF.Copy, bias=float(-RND)), inc=SC)
                        q.wait("vector", SC)
                        q.emit("vector", lambda e, half=half: e.tensor_scalar(
                            out=wring[:, half * 1024:(half + 1) * 1024],
                            in0=wfr[half][:], scalar1=1.0, scalar2=-1.0,
                            op0=ALU.min, op1=ALU.max), inc=VE)
                    q.wait("sync", VE)
                    q.emit("sync", lambda e, ti=ti: e.dma_start(
                        out=scr[ti][:, 2048:4096], in_=wring[:]), inc=SY, inck=16)
                    q.wait("sync", SY)
                else:
                    q.wait("sync", SC)
                    q.wait("sync", VE)
                    dma("sync", wf[:], wh[ti])
                    q.wait("scalar", SY)
                    q.wait("scalar", VE)
                    q.emit("scalar", lambda e, wf=wf, sj=sj: e.activation(
                        out=lnt1[:, 0:512], in_=wf[:, 0:512], func=AF.Copy,
                        scale=svec[:, sj:sj + 1], bias=float(RND)), inc=SC)
                    q.emit("scalar", lambda e, wf=wf: e.activation(
                        out=wf[:, 0:512], in_=lnt1[:, 0:512], func=AF.Copy,
                        bias=float(-RND)), inc=SC)
                    q.emit("scalar", lambda e, wf=wf, sj=sj: e.activation(
                        out=lnt1[:, 0:512], in_=wf[:, 512:1024], func=AF.Copy,
                        scale=svec[:, sj:sj + 1], bias=float(RND)), inc=SC)
                    q.emit("scalar", lambda e, wf=wf: e.activation(
                        out=wf[:, 512:1024], in_=lnt1[:, 0:512], func=AF.Copy,
                        bias=float(-RND)), inc=SC)
                    q.wait("vector", SC)
                    q.emit("vector", lambda e, wf=wf, ti=ti: e.tensor_scalar(
                        out=wring[:, (ti % 2) * 1024:(ti % 2 + 1) * 1024],
                        in0=wf[:], scalar1=1.0, scalar2=-1.0,
                        op0=ALU.min, op1=ALU.max), inc=VE)
                    q.wait("sync", VE)
                    q.emit("sync", lambda e, ti=ti, scr=scr: e.dma_start(
                        out=scr[ti], in_=wring[:, (ti % 2) * 1024:(ti % 2 + 1) * 1024]),
                        inc=SY, inck=16)
                    q.wait("sync", SY)
        q.barrier(ALLS)
        for tc in range(4):
            t0 = tc * 512
            layer_norm(512, lambda dc, ws, wn, t0=t0:
                       x_sb[dc][:, t0 + ws:t0 + ws + wn],
                       ln3b_c, out32=ln3f_c)
            for tb in range(4):
                q.wait("tensor", VE)
                q.wait("tensor", SC)
                for dc in range(8):
                    q.emit("tensor", lambda e, tb=tb, dc=dc: e.transpose(
                        out=p2.bitcast(BF16)[:, dc * 128:(dc + 1) * 128],
                        in_=ln3b_c[dc][:, tb * 128:(tb + 1) * 128],
                        identity=idm_bf[:]), inc=(TE if dc == 7 else None))
                q.wait("vector", TE)
                q.emit("vector", lambda e, tb=tb: e.reduce_max(
                    out=mxc[:, tb:tb + 1], in_=p2.bitcast(BF16)[:, 0:1024],
                    axis=mybir.AxisListType.X, apply_absolute_value=True), inc=VE)
                q.emit("vector", lambda e, tb=tb: e.tensor_scalar(
                    out=mxc[:, 4 + tb:5 + tb], in0=mxc[:, tb:tb + 1],
                    scalar1=1e-5, scalar2=None, op0=ALU.max), inc=VE)
                q.emit("vector", lambda e, tb=tb: e.reciprocal(
                    out=mxc[:, 4 + tb:5 + tb], in_=mxc[:, 4 + tb:5 + tb]), inc=VE)
                q.emit("vector", lambda e, tb=tb: e.tensor_scalar_mul(
                    out=mxc[:, 4 + tb:5 + tb], in0=mxc[:, 4 + tb:5 + tb],
                    scalar1=127.0), inc=VE)
                q.wait("scalar", VE)
                q.emit("scalar", lambda e, tb=tb: e.copy(
                    out=gtmp[:, 0:1], in_=mxc[:, 4 + tb:5 + tb]), inc=SC)
                q.wait("tensor", SC)
                q.emit("tensor", lambda e, tb=tb: e.transpose(
                    out=p1b.bitcast(BF16)[0:1, 0:128],
                    in_=gtmp[:, 0:1], identity=idm_bf[:]), inc=TE)
                q.wait("vector", TE)
                q.emit("vector", lambda e, tb=tb: e.tensor_copy(
                    out=sxrow[0:1, tb * 128:(tb + 1) * 128],
                    in_=p1b.bitcast(BF16)[0:1, 0:128]), inc=VE)
            q.wait("tensor", VE)
            q.emit("tensor", lambda e: e.matmul(
                out=p1b[:, 0:512], lhsT=onesr_f[:], rhs=sxrow[0:1, 0:512],
                start=True, stop=True), inc=TE)
            q.wait("scalar", TE)
            q.emit("scalar", lambda e: e.copy(out=sxrep[:], in_=p1b[:, 0:512]),
                   inc=SC)
            for dc in range(8):
                q.wait("vector", SC)
                q.emit("vector", lambda e, dc=dc: e.tensor_mul(
                    out=lnt1[:, 0:512], in0=ln3f_c[dc], in1=sxrep[:]), inc=VE)
                q.emit("vector", lambda e, dc=dc: e.tensor_scalar(
                    out=xq[:, dc * 512:(dc + 1) * 512], in0=lnt1[:, 0:512],
                    scalar1=RND, scalar2=RND, op0=ALU.add, op1=ALU.subtract),
                    inc=VE)
            q.emit("vector", lambda e: e.reciprocal(
                out=s2row[0:1, 0:512], in_=sxrow[0:1, 0:512]), inc=VE)
            for j, dq in ((0, dqg), (1, dqu)):
                q.wait("vector", TE)
                q.emit("vector", lambda e, j=j: e.tensor_single_scalar(
                    out=lnt1[0:1, 0:512], in_=s2row[0:1, 0:512],
                    scalar=cmean_sb[0:1, j:j + 1], op=ALU.mult), inc=VE)
                q.wait("tensor", VE)
                q.wait("tensor", SC)
                q.emit("tensor", lambda e: e.matmul(
                    out=p1b[:, 0:512], lhsT=onesr_f[:], rhs=lnt1[0:1, 0:512],
                    start=True, stop=True), inc=TE)
                q.wait("scalar", TE)
                q.emit("scalar", lambda e, dq=dq: e.copy(out=dq[:], in_=p1b[:, 0:512]),
                       inc=SC)
            for mc in range(32):
                q.wait("gpsimd", TE)
                dma("gpsimd", wring[:, 0:1024], wqg_scr[mc])
                dma("gpsimd", wring[:, 1024:2048], wqu_scr[mc])
                q.wait("tensor", GP)
                q.wait("tensor", SC)
                q.wait("tensor", VE)
                for kc in range(8):
                    q.emit("tensor", lambda e, kc=kc: e.matmul(
                        out=p2[:, 0:512], lhsT=wring[:, kc * 128:(kc + 1) * 128],
                        rhs=xq[:, kc * 512:(kc + 1) * 512],
                        start=(kc == 0), stop=(kc == 7)), inc=None)
                    q.emit("tensor", lambda e, kc=kc: e.matmul(
                        out=p2[:, 512:1024],
                        lhsT=wring[:, 1024 + kc * 128:1024 + (kc + 1) * 128],
                        rhs=xq[:, kc * 512:(kc + 1) * 512],
                        start=(kc == 0), stop=(kc == 7)),
                        inc=(TE if kc == 7 else None))
                q.wait("vector", TE)
                q.emit("vector", lambda e: e.tensor_mul(
                    out=lnt1[:, 0:512], in0=p2[:, 0:512], in1=dqg[:]), inc=VE)
                q.wait("scalar", VE)
                q.emit("scalar", lambda e: e.activation(
                    out=gtmp[:], in_=lnt1[:, 0:512], func=AF.Silu), inc=SC)
                q.wait("vector", SC)
                q.emit("vector", lambda e: e.tensor_tensor(
                    out=lnt1[:, 0:512], in0=p2[:, 512:1024], in1=dqu[:],
                    op=ALU.mult), inc=VE)
                q.emit("vector", lambda e, mc=mc: e.tensor_mul(
                    out=hact[:, mc * 512:(mc + 1) * 512], in0=gtmp[:],
                    in1=lnt1[:, 0:512].bitcast(BF16)[:, 0:512] if False else
                    lnt1[:, 0:512]), inc=VE)
            for tb in range(4):
                for hhalf in range(2):
                    q.wait("tensor", VE)
                    q.wait("tensor", SC)
                    for ic in range(16):
                        q.emit("tensor", lambda e, tb=tb, hhalf=hhalf, ic=ic: e.transpose(
                            out=p4.bitcast(BF16)[:, ic * 128:(ic + 1) * 128],
                            in_=hact[:, (hhalf * 16 + ic) * 512 + tb * 128:
                                     (hhalf * 16 + ic) * 512 + (tb + 1) * 128],
                            identity=idm_bf[:]), inc=(TE if ic == 15 else None))
                    q.wait("vector", TE)
                    q.emit("vector", lambda e, tb=tb, hhalf=hhalf: e.reduce_max(
                        out=mxc[:, hhalf:hhalf + 1],
                        in_=p4.bitcast(BF16)[:, 0:2048],
                        axis=mybir.AxisListType.X, apply_absolute_value=True),
                        inc=VE)
                q.emit("vector", lambda e, tb=tb: e.tensor_max(
                    out=mxc[:, 0:1], in0=mxc[:, 0:1], in1=mxc[:, 1:2]), inc=VE)
                q.emit("vector", lambda e: e.tensor_scalar(
                    out=mxc[:, 0:1], in0=mxc[:, 0:1], scalar1=1e-5,
                    scalar2=None, op0=ALU.max), inc=VE)
                q.emit("vector", lambda e: e.reciprocal(
                    out=mxc[:, 0:1], in_=mxc[:, 0:1]), inc=VE)
                q.emit("vector", lambda e: e.tensor_scalar_mul(
                    out=mxc[:, 0:1], in0=mxc[:, 0:1], scalar1=127.0), inc=VE)
                q.wait("scalar", VE)
                q.emit("scalar", lambda e: e.copy(out=gtmp[:, 0:1], in_=mxc[:, 0:1]),
                       inc=SC)
                q.wait("tensor", SC)
                q.emit("tensor", lambda e: e.transpose(
                    out=p1b.bitcast(BF16)[0:1, 0:128], in_=gtmp[:, 0:1],
                    identity=idm_bf[:]), inc=TE)
                q.wait("vector", TE)
                q.emit("vector", lambda e, tb=tb: e.tensor_copy(
                    out=s2row[0:1, tb * 128:(tb + 1) * 128],
                    in_=p1b.bitcast(BF16)[0:1, 0:128]), inc=VE)
            q.wait("tensor", VE)
            q.emit("tensor", lambda e: e.matmul(
                out=p1b[:, 0:512], lhsT=onesr_f[:], rhs=s2row[0:1, 0:512],
                start=True, stop=True), inc=TE)
            q.wait("scalar", TE)
            q.emit("scalar", lambda e: e.copy(out=sxrep[:], in_=p1b[:, 0:512]),
                   inc=SC)
            for mc in range(32):
                q.wait("vector", SC)
                q.emit("vector", lambda e, mc=mc: e.tensor_mul(
                    out=lnt1[:, 0:512], in0=hact[:, mc * 512:(mc + 1) * 512],
                    in1=sxrep[:]), inc=VE)
                q.emit("vector", lambda e, mc=mc: e.tensor_scalar(
                    out=hact[:, mc * 512:(mc + 1) * 512], in0=lnt1[:, 0:512],
                    scalar1=RND, scalar2=RND, op0=ALU.add, op1=ALU.subtract),
                    inc=VE)
            q.emit("vector", lambda e: e.reciprocal(
                out=s2row[0:1, 0:512], in_=s2row[0:1, 0:512]), inc=VE)
            q.emit("vector", lambda e: e.tensor_single_scalar(
                out=s2row[0:1, 0:512], in_=s2row[0:1, 0:512],
                scalar=cmean_sb[0:1, 2:3], op=ALU.mult), inc=VE)
            q.wait("tensor", VE)
            q.wait("tensor", SC)
            q.emit("tensor", lambda e: e.matmul(
                out=p1b[:, 0:512], lhsT=onesr_f[:], rhs=s2row[0:1, 0:512],
                start=True, stop=True), inc=TE)
            q.wait("scalar", TE)
            q.emit("scalar", lambda e: e.copy(out=dqd[:], in_=p1b[:, 0:512]), inc=SC)
            for dc in range(8):
                q.wait("gpsimd", TE)
                dma("gpsimd", wdr[:], wqd_scr[dc])
                q.wait("tensor", GP)
                q.wait("tensor", SC)
                q.wait("tensor", VE)
                for ic in range(32):
                    q.emit("tensor", lambda e, ic=ic: e.matmul(
                        out=p1a[:, 0:512], lhsT=wdr[:, ic * 128:(ic + 1) * 128],
                        rhs=hact[:, ic * 512:(ic + 1) * 512],
                        start=(ic == 0), stop=(ic == 31)),
                        inc=(TE if ic == 31 else None))
                q.wait("vector", TE)
                q.emit("vector", lambda e, dc=dc: e.tensor_mul(
                    out=lnt1[:, 0:512], in0=p1a[:, 0:512], in1=dqd[:]), inc=VE)
                q.emit("vector", lambda e, dc=dc, t0=t0: e.tensor_add(
                    out=x_sb[dc][:, t0:t0 + 512], in0=x_sb[dc][:, t0:t0 + 512],
                    in1=lnt1[:, 0:512]), inc=VE)
        q.barrier(ALLS)

    def finalize():
        with nc.Block() as block:
            @block.sync
            def _(e):
                for th in q.q["sync"]:
                    th(e)
            @block.vector
            def _(e):
                for th in q.q["vector"]:
                    th(e)
            @block.scalar
            def _(e):
                for th in q.q["scalar"]:
                    th(e)
            @block.tensor
            def _(e):
                for th in q.q["tensor"]:
                    th(e)
            @block.gpsimd
            def _(e):
                for th in q.q["gpsimd"]:
                    th(e)
    return nc, q, dict(out_h=out_h, x_sb=x_sb, SY=SY, finalize=finalize,
                       sems=ALLS, estack=estack)


def add_store_and_finalize(nc, q, env):
    SY = env["SY"]
    q.barrier(env["sems"], ["sync"])
    for dc in range(8):
        q.emit("sync", lambda e, dc=dc: e.dma_start(
            out=env["out_h"][dc], in_=env["x_sb"][dc][:]), inc=SY, inck=16)
    q.wait("sync", SY)
    env["finalize"]()
    env["estack"].close()
    return nc


def build_full(stage_limit=8):
    nc, q, env = build_program(stage_limit)
    return add_store_and_finalize(nc, q, env)


def kernel(**inputs):
    maps = stage_inputs(inputs)
    nc = build_full()
    res = run_bass_kernel_spmd(nc, maps, core_ids=list(range(NC)))
    return unstage_output(res.results)


# version 08
# speedup vs baseline: 1.2839x; 1.1109x over previous
import contextlib
import numpy as np
import concourse.bass as bass
import concourse.mybir as mybir
from concourse.bass_utils import run_bass_kernel_spmd

F32 = mybir.dt.float32
BF16 = mybir.dt.bfloat16
AF = mybir.ActivationFunctionType
ALU = mybir.AluOpType

NB, S, D = 4, 4096, 1024
T = 2048
NC = 8
WIN = 512
KVT = T + WIN
E, I, IP = 8, 938, 1024
DI, DS, DTR = 2048, 16, 64
INNER = 4096
CH = 512
NCH = T // CH
RND = 12582912.0



def stage_inputs(inp):
    f32 = np.float32
    x = np.asarray(inp["L0_attn_x"], f32)

    wqkv = np.asarray(inp["L0_attn_Wqkv"], f32)
    wqkv_hp = np.zeros((8, 128, 3 * 8 * 128), f32)
    for hp in range(8):
        for j in range(3):
            blk = wqkv[:, j * 1024 + hp * 128: j * 1024 + (hp + 1) * 128]
            t = blk.reshape(8, 128, 128)
            wqkv_hp[hp, :, j * 1024:(j + 1) * 1024] = (
                t.transpose(1, 0, 2).reshape(128, 1024))
    wout = np.asarray(inp["L0_attn_Wout"], f32).reshape(8, 128, 1024)
    wgate_t = np.ascontiguousarray(
        np.asarray(inp["L0_ffn_gate_weight"], f32).T.reshape(8, 128, 8)
        .transpose(1, 0, 2).reshape(128, 64))
    sh_gu = np.asarray(inp["L0_ffn_shared_gate_up"], f32)
    sh_dn = np.asarray(inp["L0_ffn_shared_down"], f32)
    ex_gu = np.asarray(inp["L0_ffn_experts_gate_up"], f32)
    ex_dn = np.asarray(inp["L0_ffn_experts_down"], f32)

    def gu_tiles(gu):
        g = np.zeros((IP, D), f32); u = np.zeros((IP, D), f32)
        g[:I] = gu[:I]; u[:I] = gu[I:]
        out = np.zeros((8, 16, 128, 128), f32)
        for kc in range(8):
            for ic in range(8):
                out[kc, ic] = g[ic * 128:(ic + 1) * 128, kc * 128:(kc + 1) * 128].T
                out[kc, 8 + ic] = u[ic * 128:(ic + 1) * 128, kc * 128:(kc + 1) * 128].T
        return out

    def dn_tiles(dn):
        dpad = np.zeros((D, IP), f32); dpad[:, :I] = dn
        out = np.zeros((8, 8, 128, 128), f32)
        for ic in range(8):
            for dc in range(8):
                out[ic, dc] = dpad[dc * 128:(dc + 1) * 128, ic * 128:(ic + 1) * 128].T
        return out

    wgu = np.stack([gu_tiles(ex_gu[e]) for e in range(E)] + [gu_tiles(sh_gu)])
    wgu = np.ascontiguousarray(wgu.transpose(0, 3, 1, 2, 4).reshape(9, 128, 8 * 16 * 128))
    wdn = np.stack([dn_tiles(ex_dn[e]) for e in range(E)] + [dn_tiles(sh_dn)])
    wdn = np.ascontiguousarray(wdn.transpose(0, 3, 1, 2, 4).reshape(9, 128, 8 * 8 * 128))

    def mamba_pack(tag):
        g = lambda n: np.asarray(inp["L1_attn_" + n + "_" + tag], f32)
        in_w = g("in_w")
        win = np.zeros((8, 128, 2048), f32)
        for grp in range(8):
            for kc in range(8):
                for m2 in range(2):
                    mc = 2 * grp + m2
                    til = in_w[mc * 128:(mc + 1) * 128, kc * 128:(kc + 1) * 128].T
                    win[grp, :, (kc * 2 + m2) * 128:(kc * 2 + m2 + 1) * 128] = til
        wz = np.zeros((16, 128, 1024), f32)
        for dc in range(16):
            for kc in range(8):
                til = in_w[2048 + dc * 128:2048 + (dc + 1) * 128,
                           kc * 128:(kc + 1) * 128].T
                wz[dc, :, kc * 128:(kc + 1) * 128] = til
        xproj = g("xproj_w")
        xpg = np.zeros((128, 16 * 96), f32)
        for kc in range(16):
            xpg[:, kc * 96:(kc + 1) * 96] = xproj[:, kc * 128:(kc + 1) * 128].T
        mout = g("mout_w")
        mg = np.zeros((8, 128, 2048), f32)
        for mc in range(8):
            for ic in range(16):
                mg[mc, :, ic * 128:(ic + 1) * 128] = (
                    mout[mc * 128:(mc + 1) * 128, ic * 128:(ic + 1) * 128].T)
        return dict(
            win=win, wz=wz, conv=g("conv_w")[:, 0, :],
            convb=np.ascontiguousarray(g("conv_b").reshape(16, 128).T),
            xproj=xpg,
            dtw=np.ascontiguousarray(g("dt_w").T),
            dtb=np.ascontiguousarray(g("dt_b").reshape(16, 128).T),
            alog=np.ascontiguousarray(g("A_log")[0:1, :]),
            dv=np.ascontiguousarray(g("D").reshape(16, 128).T),
            wmout=mg,
        )

    mbd = {"f": mamba_pack("f"), "b": mamba_pack("b")}
    outw_t = np.ascontiguousarray(np.asarray(inp["L1_attn_out_w"], f32).T)
    outb = np.ascontiguousarray(np.asarray(inp["L1_attn_out_b"], f32).reshape(8, 128).T)
    gamma = np.asarray(inp["L1_attn_gamma"], f32)
    beta = np.asarray(inp["L1_attn_beta"], f32)
    def ffn_mc(w):
        o = np.zeros((32, 128, 1024), f32)
        for mc in range(32):
            for kc in range(8):
                o[mc, :, kc * 128:(kc + 1) * 128] = (
                    w[mc * 128:(mc + 1) * 128, kc * 128:(kc + 1) * 128].T)
        return o

    def ffn_dc(w):
        o = np.zeros((8, 128, 4096), f32)
        for dc in range(8):
            for ic in range(32):
                o[dc, :, ic * 128:(ic + 1) * 128] = (
                    w[dc * 128:(dc + 1) * 128, ic * 128:(ic + 1) * 128].T)
        return o
    wg2t = ffn_mc(np.asarray(inp["L1_ffn_w_gate"], f32))
    wu2t = ffn_mc(np.asarray(inp["L1_ffn_w_up"], f32))
    wd2t = ffn_dc(np.asarray(inp["L1_ffn_w_down"], f32))

    maps = []
    for c in range(NC):
        b, h = c // 2, c % 2
        if h == 0:
            idx = np.arange(T)
            xk = np.concatenate([np.zeros((WIN, D), f32), x[b, 0:T]], 0)
        else:
            idx = np.arange(S - 1, T - 1, -1)
            halo = x[b, T - 1:T - 1 - WIN:-1]
            xk = np.concatenate([x[b, idx], halo], 0)
        r = np.arange(128)[:, None]; cc = np.arange(640)[None, :]
        am = np.zeros((16, 128, 640), f32)
        for qb in range(16):
            if h == 0:
                am[qb] = ((cc - r >= 1) & (cc - r <= WIN)
                          & (qb * 128 + cc >= WIN)).astype(f32)
            else:
                am[qb] = ((cc - r >= 0) & (cc - r <= WIN - 1)).astype(f32)
        pselbig = np.zeros((128, 8, 304), f32)
        pselbig[:, c ^ 1, :] = 1.0
        L, R = ("f", "b") if h == 0 else ("b", "f")
        mbl, mbr = mbd[L], mbd[R]
        convd = np.zeros((2, 8, 128, 1024), f32)
        for dci in range(16):
            sl = slice(dci * 128, (dci + 1) * 128)
            for j in range(4):
                o = ((dci % 2) * 4 + j) * 128
                convd[0, dci // 2, :, o:o + 128] = np.diag(mbl["conv"][sl, j])
                convd[1, dci // 2, :, o:o + 128] = np.diag(mbr["conv"][sl, 3 - j])
        owLh = outw_t[0:1024] if h == 0 else outw_t[1024:2048]
        owRh = outw_t[1024:2048] if h == 0 else outw_t[0:1024]
        owL = np.zeros((8, 128, 1024), f32)
        owR = np.zeros((8, 128, 1024), f32)
        for dci in range(8):
            for kc in range(8):
                owL[dci, :, kc * 128:(kc + 1) * 128] = (
                    owLh[kc * 128:(kc + 1) * 128, dci * 128:(dci + 1) * 128])
                owR[dci, :, kc * 128:(kc + 1) * 128] = (
                    owRh[kc * 128:(kc + 1) * 128, dci * 128:(dci + 1) * 128])
        m = {
            "xin": np.ascontiguousarray(x[b, idx].T.reshape(8, 128, T)),
            "xkv": np.ascontiguousarray(xk.T.reshape(8, 128, KVT)),
            "amask": am, "idm": np.eye(128, dtype=f32),
            "ones_row": np.ones((1, 128), f32),
            "cbias": np.tile(np.array([[1e-5, 1e-30, np.log(0.125),
                                        12582912.0, -12582912.0, 0.0]], f32),
                             (128, 1)),
            "ones_col": np.ones((128, 1), f32),
            "nsel": np.ascontiguousarray(
                np.eye(16, dtype=f32)[:, :, None].repeat(128, 2).reshape(16, 16 * 128)),
            "pselbig": pselbig.reshape(128, 8 * 304),
            "qoff": np.array([[WIN if h == 0 else 0]], np.int32).view(f32),
            "wqkv": wqkv_hp, "wout": wout, "wgatelog": wgate_t,
            "wgu": wgu, "wdn": wdn,
            "winL": mbl["win"], "winR": mbr["win"], "convd": convd,
            "convbL": mbl["convb"], "convbR": mbr["convb"],
            "xprojL": mbl["xproj"], "xprojR": mbr["xproj"],
            "dtwL": mbl["dtw"], "dtwR": mbr["dtw"],
            "dtbL": mbl["dtb"], "dtbR": mbr["dtb"],
            "alogL": mbl["alog"], "alogR": mbr["alog"],
            "dvL": mbl["dv"], "dvR": mbr["dv"],
            "wmoutL": mbl["wmout"], "wmoutR": mbr["wmout"],
            "owL": owL, "owR": owR,
            "wzL": mbl["wz"], "wzR": mbr["wz"],
            "outb": outb,
            "gammarow": gamma.reshape(1, 1024),
            "betacol": np.ascontiguousarray(beta.reshape(8, 128).T),
            "wg2t": wg2t, "wu2t": wu2t, "wd2t": wd2t,
        }
        maps.append(m)
    return maps


def unstage_output(results):
    y = np.zeros((NB, S, D), np.float32)
    for c in range(NC):
        b, h = c // 2, c % 2
        o = results[c]["out"].reshape(D, T).T
        if h == 0:
            y[b, 0:T] = o
        else:
            y[b, S - 1:T - 1:-1] = o
    return y



class Sem:
    def __init__(self, h):
        self.h = h
        self.n = 0


class Q:
    ENGS = ("sync", "vector", "scalar", "tensor", "gpsimd")

    def __init__(self, self_sems=None):
        self.q = {e: [] for e in self.ENGS}
        self.self_sems = self_sems or {}

    def emit(self, eng, fn, inc=None, inck=1):
        ss = self.self_sems.get(eng)
        if ss is not None and ss.n > 0:
            self.wait(eng, ss)
        if inc is not None:
            inc.n += inck
        def thunk(e, fn=fn, inc=inc, inck=inck):
            ins = fn(e)
            if inc is not None:
                ins.then_inc(inc.h, inck)
        self.q[eng].append(thunk)

    def wait(self, eng, sem, val=None):
        v = sem.n if val is None else val
        self.q[eng].append(lambda e, s=sem, v=v: e.wait_ge(s.h, v))

    def barrier(self, sems, engs=None):
        for e in (engs or self.ENGS):
            for s in sems:
                self.wait(e, s)


def build_program(stage_limit=99):
    nc = bass.Bass("TRN2", target_bir_lowering=False)
    din = lambda n, s: nc.dram_tensor(n, s, F32, kind="ExternalInput").ap()

    xin_h = din("xin", [8, 128, T])
    xkv_h = din("xkv", [8, 128, KVT])
    am_h = din("amask", [16, 128, 640])
    idm_h = din("idm", [128, 128])
    onesr_h = din("ones_row", [1, 128])
    onesc_h = din("ones_col", [128, 1])
    cbias_h = din("cbias", [128, 6])
    nsel_h = din("nsel", [16, 16 * 128])
    pselbig_h = din("pselbig", [128, 8 * 304])
    qoff_h = din("qoff", [1, 1])
    wqkv_h = din("wqkv", [8, 128, 3 * 8 * 128])
    wout_h = din("wout", [8, 128, 1024])
    wgl_h = din("wgatelog", [128, 64])
    wgu_h = din("wgu", [9, 128, 8 * 16 * 128])
    wdn_h = din("wdn", [9, 128, 8 * 8 * 128])
    win_h = {"L": din("winL", [8, 128, 2048]), "R": din("winR", [8, 128, 2048])}
    wz_h = {"L": din("wzL", [16, 128, 1024]), "R": din("wzR", [16, 128, 1024])}
    convd_h = din("convd", [2, 8, 128, 1024])
    convb_h = {"L": din("convbL", [128, 16]), "R": din("convbR", [128, 16])}
    xproj_h = {"L": din("xprojL", [128, 16 * 96]), "R": din("xprojR", [128, 16 * 96])}
    dtw_h = {"L": din("dtwL", [64, DI]), "R": din("dtwR", [64, DI])}
    dtb_h = {"L": din("dtbL", [128, 16]), "R": din("dtbR", [128, 16])}
    alog_h = {"L": din("alogL", [1, 16]), "R": din("alogR", [1, 16])}
    dv_h = {"L": din("dvL", [128, 16]), "R": din("dvR", [128, 16])}
    wmout_h = {"L": din("wmoutL", [8, 128, 2048]), "R": din("wmoutR", [8, 128, 2048])}
    ow_h = {"L": din("owL", [8, 128, 1024]), "R": din("owR", [8, 128, 1024])}
    outb_h = din("outb", [128, 8])
    gr_h = din("gammarow", [1, 1024])
    bc_h = din("betacol", [128, 8])
    wg2_h = din("wg2t", [32, 128, 1024])
    wu2_h = din("wu2t", [32, 128, 1024])
    wd2_h = din("wd2t", [8, 128, 4096])
    out_h = nc.dram_tensor("out", [8, 128, T], F32, kind="ExternalOutput").ap()

    bncBC = nc.dram_tensor("bncBC", [32, CH], F32, kind="Internal").ap()
    fwdscr = nc.dram_tensor("fwdscr", [NCH, 128, 8 * CH], BF16, kind="Internal").ap()
    bncCW = nc.dram_tensor("bncCW", [8, T], F32, kind="Internal").ap()
    cc_in = nc.dram_tensor("cc_in", [128, 304], F32, kind="Internal").ap()
    cc_out = nc.dram_tensor("cc_out", [8, 128, 304], F32, kind="Internal").ap()
    wqg_scr = nc.dram_tensor("wqg_scr", [32, 128, 1024], BF16, kind="Internal").ap()
    wqu_scr = nc.dram_tensor("wqu_scr", [32, 128, 1024], BF16, kind="Internal").ap()
    wqd_scr = nc.dram_tensor("wqd_scr", [8, 128, 4096], BF16, kind="Internal").ap()

    sa = nc.alloc_sbuf_tensor_at
    off = [16544]

    def alloc(name, shape, dtype, cur=None):
        c = cur if cur is not None else off
        t = sa(name, shape, dtype, offset=c[0])
        nbytes = int(np.prod(shape[1:])) * (2 if dtype == BF16 else 4)
        c[0] = (c[0] + nbytes + 31) & ~31
        assert c[0] <= 224 * 1024, f"SBUF overflow at {name}: {c[0]}"
        return t

    x_sb = [alloc(f"x{i}", [128, T], F32) for i in range(8)]
    onesr_bf = alloc("onesr_bf", [1, 128], BF16)
    onesr_f = alloc("onesr_f", [1, 128], F32)
    onesc_bf = alloc("onesc_bf", [128, 1], BF16)
    onesc_f = alloc("onesc_f", [128, 1], F32)
    idm_bf = alloc("idm_bf", [128, 128], BF16)
    idm_f = alloc("idm_f", [128, 128], F32)
    qoff_sb = alloc("qoff_sb", [1, 1], F32)
    cbias_sb = alloc("cbias_sb", [128, 6], F32)
    outb_sb = alloc("outb_sb", [128, 8], F32)
    dtb_sb = {ch: alloc(f"dtb{ch}", [128, 16], F32) for ch in "LR"}
    dv_sb = {ch: alloc(f"dv{ch}", [128, 16], F32) for ch in "LR"}
    convb_sb = {ch: alloc(f"convb{ch}", [128, 16], F32) for ch in "LR"}
    avec_sb = {ch: alloc(f"avec{ch}", [128, 16], F32) for ch in "LR"}
    grow_sb = alloc("grow_sb", [1, 1024], F32)
    bcol_sb = alloc("bcol_sb", [128, 8], F32)
    stbuf = {"L": alloc("stL", [128, 256], BF16), "R": alloc("stR", [128, 256], BF16)}
    ximg = {"L": alloc("ximgL", [128, 48], BF16), "R": alloc("ximgR", [128, 48], BF16)}
    rowA_t = alloc("rowA", [1, 2560], F32)
    rowB_t = alloc("rowB", [1, 2560], F32)
    rowA = rowA_t[0:1, :]
    rowB = rowB_t[0:1, :]
    rowbf = alloc("rowbf", [1, 2048], BF16)
    lnt1 = alloc("lnt1", [128, 512], F32)
    casA = alloc("casA", [128, 512], BF16)
    casB = alloc("casB", [128, 512], BF16)
    wmean = alloc("wmean", [128, 4], F32)
    svec = alloc("svec", [128, 4], F32)
    smean = alloc("smean", [1, 8], F32)
    cmean_sb = alloc("cmean_sb", [1, 4], F32)

    ARENA0 = off[0]

    p4 = nc.alloc_psum_tensor("p4", [128, 2048], F32)
    p2 = nc.alloc_psum_tensor("p2", [128, 1024], F32)
    p1a = nc.alloc_psum_tensor("p1a", [128, 512], F32)
    p1b = nc.alloc_psum_tensor("p1b", [128, 512], F32)

    estack = contextlib.ExitStack()
    mk = lambda n: Sem(estack.enter_context(nc.semaphore(n)))
    SY = mk("SY"); GP = mk("GP"); VE = mk("VE"); SC = mk("SC"); TE = mk("TE")
    CC = mk("CC"); GC = mk("GC")
    ALLS = [SY, GP, VE, SC, TE, CC, GC]

    q = Q(self_sems={"vector": VE, "scalar": SC, "gpsimd": GC})

    def dma(eng, out, in_, sem=None):
        s = sem if sem is not None else (GP if eng == "gpsimd" else SY)
        q.emit(eng, lambda e, o=out, i=in_: e.dma_start(out=o, in_=i), inc=s, inck=16)
        q.wait(eng, s)

    def preload(fn):
        pass

    dma("sync", qoff_sb[:], qoff_h[:])
    dma("sync", cbias_sb[:], cbias_h[:])
    nc.const_aps.register(F32, float(RND), cbias_sb[:, 3:4])
    nc.const_aps.register(F32, float(-RND), cbias_sb[:, 4:5])
    nc.const_aps.register(F32, 1e-5, cbias_sb[:, 0:1])
    nc.const_aps.register(F32, 1e-30, cbias_sb[:, 1:2])
    nc.const_aps.register(F32, float(np.log(0.125)), cbias_sb[:, 2:3])
    dma("sync", onesr_f[:], onesr_h[:])
    dma("gpsimd", onesr_bf[:], onesr_h[:])
    dma("gpsimd", onesc_bf[:], onesc_h[:])
    dma("sync", onesc_f[:], onesc_h[:])
    dma("gpsimd", idm_bf[:], idm_h[:])
    dma("sync", idm_f[:], idm_h[:])
    dma("sync", outb_sb[:], outb_h[:])
    dma("sync", grow_sb[:], gr_h[:])
    dma("sync", bcol_sb[:], bc_h[:])
    for ch in "LR":
        dma("sync", dtb_sb[ch][:], dtb_h[ch][:])
        dma("sync", dv_sb[ch][:], dv_h[ch][:])
        dma("sync", convb_sb[ch][:], convb_h[ch][:])
    for i in range(8):
        dma("sync", x_sb[i][:], xin_h[i])
    q.barrier(ALLS)
    preload(AF.Exp)
    for ch in "LR":
        q.wait("sync", SC)
        dma("sync", rowA[0:1, 0:16], alog_h[ch][:])
        q.wait("scalar", SY)
        q.wait("scalar", TE)
        q.emit("scalar", lambda e: e.activation(
            out=rowB[0:1, 0:16], in_=rowA[0:1, 0:16], func=AF.Exp), inc=SC)
        q.wait("vector", SC)
        q.emit("vector", lambda e: e.tensor_scalar_mul(
            out=rowB[0:1, 0:16], in0=rowB[0:1, 0:16], scalar1=-1.0), inc=VE)
        q.wait("tensor", VE)
        q.wait("tensor", SC)
        q.emit("tensor", lambda e: e.matmul(
            out=p1a[0:128, 0:16], lhsT=onesr_f[:], rhs=rowB[0:1, 0:16],
            start=True, stop=True), inc=TE)
        q.wait("scalar", TE)
        q.emit("scalar", lambda e, ch=ch: e.copy(
            out=avec_sb[ch][:], in_=p1a[:, 0:16]), inc=SC)
    q.barrier(ALLS)

    def layer_norm(W, src_fn, outs, out32=None, affine=False, parts=3, maxiter=999, double=False):
        nw = (W + 511) // 512
        it = 0
        for w in range(nw):
            ws = w * 512
            wn = min(512, W - ws)
            q.wait("tensor", VE)
            for dc in range(8):
                it += 1
                if it > maxiter:
                    return
                sap = src_fn(dc, ws, wn)
                ca, cb = casA, casB
                q.wait("scalar", TE)
                q.emit("scalar", lambda e, sap=sap, wn=wn, ca=ca: e.copy(
                    out=ca[:, 0:wn], in_=sap), inc=SC)
                q.emit("scalar", lambda e, sap=sap, wn=wn, cb=cb: e.activation(
                    out=cb[:, 0:wn], in_=sap, func=AF.Square), inc=SC)
                q.wait("tensor", SC)
                q.emit("tensor", lambda e, dc=dc, wn=wn, ca=ca: e.matmul(
                    out=p1a[0:1, 0:wn], lhsT=onesc_bf[:], rhs=ca[:, 0:wn],
                    start=(dc == 0), stop=(dc == 7)), inc=TE)
                q.emit("tensor", lambda e, dc=dc, wn=wn, cb=cb: e.matmul(
                    out=p1b[0:1, 0:wn], lhsT=onesc_bf[:], rhs=cb[:, 0:wn],
                    start=(dc == 0), stop=(dc == 7)), inc=TE)
            q.wait("vector", TE)
            q.emit("vector", lambda e, ws=ws, wn=wn: e.tensor_scalar_mul(
                out=rowA[0:1, ws:ws + wn], in0=p1a[0:1, 0:wn], scalar1=1.0 / D),
                inc=VE)
            q.emit("vector", lambda e, ws=ws, wn=wn: e.tensor_scalar_mul(
                out=rowB[0:1, ws:ws + wn], in0=p1b[0:1, 0:wn], scalar1=1.0 / D),
                inc=VE)
        if parts < 2:
            return
        q.wait("scalar", VE)
        for w in range(nw):
            ws = w * 512
            wn = min(512, W - ws)
            q.emit("scalar", lambda e, ws=ws, wn=wn: e.activation(
                out=lnt1[0:1, 0:wn], in_=rowA[0:1, ws:ws + wn],
                func=AF.Square), inc=SC)
            q.wait("vector", SC)
            q.emit("vector", lambda e, ws=ws, wn=wn: e.tensor_sub(
                out=rowB[0:1, ws:ws + wn], in0=rowB[0:1, ws:ws + wn],
                in1=lnt1[0:1, 0:wn]), inc=VE)
            q.wait("scalar", VE)
        if not double:
            q.emit("scalar", lambda e: e.activation(
                out=rowB[0:1, 0:W], in_=rowB[0:1, 0:W], func=AF.Ln,
                bias=cbias_sb[0:1, 0:1]), inc=SC)
            q.wait("scalar", SC)
            q.emit("scalar", lambda e: e.activation(
                out=rowB[0:1, 0:W], in_=rowB[0:1, 0:W], func=AF.Exp, scale=-0.5),
                inc=SC)
        else:
            for w in range(nw):
                ws = w * 512
                wn = min(512, W - ws)
                rB = lambda: rowB[0:1, ws:ws + wn]
                q.emit("scalar", lambda e, ws=ws, wn=wn: e.activation(
                    out=lnt1[0:1, 0:wn], in_=rowB[0:1, ws:ws + wn], func=AF.Ln,
                    bias=cbias_sb[0:1, 0:1]), inc=SC)
                q.emit("scalar", lambda e, ws=ws, wn=wn: e.activation(
                    out=rowB[0:1, ws:ws + wn], in_=rowB[0:1, ws:ws + wn],
                    func=AF.Ln), inc=SC)
                q.wait("vector", SC)
                q.emit("vector", lambda e, ws=ws, wn=wn: e.tensor_sub(
                    out=rowB[0:1, ws:ws + wn], in0=rowB[0:1, ws:ws + wn],
                    in1=lnt1[0:1, 0:wn]), inc=VE)
                q.wait("scalar", VE)
                q.emit("scalar", lambda e, ws=ws, wn=wn: e.activation(
                    out=rowB[0:1, ws:ws + wn], in_=rowB[0:1, ws:ws + wn],
                    func=AF.Exp), inc=SC)
                q.emit("scalar", lambda e, ws=ws, wn=wn: e.activation(
                    out=rowB[0:1, ws:ws + wn], in_=rowB[0:1, ws:ws + wn],
                    func=AF.Ln, bias=cbias_sb[0:1, 0:1]), inc=SC)
                q.wait("vector", SC)
                q.emit("vector", lambda e, ws=ws, wn=wn: e.tensor_add(
                    out=rowB[0:1, ws:ws + wn], in0=rowB[0:1, ws:ws + wn],
                    in1=lnt1[0:1, 0:wn]), inc=VE)
                q.wait("scalar", VE)
                q.emit("scalar", lambda e, ws=ws, wn=wn: e.activation(
                    out=rowB[0:1, ws:ws + wn], in_=rowB[0:1, ws:ws + wn],
                    func=AF.Exp, scale=-0.5), inc=SC)
        q.wait("vector", SC)
        q.emit("vector", lambda e: e.scalar_tensor_tensor(
            out=rowA[0:1, 0:W], in0=rowA[0:1, 0:W], scalar=-1.0,
            in1=rowB[0:1, 0:W], op0=ALU.mult, op1=ALU.mult), inc=VE)
        if parts < 3:
            return
        for w in range(nw):
            ws = w * 512
            wn = min(512, W - ws)
            for dc in range(8):
                sap = src_fn(dc, ws, wn)
                q.wait("tensor", VE)
                lhs_r = (grow_sb[0:1, dc * 128:(dc + 1) * 128] if affine
                         else onesr_f[0:1, :])
                q.emit("tensor", lambda e, lhs_r=lhs_r, ws=ws, wn=wn: e.matmul(
                    out=p1a[:, 0:wn], lhsT=lhs_r, rhs=rowB[0:1, ws:ws + wn],
                    start=True, stop=True), inc=TE)
                q.emit("tensor", lambda e, lhs_r=lhs_r, ws=ws, wn=wn: e.matmul(
                    out=p1b[:, 0:wn], lhsT=lhs_r, rhs=rowA[0:1, ws:ws + wn],
                    start=True, stop=True), inc=TE)
                q.wait("vector", TE)
                q.emit("vector", lambda e, sap=sap, wn=wn: e.tensor_mul(
                    out=lnt1[:, 0:wn], in0=sap, in1=p1a[:, 0:wn]), inc=VE)
                if affine:
                    q.emit("vector", lambda e, dc=dc, ws=ws, wn=wn: e.scalar_tensor_tensor(
                        out=outs[dc][:, ws:ws + wn], in0=lnt1[:, 0:wn],
                        scalar=bcol_sb[:, dc:dc + 1], in1=p1b[:, 0:wn],
                        op0=ALU.add, op1=ALU.add), inc=VE)
                else:
                    q.emit("vector", lambda e, dc=dc, ws=ws, wn=wn: e.tensor_add(
                        out=outs[dc][:, ws:ws + wn], in0=lnt1[:, 0:wn],
                        in1=p1b[:, 0:wn]), inc=VE)
                if out32 is not None:
                    q.emit("vector", lambda e, dc=dc, ws=ws, wn=wn: e.tensor_add(
                        out=out32[dc][:, ws:ws + wn], in0=lnt1[:, 0:wn],
                        in1=p1b[:, 0:wn]), inc=VE)

    cur = [ARENA0]
    lnkv = alloc("lnkv", [128, 8 * KVT], BF16, cur)
    lnkv_c = [lnkv[:, i * KVT:(i + 1) * KVT] for i in range(8)]
    lnown = alloc("lnown", [128, 8 * T], BF16, cur)
    lnown_c = [lnown[:, i * T:(i + 1) * T] for i in range(8)]
    MOE_ARENA = list(cur)
    kvring = [alloc(f"kvr{i}", [128, 512], F32, cur) for i in range(4)]
    kv_cnt = [0]

    def kv_src(dc, ws, wn):
        i = kv_cnt[0] % 4
        kv_cnt[0] += 1
        q.wait("sync", SC)
        q.wait("sync", VE)
        dma("sync", kvring[i][:, 0:wn], xkv_h[dc][:, ws:ws + wn])
        q.wait("scalar", SY)
        q.wait("vector", SY)
        return kvring[i][:, 0:wn]

    if stage_limit == 18:
        q.barrier(ALLS)
    if stage_limit == 19:
        preload(AF.Exp)
    if stage_limit == 14 or 20 <= stage_limit <= 60:
        preload(AF.Exp)
        layer_norm(T, lambda dc, ws, wn: x_sb[dc][:, ws:ws + wn], lnown_c,
                   parts=1, maxiter=(999 if stage_limit == 14 else stage_limit - 20))
        q.barrier(ALLS)
    elif stage_limit >= 1:
        preload(AF.Exp)
        layer_norm(KVT, kv_src, lnkv_c,
                   parts=(1 if stage_limit == 11 else 2 if stage_limit == 12 else 3))
        q.barrier(ALLS)
        if stage_limit not in (11, 12, 13):
            layer_norm(T, lambda dc, ws, wn: x_sb[dc][:, ws:ws + wn], lnown_c)
            q.barrier(ALLS)

    if 2 <= stage_limit < 11:
        acur = list(MOE_ARENA)
        Y2 = alloc("Y2", [128, T], BF16, acur)
        nB = alloc("nB", [128, 2 * 16], F32, acur)
        expb = alloc("expb", [128, 640], BF16, acur)
        mexp = alloc("mexp", [128, 640], BF16, acur)
        attnT = alloc("attnT", [128, 640], BF16, acur)
        rsum = alloc("rsum", [128, 1], F32, acur)
        rrec = alloc("rrec", [128, 1], F32, acur)
        amsk = alloc("amsk", [128, 2 * 640], BF16, acur)
        wqp = alloc("wqp", [128, 3 * 8 * 128], BF16, acur)
        woutp = alloc("woutp", [128, 1024], BF16, acur)
        Q2 = alloc("Q2", [128, T], BF16, acur)
        K2 = alloc("K2", [128, KVT], BF16, acur)
        V2 = alloc("V2", [128, 20 * 128], BF16, acur)
        preload(AF.Exp)
        for hp in range(8):
            q.wait("gpsimd", TE)
            dma("gpsimd", wqp[:], wqkv_h[hp])
            dma("gpsimd", woutp[:], wout_h[hp])
            q.wait("tensor", GP)
            q.wait("tensor", SY)
            for tc in range(4):
                q.wait("tensor", SC)
                q.wait("tensor", VE)
                for kc in range(8):
                    q.emit("tensor", lambda e, kc=kc, tc=tc: e.matmul(
                        out=p1a[:], lhsT=wqp[:, kc * 128:(kc + 1) * 128],
                        rhs=lnown_c[kc][:, tc * 512:(tc + 1) * 512],
                        start=(kc == 0), stop=(kc == 7)),
                        inc=(TE if kc == 7 else None))
                q.wait("scalar", TE)
                q.emit("scalar", lambda e, tc=tc: e.copy(
                    out=Q2[:, tc * 512:(tc + 1) * 512], in_=p1a[:]), inc=SC)
            for tk in range(5):
                q.wait("tensor", SC)
                q.wait("tensor", VE)
                for kc in range(8):
                    q.emit("tensor", lambda e, kc=kc, tk=tk: e.matmul(
                        out=p1b[:], lhsT=wqp[:, 1024 + kc * 128:1024 + (kc + 1) * 128],
                        rhs=lnkv_c[kc][:, tk * 512:(tk + 1) * 512],
                        start=(kc == 0), stop=(kc == 7)),
                        inc=(TE if kc == 7 else None))
                q.wait("scalar", TE)
                q.emit("scalar", lambda e, tk=tk: e.copy(
                    out=K2[:, tk * 512:(tk + 1) * 512], in_=p1b[:]), inc=SC)
            for kb in range(20):
                q.wait("tensor", SC)
                q.wait("tensor", VE)
                for kc in range(8):
                    q.emit("tensor", lambda e, kb=kb, kc=kc: e.matmul(
                        out=p1a[:, 0:128],
                        lhsT=lnkv_c[kc][:, kb * 128:(kb + 1) * 128],
                        rhs=wqp[:, 2048 + kc * 128:2048 + (kc + 1) * 128],
                        start=(kc == 0), stop=(kc == 7)),
                        inc=(TE if kc == 7 else None))
                q.wait("scalar", TE)
                q.emit("scalar", lambda e, kb=kb: e.copy(
                    out=V2[:, kb * 128:(kb + 1) * 128], in_=p1a[:, 0:128]), inc=SC)
            for hd in range(2):
                dst = rowA if hd == 0 else rowB
                for tc in range(4):
                    q.wait("scalar", TE)
                    q.emit("scalar", lambda e, hd=hd, tc=tc: e.activation(
                        out=casA[hd * 64:hd * 64 + 64, :],
                        in_=Q2[hd * 64:hd * 64 + 64, tc * 512:(tc + 1) * 512],
                        func=AF.Square), inc=SC)
                    q.wait("tensor", SC)
                    q.wait("tensor", VE)
                    q.emit("tensor", lambda e, hd=hd, tc=tc: e.matmul(
                        out=p1a[0:1, 0:512],
                        lhsT=onesc_bf[hd * 64:hd * 64 + 64, :],
                        rhs=casA[hd * 64:hd * 64 + 64, :],
                        start=True, stop=True), inc=TE)
                    q.wait("vector", TE)
                    q.emit("vector", lambda e, dst=dst, tc=tc: e.tensor_copy(
                        out=dst[0:1, tc * 512:(tc + 1) * 512],
                        in_=p1a[0:1, 0:512]), inc=VE)
            for hd in range(2):
                nq = rowA if hd == 0 else rowB
                for tk in range(5):
                    q.wait("scalar", TE)
                    q.emit("scalar", lambda e, hd=hd, tk=tk: e.activation(
                        out=casB[hd * 64:hd * 64 + 64, :],
                        in_=K2[hd * 64:hd * 64 + 64, tk * 512:(tk + 1) * 512],
                        func=AF.Square), inc=SC)
                    q.wait("tensor", SC)
                    q.wait("tensor", VE)
                    q.emit("tensor", lambda e, hd=hd, tk=tk: e.matmul(
                        out=p1b[0:1, 0:512],
                        lhsT=onesc_bf[hd * 64:hd * 64 + 64, :],
                        rhs=casB[hd * 64:hd * 64 + 64, :],
                        start=True, stop=True), inc=TE)
                    q.wait("vector", TE)
                    q.emit("vector", lambda e, nq=nq, tk=tk: e.reduce_max(
                        out=nq[0:1, 2049 + tk:2050 + tk], in_=p1b[0:1, 0:512],
                        axis=mybir.AxisListType.X), inc=VE)
                q.emit("vector", lambda e, nq=nq: e.reduce_max(
                    out=nq[0:1, 2056:2057], in_=nq[0:1, 2049:2054],
                    axis=mybir.AxisListType.X), inc=VE)
                q.emit("vector", lambda e, nq=nq: e.tensor_single_scalar(
                    out=nq[0:1, 0:T], in_=nq[0:1, 0:T],
                    scalar=nq[0:1, 2056:2057], op=ALU.mult), inc=VE)
                q.wait("scalar", VE)
                q.emit("scalar", lambda e, nq=nq: e.activation(
                    out=nq[0:1, 0:T], in_=nq[0:1, 0:T], func=AF.Ln,
                    bias=cbias_sb[0:1, 1:2]), inc=SC)
                q.wait("scalar", SC)
                q.emit("scalar", lambda e, nq=nq: e.activation(
                    out=nq[0:1, 0:T], in_=nq[0:1, 0:T], func=AF.Exp,
                    scale=0.5, bias=cbias_sb[0:1, 2:3]), inc=SC)
                q.wait("vector", SC)
                q.emit("vector", lambda e, nq=nq: e.tensor_scalar_mul(
                    out=rowbf[0:1, 0:T], in0=nq[0:1, 0:T], scalar1=-1.0), inc=VE)
                q.wait("tensor", VE)
                for qb in range(16):
                    q.emit("tensor", lambda e, qb=qb: e.transpose(
                        out=p1a.bitcast(BF16)[0:128, 0:1],
                        in_=rowbf[0:1, qb * 128: qb * 128 + 128],
                        identity=idm_bf[0:1, 0:1]), inc=TE)
                    q.wait("scalar", TE)
                    q.emit("scalar", lambda e, hd=hd, qb=qb: e.copy(
                        out=nB[:, hd * 16 + qb:hd * 16 + qb + 1],
                        in_=p1a.bitcast(BF16)[0:128, 0:1]), inc=SC)
                    q.wait("tensor", SC)
            for qb in range(16):
                mtile = amsk[:, (qb % 2) * 640:(qb % 2) * 640 + 640]
                q.wait("gpsimd", VE)
                dma("gpsimd", mtile, am_h[qb])
                q.wait("vector", GP)
                for hd in range(2):
                    b0 = hd * 64
                    q.wait("tensor", SC)
                    q.emit("tensor", lambda e, b0=b0, qb=qb: e.matmul(
                        out=p2[:, 0:512],
                        lhsT=Q2[b0:b0 + 64, qb * 128:(qb + 1) * 128],
                        rhs=K2[b0:b0 + 64, qb * 128:qb * 128 + 512],
                        start=True, stop=True), inc=None)
                    q.emit("tensor", lambda e, b0=b0, qb=qb: e.matmul(
                        out=p2[:, 512:640],
                        lhsT=Q2[b0:b0 + 64, qb * 128:(qb + 1) * 128],
                        rhs=K2[b0:b0 + 64, qb * 128 + 512:qb * 128 + 640],
                        start=True, stop=True), inc=TE)
                    q.wait("scalar", TE)
                    q.emit("scalar", lambda e, hd=hd, qb=qb: e.activation(
                        out=expb[:], in_=p2[:, 0:640], func=AF.Exp, scale=0.125,
                        bias=nB[:, hd * 16 + qb:hd * 16 + qb + 1]), inc=SC)
                    q.wait("vector", SC)
                    q.emit("vector", lambda e, mtile=mtile: e.scalar_tensor_tensor(
                        out=mexp[:], in0=expb[:], scalar=1.0, in1=mtile,
                        op0=ALU.mult, op1=ALU.mult, accum_out=rsum[:]), inc=VE)
                    q.emit("vector", lambda e: e.reciprocal(
                        out=rrec[:], in_=rsum[:]), inc=VE)
                    q.emit("vector", lambda e: e.tensor_scalar_mul(
                        out=mexp[:], in0=mexp[:], scalar1=rrec[:]), inc=VE)
                    q.wait("tensor", VE)
                    for kb in range(5):
                        q.emit("tensor", lambda e, kb=kb: e.transpose(
                            out=p1a.bitcast(BF16)[:, kb * 128:(kb + 1) * 128],
                            in_=mexp[:, kb * 128:(kb + 1) * 128],
                            identity=idm_bf[:]), inc=(TE if kb == 4 else None))
                    q.wait("scalar", TE)
                    q.emit("scalar", lambda e: e.copy(
                        out=attnT[:], in_=p1a.bitcast(BF16)[:, 0:640]), inc=SC)
                    q.wait("tensor", SC)
                    for kb in range(5):
                        q.emit("tensor", lambda e, kb=kb, qb=qb, b0=b0: e.matmul(
                            out=p1b[b0:b0 + 64, 0:128],
                            lhsT=V2[:, (qb + kb) * 128 + b0:(qb + kb) * 128 + b0 + 64],
                            rhs=attnT[:, kb * 128:(kb + 1) * 128],
                            start=(kb == 0), stop=(kb == 4)),
                            inc=(TE if kb == 4 else None))
                    q.wait("scalar", TE)
                    q.emit("scalar", lambda e, qb=qb, b0=b0: e.copy(
                        out=Y2[b0:b0 + 64, qb * 128:(qb + 1) * 128],
                        in_=p1b[b0:b0 + 64, 0:128]), inc=SC)
            q.wait("tensor", SC)
            for tc in range(4):
                for dc in range(8):
                    q.wait("tensor", VE)
                    q.emit("tensor", lambda e, tc=tc, dc=dc: e.matmul(
                        out=p1a[:], lhsT=woutp[:, dc * 128:(dc + 1) * 128],
                        rhs=Y2[:, tc * 512:(tc + 1) * 512],
                        start=True, stop=True), inc=TE)
                    q.wait("vector", TE)
                    q.emit("vector", lambda e, tc=tc, dc=dc: e.tensor_add(
                        out=x_sb[dc][:, tc * 512:(tc + 1) * 512],
                        in0=x_sb[dc][:, tc * 512:(tc + 1) * 512],
                        in1=p1a[:]), inc=VE)
        q.barrier(ALLS)

    if 3 <= stage_limit < 11:
        mcur = [ARENA0]
        lnf = alloc("lnf", [128, 8 * T], BF16, mcur)
        lnf_c = [lnf[:, i * T:(i + 1) * T] for i in range(8)]
        MSNAP = list(mcur)
        lgbuf = alloc("lgbuf", [8, T], F32, mcur)
        lg_tm = alloc("lg_tm", [128, 8], F32, mcur)
        mcolA = alloc("mcolA", [128, 1], F32, mcur)
        mcolB = alloc("mcolB", [128, 1], F32, mcur)
        mcolC = alloc("mcolC", [128, 8], F32, mcur)
        cwf = alloc("cwf", [8, T], F32, mcur)
        cwb0 = alloc("cwb0", [128, 8], F32, mcur)
        ecur = list(MSNAP)
        cwrow = alloc("cwrow", [1, T], F32, ecur)
        cwb = alloc("cwb", [128, 256], BF16, ecur)
        wgl_sb = alloc("wgl_sb", [128, 64], BF16, mcur)
        wgu_sb = alloc("wgu_sb", [128, 8 * 16 * 128], BF16, ecur)
        wdn_sb = alloc("wdn_sb", [128, 8 * 8 * 128], BF16, ecur)
        gact = alloc("gact", [128, 256], BF16, ecur)
        uact = alloc("uact", [128, 256], BF16, ecur)
        gactB = alloc("gactB", [128, 256], BF16, ecur)
        uactB = alloc("uactB", [128, 256], BF16, ecur)
        act = [alloc(f"act{i}", [128, 256], BF16, ecur) for i in range(8)]

        layer_norm(T, lambda dc, ws, wn: x_sb[dc][:, ws:ws + wn], lnf_c)
        dma("gpsimd", wgl_sb[:], wgl_h[:])
        q.barrier(ALLS)
        for tc4 in range(4):
            q.wait("tensor", SC)
            for kc in range(8):
                q.emit("tensor", lambda e, tc4=tc4, kc=kc: e.matmul(
                    out=p1a[0:8, 0:512], lhsT=wgl_sb[:, kc * 8:(kc + 1) * 8],
                    rhs=lnf_c[kc][:, tc4 * 512:(tc4 + 1) * 512],
                    start=(kc == 0), stop=(kc == 7)),
                    inc=(TE if kc == 7 else None))
            q.wait("scalar", TE)
            q.emit("scalar", lambda e, tc4=tc4: e.copy(
                out=lgbuf[:, tc4 * 512:(tc4 + 1) * 512], in_=p1a[0:8, 0:512]),
                inc=SC)
        for tb in range(16):
            q.wait("tensor", SC)
            q.wait("tensor", VE)
            q.emit("tensor", lambda e, tb=tb: e.transpose(
                out=p1b[0:128, 0:8], in_=lgbuf[:, tb * 128:(tb + 1) * 128],
                identity=idm_f[0:8, 0:8]), inc=TE)
            q.wait("vector", TE)
            q.emit("vector", lambda e: e.tensor_copy(out=lg_tm[:], in_=p1b[:, 0:8]),
                   inc=VE)
            q.emit("vector", lambda e: e.reduce_max(
                out=mcolA[:], in_=lg_tm[:], axis=mybir.AxisListType.X,
                negate=True), inc=VE)
            q.wait("scalar", VE)
            q.emit("scalar", lambda e: e.activation(
                out=lg_tm[:], in_=lg_tm[:], func=AF.Exp, bias=mcolA[:],
                accum_out=mcolB[:]), inc=SC)
            q.wait("vector", SC)
            q.emit("vector", lambda e: e.reciprocal(out=mcolB[:], in_=mcolB[:]),
                   inc=VE)
            q.emit("vector", lambda e: e.tensor_scalar_mul(
                out=lg_tm[:], in0=lg_tm[:], scalar1=mcolB[:]), inc=VE)
            q.emit("vector", lambda e: e.reduce_max(
                out=mcolA[:], in_=lg_tm[:], axis=mybir.AxisListType.X), inc=VE)
            q.emit("vector", lambda e: e.tensor_scalar(
                out=mcolC[:, 0:8], in0=lg_tm[:], scalar1=mcolA[:], scalar2=None,
                op0=ALU.is_equal), inc=VE)
            q.emit("vector", lambda e: e.tensor_scalar(
                out=mcolC[:, 0:8], in0=mcolC[:, 0:8], scalar1=-1.0, scalar2=1.0,
                op0=ALU.mult, op1=ALU.add), inc=VE)
            q.emit("vector", lambda e: e.tensor_mul(
                out=mcolC[:, 0:8], in0=mcolC[:, 0:8], in1=lg_tm[:]), inc=VE)
            q.emit("vector", lambda e: e.reduce_max(
                out=mcolB[:], in_=mcolC[:, 0:8], axis=mybir.AxisListType.X),
                inc=VE)
            q.emit("vector", lambda e: e.scalar_tensor_tensor(
                out=cwb0[:, 1:2], in0=mcolA[:], scalar=1e-8, in1=mcolB[:],
                op0=ALU.add, op1=ALU.add), inc=VE)
            q.emit("vector", lambda e: e.reciprocal(
                out=cwb0[:, 1:2], in_=cwb0[:, 1:2]), inc=VE)
            q.emit("vector", lambda e: e.tensor_mul(
                out=cwb0[:, 2:3], in0=mcolA[:], in1=cwb0[:, 1:2]), inc=VE)
            q.emit("vector", lambda e: e.tensor_mul(
                out=cwb0[:, 3:4], in0=mcolB[:], in1=cwb0[:, 1:2]), inc=VE)
            q.emit("vector", lambda e: e.tensor_scalar(
                out=mcolC[:, 0:8], in0=mcolC[:, 0:8], scalar1=mcolB[:],
                scalar2=None, op0=ALU.is_equal), inc=VE)
            q.emit("vector", lambda e: e.tensor_scalar_mul(
                out=mcolC[:, 0:8], in0=mcolC[:, 0:8], scalar1=cwb0[:, 3:4]),
                inc=VE)
            q.emit("vector", lambda e: e.tensor_scalar(
                out=cwb0[:, 8:16] if False else lg_tm[:], in0=lg_tm[:],
                scalar1=mcolA[:], scalar2=None, op0=ALU.is_equal), inc=VE)
            q.emit("vector", lambda e: e.tensor_scalar_mul(
                out=lg_tm[:], in0=lg_tm[:], scalar1=cwb0[:, 2:3]), inc=VE)
            q.emit("vector", lambda e: e.tensor_add(
                out=mcolC[:, 0:8], in0=mcolC[:, 0:8], in1=lg_tm[:]), inc=VE)
            q.wait("tensor", VE)
            q.emit("tensor", lambda e, tb=tb: e.transpose(
                out=p1b[0:8, 0:128], in_=mcolC[:, 0:8],
                identity=idm_f[:]), inc=TE)
            q.wait("scalar", TE)
            q.emit("scalar", lambda e, tb=tb: e.copy(
                out=cwf[:, tb * 128:(tb + 1) * 128], in_=p1b[0:8, 0:128]), inc=SC)
        q.wait("sync", SC)
        dma("sync", bncCW[:], cwf[:])
        q.barrier(ALLS)
        gp = (p2[:, 0:256], p2[:, 512:768])
        upp = (p4[:, 0:256], p4[:, 512:768])
        ga2 = (gact, gactB)
        ua2 = (uact, uactB)
        pdn = (p1a, p1b)
        mo = {"gevm": [SC.n, SC.n], "gmm": [TE.n, TE.n],
              "actm": [VE.n, VE.n], "dmm": [TE.n, TE.n],
              "xaddm": [VE.n, VE.n], "dall": TE.n, "vlast": VE.n,
              "cwm": SC.n}
        for ex in range(9):
            q.wait("gpsimd", TE)
            dma("gpsimd", wgu_sb[:], wgu_h[ex])
            dma("gpsimd", wdn_sb[:], wdn_h[ex])
            if ex < 8:
                q.wait("sync", TE)
                dma("sync", cwrow[:], bncCW[ex:ex + 1, :])
            q.wait("tensor", GP)
            q.wait("tensor", SY)
            for tc in range(8):
                if ex < 8:
                    q.wait("tensor", SC, mo["cwm"])
                    q.wait("tensor", VE, mo["xaddm"][1])
                    q.emit("tensor", lambda e, tc=tc: e.matmul(
                        out=p1b[:, 0:256], lhsT=onesr_f[:],
                        rhs=cwrow[0:1, tc * 256:(tc + 1) * 256],
                        start=True, stop=True), inc=TE)
                    rm = TE.n
                    q.wait("scalar", TE, rm)
                    q.wait("scalar", VE, mo["vlast"])
                    q.emit("scalar", lambda e: e.copy(
                        out=cwb[:], in_=p1b[:, 0:256]), inc=SC)
                    mo["cwm"] = SC.n
                q.wait("vector", TE, mo["dall"])
                for ic in range(8):
                    pr = ic % 2
                    q.wait("tensor", SC, mo["gevm"][pr])
                    for kc in range(8):
                        q.emit("tensor", lambda e, tc=tc, ic=ic, kc=kc, pr=pr:
                               e.matmul(
                            out=gp[pr],
                            lhsT=wgu_sb[:, (kc * 16 + ic) * 128:(kc * 16 + ic + 1) * 128],
                            rhs=lnf_c[kc][:, tc * 256:(tc + 1) * 256],
                            start=(kc == 0), stop=(kc == 7)), inc=None)
                        q.emit("tensor", lambda e, tc=tc, ic=ic, kc=kc, pr=pr:
                               e.matmul(
                            out=upp[pr],
                            lhsT=wgu_sb[:, (kc * 16 + 8 + ic) * 128:(kc * 16 + 9 + ic) * 128],
                            rhs=lnf_c[kc][:, tc * 256:(tc + 1) * 256],
                            start=(kc == 0), stop=(kc == 7)),
                            inc=(TE if kc == 7 else None))
                    mo["gmm"][pr] = TE.n
                    q.wait("scalar", TE, mo["gmm"][pr])
                    q.wait("scalar", VE, mo["actm"][pr])
                    q.emit("scalar", lambda e, pr=pr: e.activation(
                        out=ga2[pr][:], in_=gp[pr], func=AF.Silu), inc=SC)
                    q.emit("scalar", lambda e, pr=pr: e.copy(
                        out=ua2[pr][:], in_=upp[pr]), inc=SC)
                    mo["gevm"][pr] = SC.n
                    q.wait("vector", SC, mo["gevm"][pr])
                    q.emit("vector", lambda e, pr=pr: e.tensor_mul(
                        out=ga2[pr][:], in0=ga2[pr][:], in1=ua2[pr][:]), inc=VE)
                    if ex < 8:
                        q.emit("vector", lambda e, ic=ic, pr=pr: e.tensor_mul(
                            out=act[ic][:], in0=ga2[pr][:], in1=cwb[:]), inc=VE)
                    else:
                        q.emit("vector", lambda e, ic=ic, pr=pr: e.tensor_copy(
                            out=act[ic][:], in_=ga2[pr][:]), inc=VE)
                    mo["actm"][pr] = VE.n
                mo["vlast"] = VE.n
                for dc in range(8):
                    pr = dc % 2
                    q.wait("tensor", VE, mo["xaddm"][pr])
                    if dc == 0:
                        q.wait("tensor", VE, mo["actm"][0])
                        q.wait("tensor", VE, mo["actm"][1])
                    if pr == 1:
                        q.wait("tensor", SC, mo["cwm"])
                    for ic in range(8):
                        q.emit("tensor", lambda e, dc=dc, ic=ic, pr=pr: e.matmul(
                            out=pdn[pr][:, 0:256],
                            lhsT=wdn_sb[:, (ic * 8 + dc) * 128:(ic * 8 + dc + 1) * 128],
                            rhs=act[ic][:], start=(ic == 0), stop=(ic == 7)),
                            inc=(TE if ic == 7 else None))
                    mo["dmm"][pr] = TE.n
                    q.wait("vector", TE, mo["dmm"][pr])
                    q.emit("vector", lambda e, dc=dc, tc=tc, pr=pr: e.tensor_add(
                        out=x_sb[dc][:, tc * 256:(tc + 1) * 256],
                        in0=x_sb[dc][:, tc * 256:(tc + 1) * 256],
                        in1=pdn[pr][:, 0:256]), inc=VE)
                    mo["xaddm"][pr] = VE.n
                mo["dall"] = TE.n
        q.barrier(ALLS)


    if 4 <= stage_limit < 11:
        ccur = [ARENA0]
        hlnc = alloc("hlnc", [128, 8 * CH], BF16, ccur)
        hlnc_c = [hlnc[:, i * CH:(i + 1) * CH] for i in range(8)]
        xit = [alloc(f"xit{i}", [128, 515], BF16, ccur) for i in range(2)]
        xcb = alloc("xcb", [128, 16 * CH], BF16, ccur)
        dtrows = alloc("dtrows", [64, CH], BF16, ccur)
        Bsb = alloc("Bsb", [16, CH], BF16, ccur)
        Csb = alloc("Csb", [16, CH], BF16, ccur)
        nsel_sb = alloc("nsel_sb", [16, 8 * 128], BF16, ccur)
        Brep = alloc("Brep", [128, 16 * CH], BF16, ccur)
        Crep = alloc("Crep", [128, 16 * CH], BF16, ccur)
        dtile = alloc("dtile", [128, CH], BF16, ccur)
        dutile = alloc("dutile", [128, CH], BF16, ccur)
        atile = alloc("atile", [128, CH], BF16, ccur)
        wtile = alloc("wtile", [128, CH], BF16, ccur)
        htile = alloc("htile", [128, CH], BF16, ccur)
        hctmp = alloc("hctmp", [128, CH], BF16, ccur)
        atile2 = alloc("atile2", [128, CH], BF16, ccur)
        htile2 = alloc("htile2", [128, CH], BF16, ccur)
        hctmp2 = alloc("hctmp2", [128, CH], BF16, ccur)
        dtile2 = alloc("dtile2", [128, CH], BF16, ccur)
        dutile2 = alloc("dutile2", [128, CH], BF16, ccur)
        zst2 = alloc("zst2", [128, CH], BF16, ccur)
        yacc = alloc("yacc", [128, 8 * CH], BF16, ccur)
        zst = alloc("zst", [128, CH], BF16, ccur)
        ygated = xcb
        winr = alloc("winr", [128, 2048], BF16, ccur)
        moutr = winr
        xprojw = alloc("xprojw", [128, 16 * 96], BF16, ccur)
        dtwsb = alloc("dtwsb", [64, DI], BF16, ccur)
        convr = alloc("convr", [128, 1024], BF16, ccur)
        zwr = convr
        fwdt = alloc("fwdt", [128, CH], BF16, ccur)
        fwdin = alloc("fwdin", [128, 8 * CH], BF16, ccur)
        owr = alloc("owr", [128, 8 * 128], BF16, ccur)
        kcur = [ARENA0]
        gath = alloc("gath", [128, 8 * 304], F32, kcur)
        pselbig_sb = alloc("pselbig_sb", [128, 8 * 304], F32, kcur)
        ccbuf = alloc("ccbuf", [128, 304], F32, kcur)
        ext_sb = alloc("ext_sb", [128, 304], F32, kcur)

        def chain(ch, phase):
            chunks = range(NCH) if phase == 0 else range(NCH - 1, -1, -1)
            cdb = 0 if ch == "L" else 1
            pl = {"scanm": [VE.n, VE.n], "expm": [SC.n, SC.n],
                  "hcm": [GC.n, GC.n], "savem": [SC.n, SC.n],
                  "pam": [VE.n, VE.n], "dtm": TE.n, "expbm": SC.n,
                  "spm": SC.n, "silum": SC.n, "zmmm": TE.n,
                  "ygm": [VE.n, VE.n]}
            q.wait("gpsimd", TE)
            dma("gpsimd", xprojw[:], xproj_h[ch][:])
            dma("gpsimd", dtwsb[:], dtw_h[ch][:])
            rv = slice(None, None, -1) if phase == 1 else slice(None, None, None)
            for cc in chunks:
                t0 = cc * CH
                layer_norm(CH, lambda dc, ws, wn, t0=t0:
                           x_sb[dc][:, t0 + ws:t0 + ws + wn],
                           hlnc_c, affine=True, double=True)
                q.wait("tensor", VE)
                for dc in range(16):
                    g = dc % 2
                    if g == 0:
                        q.wait("gpsimd", TE)
                        dma("gpsimd", winr[:], win_h[ch][dc // 2])
                        dma("gpsimd", convr[:], convd_h[cdb, dc // 2])
                        q.wait("tensor", GP)
                    xi = xit[g]
                    q.wait("scalar", TE)
                    if phase == 0:
                        q.emit("scalar", lambda e, dc=dc, ch=ch, xi=xi: e.copy(
                            out=xi[:, 0:3], in_=ximg[ch][:, dc * 3:dc * 3 + 3]),
                            inc=SC)
                    else:
                        q.emit("scalar", lambda e, dc=dc, ch=ch, xi=xi: e.copy(
                            out=xi[:, 512:515], in_=ximg[ch][:, dc * 3:dc * 3 + 3]),
                            inc=SC)
                    q.wait("tensor", SC)
                    for kc in range(8):
                        q.emit("tensor", lambda e, g=g, kc=kc: e.matmul(
                            out=p2[:, 0:512],
                            lhsT=winr[:, (kc * 2 + g) * 128:(kc * 2 + g + 1) * 128],
                            rhs=hlnc_c[kc], start=(kc == 0), stop=(kc == 7)),
                            inc=(TE if kc == 7 else None))
                    q.wait("scalar", TE)
                    xoff = 3 if phase == 0 else 0
                    q.emit("scalar", lambda e, xi=xi, xoff=xoff: e.copy(
                        out=xi[:, xoff:xoff + 512], in_=p2[:, 0:512]), inc=SC)
                    moff = 512 if phase == 0 else 0
                    q.emit("scalar", lambda e, xi=xi, dc=dc, ch=ch, moff=moff: e.copy(
                        out=ximg[ch][:, dc * 3:dc * 3 + 3], in_=xi[:, moff:moff + 3]),
                        inc=SC)
                    q.wait("tensor", SC)
                    for j in range(4):
                        q.emit("tensor", lambda e, g=g, j=j, xi=xi: e.matmul(
                            out=p1a[:, 0:512],
                            lhsT=convr[:, (g * 4 + j) * 128:(g * 4 + j + 1) * 128],
                            rhs=xi[:, j:j + 512], start=(j == 0), stop=(j == 3)),
                            inc=(TE if j == 3 else None))
                    q.wait("scalar", TE)
                    q.emit("scalar", lambda e, dc=dc, ch=ch: e.activation(
                        out=xcb[:, dc * CH:(dc + 1) * CH], in_=p1a[:, 0:512],
                        func=AF.Silu, bias=convb_sb[ch][:, dc:dc + 1]), inc=SC)
                q.wait("tensor", SC)
                q.wait("tensor", VE)
                q.wait("tensor", GC)
                for kc in range(16):
                    q.emit("tensor", lambda e, kc=kc: e.matmul(
                        out=p4[0:64, 0:512], lhsT=xprojw[:, kc * 96:kc * 96 + 64],
                        rhs=xcb[:, kc * CH:(kc + 1) * CH],
                        start=(kc == 0), stop=(kc == 15)), inc=None)
                    q.emit("tensor", lambda e, kc=kc: e.matmul(
                        out=p4[0:16, 512:1024],
                        lhsT=xprojw[:, kc * 96 + 64:kc * 96 + 80],
                        rhs=xcb[:, kc * CH:(kc + 1) * CH],
                        start=(kc == 0), stop=(kc == 15)), inc=None)
                    q.emit("tensor", lambda e, kc=kc: e.matmul(
                        out=p4[0:16, 1024:1536],
                        lhsT=xprojw[:, kc * 96 + 80:kc * 96 + 96],
                        rhs=xcb[:, kc * CH:(kc + 1) * CH],
                        start=(kc == 0), stop=(kc == 15)),
                        inc=(TE if kc == 15 else None))
                q.wait("scalar", TE)
                q.emit("scalar", lambda e: e.copy(out=dtrows[:], in_=p4[0:64, 0:512]),
                       inc=SC)
                q.emit("scalar", lambda e: e.copy(out=Bsb[:], in_=p4[0:16, 512:1024]),
                       inc=SC)
                q.emit("scalar", lambda e: e.copy(out=Csb[:], in_=p4[0:16, 1024:1536]),
                       inc=SC)
                for n in range(16):
                    if n % 8 == 0:
                        q.wait("gpsimd", TE)
                        dma("gpsimd", nsel_sb[:], nsel_h[:, n * 128:(n + 8) * 128])
                        q.wait("tensor", GP)
                    q.wait("tensor", SC)
                    q.emit("tensor", lambda e, n=n: e.matmul(
                        out=p1b[:, 0:512], lhsT=nsel_sb[:, (n % 8) * 128:(n % 8 + 1) * 128],
                        rhs=Bsb[:], start=True, stop=True), inc=TE)
                    q.wait("scalar", TE)
                    q.emit("scalar", lambda e, n=n: e.copy(
                        out=Brep[:, n * CH:(n + 1) * CH], in_=p1b[:, 0:512]), inc=SC)
                    q.wait("tensor", SC)
                    q.emit("tensor", lambda e, n=n: e.matmul(
                        out=p1b[:, 0:512], lhsT=nsel_sb[:, (n % 8) * 128:(n % 8 + 1) * 128],
                        rhs=Csb[:], start=True, stop=True), inc=TE)
                    q.wait("scalar", TE)
                    q.emit("scalar", lambda e, n=n: e.copy(
                        out=Crep[:, n * CH:(n + 1) * CH], in_=p1b[:, 0:512]), inc=SC)
                at2 = (atile, atile2)
                ht2 = (htile, htile2)
                hc2 = (hctmp, hctmp2)
                dt2 = (dtile, dtile2)
                du2 = (dutile, dutile2)
                zs2 = (zst, zst2)
                sv = 511 if phase == 0 else 0
                pl["expbm"] = SC.n

                def dchead_te(dc):
                    q.wait("tensor", SC, pl["expbm"])
                    q.emit("tensor", lambda e, dc=dc: e.matmul(
                        out=p1b[:, 0:512],
                        lhsT=dtwsb[:, dc * 128:(dc + 1) * 128],
                        rhs=dtrows[:], start=True, stop=True), inc=TE)
                    pl["dtm"] = TE.n

                def dchead_sc(dc):
                    q.wait("scalar", TE, pl["dtm"])
                    q.emit("scalar", lambda e, dc=dc, ch=ch: e.activation(
                        out=lnt1[:, 0:512], in_=p1b[:, 0:512], func=AF.Exp,
                        bias=dtb_sb[ch][:, dc:dc + 1]), inc=SC)
                    pl["expbm"] = SC.n
                    q.emit("scalar", lambda e, dc=dc: e.activation(
                        out=dt2[dc % 2][:], in_=lnt1[:, 0:512], func=AF.Ln,
                        bias=1.0), inc=SC)
                    pl["spm"] = SC.n

                def dchead_ve(dc):
                    q.wait("vector", SC, pl["spm"])
                    q.emit("vector", lambda e, dc=dc: e.tensor_mul(
                        out=du2[dc % 2][:], in0=dt2[dc % 2][:],
                        in1=xcb[:, dc * CH:(dc + 1) * CH]), inc=VE)

                dchead_te(0)
                dchead_sc(0)
                dchead_ve(0)
                for dc in range(16):
                    for n in range(17):
                        if n < 16:
                            q.wait("scalar", VE, pl["scanm"][n % 2])
                            q.emit("scalar", lambda e, n=n, dc=dc, ch=ch:
                                   e.activation(
                                out=at2[n % 2][:], in_=dt2[dc % 2][:],
                                func=AF.Exp,
                                scale=avec_sb[ch][:, n:n + 1]), inc=SC)
                            pl["expm"][n % 2] = SC.n
                            q.emit("vector", lambda e, n=n, dc=dc: e.tensor_mul(
                                out=wtile[:], in0=du2[dc % 2][:],
                                in1=Brep[:, n * CH:(n + 1) * CH]), inc=VE)
                            q.wait("vector", SC, pl["expm"][n % 2])
                            q.wait("vector", SC, pl["savem"][n % 2])
                            q.emit("vector", lambda e, n=n, dc=dc, ch=ch, rv=rv:
                                   e.tensor_tensor_scan(
                                out=ht2[n % 2][:, rv], data0=at2[n % 2][:, rv],
                                data1=wtile[:, rv],
                                initial=stbuf[ch][:, dc * 16 + n:dc * 16 + n + 1],
                                op0=ALU.mult, op1=ALU.add), inc=VE)
                            pl["scanm"][n % 2] = VE.n
                        if n >= 1:
                            m = n - 1
                            q.wait("scalar", VE, pl["scanm"][m % 2])
                            q.emit("scalar", lambda e, m=m, dc=dc, ch=ch, sv=sv:
                                   e.copy(
                                out=stbuf[ch][:, dc * 16 + m:dc * 16 + m + 1],
                                in_=ht2[m % 2][:, sv:sv + 1]), inc=SC)
                            pl["savem"][m % 2] = SC.n
                            q.emit("vector", lambda e, m=m: e.tensor_mul(
                                out=hc2[m % 2][:], in0=ht2[m % 2][:],
                                in1=Crep[:, m * CH:(m + 1) * CH]), inc=VE)
                        if n >= 2 and n % 2 == 0:
                            p = (n - 2) // 2
                            q.emit("vector", lambda e, p=p: e.tensor_add(
                                out=yacc[:, p * CH:(p + 1) * CH],
                                in0=hc2[0][:], in1=hc2[1][:]), inc=VE)
                        if n == 2:
                            q.wait("gpsimd", TE)
                            dma("gpsimd", zwr[:], wz_h[ch][dc])
                        if n == 4:
                            q.wait("tensor", GP)
                            q.wait("tensor", SC, pl["silum"])
                            for kc in range(8):
                                q.emit("tensor", lambda e, kc=kc: e.matmul(
                                    out=p2[:, 512:1024],
                                    lhsT=zwr[:, kc * 128:(kc + 1) * 128],
                                    rhs=hlnc_c[kc], start=(kc == 0),
                                    stop=(kc == 7)),
                                    inc=(TE if kc == 7 else None))
                            pl["zmmm"] = TE.n
                        if n == 6 and dc < 15:
                            dchead_te(dc + 1)
                        if n == 8:
                            q.wait("scalar", TE, pl["zmmm"])
                            q.wait("scalar", VE, pl["ygm"][dc % 2])
                            q.emit("scalar", lambda e, dc=dc: e.activation(
                                out=zs2[dc % 2][:], in_=p2[:, 512:1024],
                                func=AF.Silu), inc=SC)
                            pl["silum"] = SC.n
                        if n == 10 and dc < 15:
                            dchead_sc(dc + 1)
                        if n == 12 and dc < 15:
                            dchead_ve(dc + 1)
                    q.emit("vector", lambda e: e.tensor_add(
                        out=yacc[:, 0:CH], in0=yacc[:, 0:CH], in1=yacc[:, CH:2 * CH]),
                        inc=VE)
                    q.emit("vector", lambda e: e.tensor_add(
                        out=yacc[:, 2 * CH:3 * CH], in0=yacc[:, 2 * CH:3 * CH],
                        in1=yacc[:, 3 * CH:4 * CH]), inc=VE)
                    q.emit("vector", lambda e: e.tensor_add(
                        out=yacc[:, 4 * CH:5 * CH], in0=yacc[:, 4 * CH:5 * CH],
                        in1=yacc[:, 5 * CH:6 * CH]), inc=VE)
                    q.emit("vector", lambda e: e.tensor_add(
                        out=yacc[:, 6 * CH:7 * CH], in0=yacc[:, 6 * CH:7 * CH],
                        in1=yacc[:, 7 * CH:8 * CH]), inc=VE)
                    q.emit("vector", lambda e: e.tensor_add(
                        out=yacc[:, 0:CH], in0=yacc[:, 0:CH],
                        in1=yacc[:, 2 * CH:3 * CH]), inc=VE)
                    q.emit("vector", lambda e: e.tensor_add(
                        out=yacc[:, 4 * CH:5 * CH], in0=yacc[:, 4 * CH:5 * CH],
                        in1=yacc[:, 6 * CH:7 * CH]), inc=VE)
                    q.emit("vector", lambda e: e.tensor_add(
                        out=yacc[:, 0:CH], in0=yacc[:, 0:CH],
                        in1=yacc[:, 4 * CH:5 * CH]), inc=VE)
                    q.emit("vector", lambda e, dc=dc, ch=ch: e.scalar_tensor_tensor(
                        out=wtile[:], in0=xcb[:, dc * CH:(dc + 1) * CH],
                        scalar=dv_sb[ch][:, dc:dc + 1], in1=yacc[:, 0:CH],
                        op0=ALU.mult, op1=ALU.add), inc=VE)
                    q.wait("vector", SC, pl["silum"])
                    q.emit("vector", lambda e, dc=dc: e.tensor_mul(
                        out=ygated[:, dc * CH:(dc + 1) * CH], in0=wtile[:],
                        in1=zs2[dc % 2][:]), inc=VE)
                    pl["ygm"][dc % 2] = VE.n
                for mc in range(8):
                    q.wait("gpsimd", TE)
                    q.wait("gpsimd", SY)
                    dma("gpsimd", moutr[:], wmout_h[ch][mc])
                    q.wait("tensor", GP)
                    q.wait("tensor", SC)
                    q.wait("tensor", VE)
                    for ic in range(16):
                        q.emit("tensor", lambda e, ic=ic: e.matmul(
                            out=p2[:, 0:512], lhsT=moutr[:, ic * 128:(ic + 1) * 128],
                            rhs=ygated[:, ic * CH:(ic + 1) * CH],
                            start=(ic == 0), stop=(ic == 15)),
                            inc=(TE if ic == 15 else None))
                    q.wait("scalar", TE)
                    if phase == 0:
                        q.wait("scalar", SY)
                        q.emit("scalar", lambda e, mc=mc: e.copy(
                            out=fwdt[:], in_=p2[:, 0:512]), inc=SC)
                        q.wait("sync", SC)
                        q.emit("sync", lambda e, mc=mc, cc=cc: e.dma_start(
                            out=fwdscr[cc][:, mc * CH:(mc + 1) * CH],
                            in_=fwdt[:]), inc=SY, inck=16)
                    else:
                        q.emit("scalar", lambda e, mc=mc: e.copy(
                            out=yacc[:, mc * CH:(mc + 1) * CH], in_=p2[:, 0:512]),
                            inc=SC)
                if phase == 1:
                    layer_norm(CH, lambda dc, ws, wn, t0=t0:
                               x_sb[dc][:, t0 + ws:t0 + ws + wn], hlnc_c)
                    q.wait("gpsimd", TE)
                    dma("gpsimd", fwdin[:], fwdscr[cc])
                    q.wait("tensor", GP)
                    for dc in range(8):
                        q.wait("gpsimd", TE)
                        dma("gpsimd", owr[:], ow_h["L"][dc])
                        q.wait("tensor", GP)
                        q.wait("tensor", SC)
                        q.wait("tensor", VE)
                        for kc in range(8):
                            q.emit("tensor", lambda e, kc=kc: e.matmul(
                                out=p1a[:, 0:512],
                                lhsT=owr[:, kc * 128:(kc + 1) * 128],
                                rhs=fwdin[:, kc * CH:(kc + 1) * CH],
                                start=(kc == 0), stop=False), inc=TE)
                        q.wait("gpsimd", TE)
                        dma("gpsimd", owr[:], ow_h["R"][dc])
                        q.wait("tensor", GP)
                        for kc in range(8):
                            q.emit("tensor", lambda e, kc=kc: e.matmul(
                                out=p1a[:, 0:512],
                                lhsT=owr[:, kc * 128:(kc + 1) * 128],
                                rhs=yacc[:, kc * CH:(kc + 1) * CH],
                                start=False, stop=(kc == 7)),
                                inc=(TE if kc == 7 else None))
                        q.wait("vector", TE)
                        q.emit("vector", lambda e, dc=dc: e.scalar_tensor_tensor(
                            out=lnt1[:, 0:512], in0=p1a[:, 0:512],
                            scalar=outb_sb[:, dc:dc + 1], in1=hlnc_c[dc],
                            op0=ALU.add, op1=ALU.add), inc=VE)
                        q.emit("vector", lambda e, dc=dc, t0=t0: e.tensor_add(
                            out=x_sb[dc][:, t0:t0 + CH], in0=x_sb[dc][:, t0:t0 + CH],
                            in1=lnt1[:, 0:512]), inc=VE)

        q.emit("vector", lambda e: e.memset(stbuf["L"][:], 0.0), inc=VE)
        q.emit("vector", lambda e: e.memset(stbuf["R"][:], 0.0), inc=VE)
        q.emit("vector", lambda e: e.memset(ximg["L"][:], 0.0), inc=VE)
        q.emit("vector", lambda e: e.memset(ximg["R"][:], 0.0), inc=VE)
        q.barrier([VE])
        chain("L", 0)
        q.barrier(ALLS)
        if stage_limit == 6:
            pass
        if stage_limit != 6:
            dma("sync", pselbig_sb[:], pselbig_h[:])
            q.wait("vector", SY)
            q.emit("vector", lambda e: e.tensor_copy(out=ccbuf[:, 0:256],
                                                     in_=stbuf["L"][:]), inc=VE)
            q.emit("vector", lambda e: e.tensor_copy(out=ccbuf[:, 256:304],
                                                     in_=ximg["L"][:]), inc=VE)
            q.wait("sync", VE)
            dma("sync", cc_in[:], ccbuf[:])
            q.wait("gpsimd", SY)
            q.emit("gpsimd", lambda e: e.collective_compute(
                "AllGather", ALU.bypass, replica_groups=[list(range(8))],
                ins=[cc_in[:]], outs=[cc_out[:]]), inc=CC)
            q.wait("sync", CC)
            dma("sync", gath[:].rearrange("p (a b) -> p a b", a=8),
                cc_out.transpose([1, 0, 2]))
            q.wait("vector", SY)
            q.emit("vector", lambda e: e.tensor_mul(
                out=gath[:], in0=gath[:], in1=pselbig_sb[:]), inc=VE)
            for pr in ((0, 1), (2, 3), (4, 5), (6, 7), (0, 2), (4, 6), (0, 4)):
                q.emit("vector", lambda e, pr=pr: e.tensor_add(
                    out=gath[:, pr[0] * 304:pr[0] * 304 + 304],
                    in0=gath[:, pr[0] * 304:pr[0] * 304 + 304],
                    in1=gath[:, pr[1] * 304:pr[1] * 304 + 304]), inc=VE)
            q.wait("scalar", VE)
            q.emit("scalar", lambda e: e.copy(out=stbuf["R"][:], in_=gath[:, 0:256]),
                   inc=SC)
            for dc in range(16):
                q.emit("scalar", lambda e, dc=dc: e.copy(
                    out=ximg["R"][:, dc * 3:dc * 3 + 3],
                    in_=gath[:, 256 + dc * 3:256 + dc * 3 + 3][:, ::-1]), inc=SC)
            q.barrier(ALLS)
            chain("R", 1)
            q.barrier(ALLS)


    if 8 <= stage_limit < 11:
        fcur = [ARENA0]
        wring = alloc("wring", [128, 2 * 1024], BF16, fcur)
        wfr = [alloc(f"wfr{i}", [128, 1024], F32, fcur) for i in range(2)]
        ln3b = alloc("ln3b", [128, 8 * 512], BF16, fcur)
        ln3b_c = [ln3b[:, i * 512:(i + 1) * 512] for i in range(8)]
        ln3f = alloc("ln3f", [128, 8 * 512], F32, fcur)
        ln3f_c = [ln3f[:, i * 512:(i + 1) * 512] for i in range(8)]
        xq = alloc("xq", [128, 8 * 512], BF16, fcur)
        hact = alloc("hact", [128, 32 * 512], BF16, fcur)
        mxc = alloc("mxc", [128, 8], F32, fcur)
        sxrow = alloc("sxrow", [1, 512], F32, fcur)
        s2row = alloc("s2row", [1, 512], F32, fcur)
        dqg = alloc("dqg", [128, 512], F32, fcur)
        dqu = alloc("dqu", [128, 512], F32, fcur)
        dqd = alloc("dqd", [128, 512], F32, fcur)
        sxrep = alloc("sxrep", [128, 512], F32, fcur)
        gtmp = alloc("gtmp", [128, 512], BF16, fcur)
        wdr = alloc("wdr", [128, 4096], BF16, fcur)

        q.wait("vector", VE)
        q.emit("vector", lambda e: e.memset(wmean[:], 0.0), inc=VE)
        for j, wh in enumerate((wg2_h, wu2_h)):
            for mc in range(32):
                q.wait("sync", VE)
                dma("sync", wfr[mc % 2][:], wh[mc])
                q.wait("vector", SY)
                q.emit("vector", lambda e, mc=mc: e.reduce_sum(
                    out=lnt1[:, 0:1], in_=wfr[mc % 2][:],
                    axis=mybir.AxisListType.X, apply_absolute_value=True), inc=VE)
                q.emit("vector", lambda e, j=j: e.tensor_add(
                    out=wmean[:, j:j + 1], in0=wmean[:, j:j + 1],
                    in1=lnt1[:, 0:1]), inc=VE)
        for dc in range(8):
            for half in range(2):
                q.wait("sync", VE)
                dma("sync", wdr[:].bitcast(F32)[:, 0:2048],
                    wd2_h[dc][:, half * 2048:(half + 1) * 2048])
                q.wait("vector", SY)
                q.emit("vector", lambda e: e.reduce_sum(
                    out=lnt1[:, 0:1], in_=wdr[:].bitcast(F32)[:, 0:2048],
                    axis=mybir.AxisListType.X, apply_absolute_value=True), inc=VE)
                q.emit("vector", lambda e: e.tensor_add(
                    out=wmean[:, 2:3], in0=wmean[:, 2:3], in1=lnt1[:, 0:1]), inc=VE)
        q.wait("tensor", VE)
        q.emit("tensor", lambda e: e.matmul(
            out=p1a[0:1, 0:4], lhsT=onesc_f[:], rhs=wmean[:, 0:4],
            start=True, stop=True), inc=TE)
        q.wait("vector", TE)
        q.emit("vector", lambda e: e.tensor_scalar(
            out=cmean_sb[0:1, 0:4], in0=p1a[0:1, 0:4],
            scalar1=1.0 / (INNER * D), scalar2=1e-5, op0=ALU.mult,
            op1=ALU.max), inc=VE)
        q.emit("vector", lambda e: e.reciprocal(
            out=smean[0:1, 0:4], in_=cmean_sb[0:1, 0:4]), inc=VE)
        q.wait("tensor", VE)
        q.emit("tensor", lambda e: e.matmul(
            out=p1a[:, 0:4], lhsT=onesr_f[:], rhs=smean[0:1, 0:4],
            start=True, stop=True), inc=TE)
        q.wait("scalar", TE)
        q.emit("scalar", lambda e: e.copy(out=svec[:, 0:4], in_=p1a[:, 0:4]), inc=SC)
        for j, (wh, scr, nt) in enumerate(
                ((wg2_h, wqg_scr, 32), (wu2_h, wqu_scr, 32), (wd2_h, wqd_scr, 8))):
            sj = 2 if j == 2 else j
            for ti in range(nt):
                wf = wfr[ti % 2]
                wide = 4096 if j == 2 else 1024
                if j == 2:
                    q.wait("sync", SY)
                    q.wait("sync", SC)
                    dma("sync", wfr[0][:], wh[ti][:, 0:1024])
                    dma("sync", wfr[1][:], wh[ti][:, 1024:2048])
                    for half in range(2):
                        q.wait("scalar", SY)
                        q.wait("scalar", VE)
                        q.emit("scalar", lambda e, half=half, sj=sj: e.activation(
                            out=lnt1[:, 0:512], in_=wfr[half][:, 0:512],
                            func=AF.Copy, scale=svec[:, sj:sj + 1],
                            bias=float(RND)), inc=SC)
                        q.emit("scalar", lambda e, half=half, sj=sj: e.activation(
                            out=wfr[half][:, 0:512], in_=lnt1[:, 0:512],
                            func=AF.Copy, bias=float(-RND)), inc=SC)
                        q.emit("scalar", lambda e, half=half, sj=sj: e.activation(
                            out=lnt1[:, 0:512], in_=wfr[half][:, 512:1024],
                            func=AF.Copy, scale=svec[:, sj:sj + 1],
                            bias=float(RND)), inc=SC)
                        q.emit("scalar", lambda e, half=half, sj=sj: e.activation(
                            out=wfr[half][:, 512:1024], in_=lnt1[:, 0:512],
                            func=AF.Copy, bias=float(-RND)), inc=SC)
                        q.wait("vector", SC)
                        q.emit("vector", lambda e, half=half: e.tensor_scalar(
                            out=wring[:, half * 1024:(half + 1) * 1024],
                            in0=wfr[half][:], scalar1=1.0, scalar2=-1.0,
                            op0=ALU.min, op1=ALU.max), inc=VE)
                    q.wait("sync", VE)
                    q.emit("sync", lambda e, ti=ti: e.dma_start(
                        out=scr[ti][:, 0:2048], in_=wring[:]), inc=SY, inck=16)
                    q.wait("sync", SY)
                    dma("sync", wfr[0][:], wh[ti][:, 2048:3072])
                    dma("sync", wfr[1][:], wh[ti][:, 3072:4096])
                    for half in range(2):
                        q.wait("scalar", SY)
                        q.wait("scalar", VE)
                        q.emit("scalar", lambda e, half=half, sj=sj: e.activation(
                            out=lnt1[:, 0:512], in_=wfr[half][:, 0:512],
                            func=AF.Copy, scale=svec[:, sj:sj + 1],
                            bias=float(RND)), inc=SC)
                        q.emit("scalar", lambda e, half=half, sj=sj: e.activation(
                            out=wfr[half][:, 0:512], in_=lnt1[:, 0:512],
                            func=AF.Copy, bias=float(-RND)), inc=SC)
                        q.emit("scalar", lambda e, half=half, sj=sj: e.activation(
                            out=lnt1[:, 0:512], in_=wfr[half][:, 512:1024],
                            func=AF.Copy, scale=svec[:, sj:sj + 1],
                            bias=float(RND)), inc=SC)
                        q.emit("scalar", lambda e, half=half, sj=sj: e.activation(
                            out=wfr[half][:, 512:1024], in_=lnt1[:, 0:512],
                            func=AF.Copy, bias=float(-RND)), inc=SC)
                        q.wait("vector", SC)
                        q.emit("vector", lambda e, half=half: e.tensor_scalar(
                            out=wring[:, half * 1024:(half + 1) * 1024],
                            in0=wfr[half][:], scalar1=1.0, scalar2=-1.0,
                            op0=ALU.min, op1=ALU.max), inc=VE)
                    q.wait("sync", VE)
                    q.emit("sync", lambda e, ti=ti: e.dma_start(
                        out=scr[ti][:, 2048:4096], in_=wring[:]), inc=SY, inck=16)
                    q.wait("sync", SY)
                else:
                    q.wait("sync", SC)
                    q.wait("sync", VE)
                    dma("sync", wf[:], wh[ti])
                    q.wait("scalar", SY)
                    q.wait("scalar", VE)
                    q.emit("scalar", lambda e, wf=wf, sj=sj: e.activation(
                        out=lnt1[:, 0:512], in_=wf[:, 0:512], func=AF.Copy,
                        scale=svec[:, sj:sj + 1], bias=float(RND)), inc=SC)
                    q.emit("scalar", lambda e, wf=wf: e.activation(
                        out=wf[:, 0:512], in_=lnt1[:, 0:512], func=AF.Copy,
                        bias=float(-RND)), inc=SC)
                    q.emit("scalar", lambda e, wf=wf, sj=sj: e.activation(
                        out=lnt1[:, 0:512], in_=wf[:, 512:1024], func=AF.Copy,
                        scale=svec[:, sj:sj + 1], bias=float(RND)), inc=SC)
                    q.emit("scalar", lambda e, wf=wf: e.activation(
                        out=wf[:, 512:1024], in_=lnt1[:, 0:512], func=AF.Copy,
                        bias=float(-RND)), inc=SC)
                    q.wait("vector", SC)
                    q.emit("vector", lambda e, wf=wf, ti=ti: e.tensor_scalar(
                        out=wring[:, (ti % 2) * 1024:(ti % 2 + 1) * 1024],
                        in0=wf[:], scalar1=1.0, scalar2=-1.0,
                        op0=ALU.min, op1=ALU.max), inc=VE)
                    q.wait("sync", VE)
                    q.emit("sync", lambda e, ti=ti, scr=scr: e.dma_start(
                        out=scr[ti], in_=wring[:, (ti % 2) * 1024:(ti % 2 + 1) * 1024]),
                        inc=SY, inck=16)
                    q.wait("sync", SY)
        q.barrier(ALLS)
        for tc in range(4):
            t0 = tc * 512
            layer_norm(512, lambda dc, ws, wn, t0=t0:
                       x_sb[dc][:, t0 + ws:t0 + ws + wn],
                       ln3b_c, out32=ln3f_c)
            for tb in range(4):
                q.wait("tensor", VE)
                q.wait("tensor", SC)
                for dc in range(8):
                    q.emit("tensor", lambda e, tb=tb, dc=dc: e.transpose(
                        out=p2.bitcast(BF16)[:, dc * 128:(dc + 1) * 128],
                        in_=ln3b_c[dc][:, tb * 128:(tb + 1) * 128],
                        identity=idm_bf[:]), inc=(TE if dc == 7 else None))
                q.wait("vector", TE)
                q.emit("vector", lambda e, tb=tb: e.reduce_max(
                    out=mxc[:, tb:tb + 1], in_=p2.bitcast(BF16)[:, 0:1024],
                    axis=mybir.AxisListType.X, apply_absolute_value=True), inc=VE)
                q.emit("vector", lambda e, tb=tb: e.tensor_scalar(
                    out=mxc[:, 4 + tb:5 + tb], in0=mxc[:, tb:tb + 1],
                    scalar1=1e-5, scalar2=None, op0=ALU.max), inc=VE)
                q.emit("vector", lambda e, tb=tb: e.reciprocal(
                    out=mxc[:, 4 + tb:5 + tb], in_=mxc[:, 4 + tb:5 + tb]), inc=VE)
                q.emit("vector", lambda e, tb=tb: e.tensor_scalar_mul(
                    out=mxc[:, 4 + tb:5 + tb], in0=mxc[:, 4 + tb:5 + tb],
                    scalar1=127.0), inc=VE)
                q.wait("scalar", VE)
                q.emit("scalar", lambda e, tb=tb: e.copy(
                    out=gtmp[:, 0:1], in_=mxc[:, 4 + tb:5 + tb]), inc=SC)
                q.wait("tensor", SC)
                q.emit("tensor", lambda e, tb=tb: e.transpose(
                    out=p1b.bitcast(BF16)[0:1, 0:128],
                    in_=gtmp[:, 0:1], identity=idm_bf[:]), inc=TE)
                q.wait("vector", TE)
                q.emit("vector", lambda e, tb=tb: e.tensor_copy(
                    out=sxrow[0:1, tb * 128:(tb + 1) * 128],
                    in_=p1b.bitcast(BF16)[0:1, 0:128]), inc=VE)
            q.wait("tensor", VE)
            q.emit("tensor", lambda e: e.matmul(
                out=p1b[:, 0:512], lhsT=onesr_f[:], rhs=sxrow[0:1, 0:512],
                start=True, stop=True), inc=TE)
            q.wait("scalar", TE)
            q.emit("scalar", lambda e: e.copy(out=sxrep[:], in_=p1b[:, 0:512]),
                   inc=SC)
            for dc in range(8):
                q.wait("vector", SC)
                q.emit("vector", lambda e, dc=dc: e.tensor_mul(
                    out=lnt1[:, 0:512], in0=ln3f_c[dc], in1=sxrep[:]), inc=VE)
                q.emit("vector", lambda e, dc=dc: e.tensor_scalar(
                    out=xq[:, dc * 512:(dc + 1) * 512], in0=lnt1[:, 0:512],
                    scalar1=RND, scalar2=RND, op0=ALU.add, op1=ALU.subtract),
                    inc=VE)
            q.emit("vector", lambda e: e.reciprocal(
                out=s2row[0:1, 0:512], in_=sxrow[0:1, 0:512]), inc=VE)
            for j, dq in ((0, dqg), (1, dqu)):
                q.wait("vector", TE)
                q.emit("vector", lambda e, j=j: e.tensor_single_scalar(
                    out=lnt1[0:1, 0:512], in_=s2row[0:1, 0:512],
                    scalar=cmean_sb[0:1, j:j + 1], op=ALU.mult), inc=VE)
                q.wait("tensor", VE)
                q.wait("tensor", SC)
                q.emit("tensor", lambda e: e.matmul(
                    out=p1b[:, 0:512], lhsT=onesr_f[:], rhs=lnt1[0:1, 0:512],
                    start=True, stop=True), inc=TE)
                q.wait("scalar", TE)
                q.emit("scalar", lambda e, dq=dq: e.copy(out=dq[:], in_=p1b[:, 0:512]),
                       inc=SC)
            for mc in range(32):
                q.wait("gpsimd", TE)
                dma("gpsimd", wring[:, 0:1024], wqg_scr[mc])
                dma("gpsimd", wring[:, 1024:2048], wqu_scr[mc])
                q.wait("tensor", GP)
                q.wait("tensor", SC)
                q.wait("tensor", VE)
                for kc in range(8):
                    q.emit("tensor", lambda e, kc=kc: e.matmul(
                        out=p2[:, 0:512], lhsT=wring[:, kc * 128:(kc + 1) * 128],
                        rhs=xq[:, kc * 512:(kc + 1) * 512],
                        start=(kc == 0), stop=(kc == 7)), inc=None)
                    q.emit("tensor", lambda e, kc=kc: e.matmul(
                        out=p2[:, 512:1024],
                        lhsT=wring[:, 1024 + kc * 128:1024 + (kc + 1) * 128],
                        rhs=xq[:, kc * 512:(kc + 1) * 512],
                        start=(kc == 0), stop=(kc == 7)),
                        inc=(TE if kc == 7 else None))
                q.wait("vector", TE)
                q.emit("vector", lambda e: e.tensor_mul(
                    out=lnt1[:, 0:512], in0=p2[:, 0:512], in1=dqg[:]), inc=VE)
                q.wait("scalar", VE)
                q.emit("scalar", lambda e: e.activation(
                    out=gtmp[:], in_=lnt1[:, 0:512], func=AF.Silu), inc=SC)
                q.wait("vector", SC)
                q.emit("vector", lambda e: e.tensor_tensor(
                    out=lnt1[:, 0:512], in0=p2[:, 512:1024], in1=dqu[:],
                    op=ALU.mult), inc=VE)
                q.emit("vector", lambda e, mc=mc: e.tensor_mul(
                    out=hact[:, mc * 512:(mc + 1) * 512], in0=gtmp[:],
                    in1=lnt1[:, 0:512].bitcast(BF16)[:, 0:512] if False else
                    lnt1[:, 0:512]), inc=VE)
            for tb in range(4):
                for hhalf in range(2):
                    q.wait("tensor", VE)
                    q.wait("tensor", SC)
                    for ic in range(16):
                        q.emit("tensor", lambda e, tb=tb, hhalf=hhalf, ic=ic: e.transpose(
                            out=p4.bitcast(BF16)[:, ic * 128:(ic + 1) * 128],
                            in_=hact[:, (hhalf * 16 + ic) * 512 + tb * 128:
                                     (hhalf * 16 + ic) * 512 + (tb + 1) * 128],
                            identity=idm_bf[:]), inc=(TE if ic == 15 else None))
                    q.wait("vector", TE)
                    q.emit("vector", lambda e, tb=tb, hhalf=hhalf: e.reduce_max(
                        out=mxc[:, hhalf:hhalf + 1],
                        in_=p4.bitcast(BF16)[:, 0:2048],
                        axis=mybir.AxisListType.X, apply_absolute_value=True),
                        inc=VE)
                q.emit("vector", lambda e, tb=tb: e.tensor_max(
                    out=mxc[:, 0:1], in0=mxc[:, 0:1], in1=mxc[:, 1:2]), inc=VE)
                q.emit("vector", lambda e: e.tensor_scalar(
                    out=mxc[:, 0:1], in0=mxc[:, 0:1], scalar1=1e-5,
                    scalar2=None, op0=ALU.max), inc=VE)
                q.emit("vector", lambda e: e.reciprocal(
                    out=mxc[:, 0:1], in_=mxc[:, 0:1]), inc=VE)
                q.emit("vector", lambda e: e.tensor_scalar_mul(
                    out=mxc[:, 0:1], in0=mxc[:, 0:1], scalar1=127.0), inc=VE)
                q.wait("scalar", VE)
                q.emit("scalar", lambda e: e.copy(out=gtmp[:, 0:1], in_=mxc[:, 0:1]),
                       inc=SC)
                q.wait("tensor", SC)
                q.emit("tensor", lambda e: e.transpose(
                    out=p1b.bitcast(BF16)[0:1, 0:128], in_=gtmp[:, 0:1],
                    identity=idm_bf[:]), inc=TE)
                q.wait("vector", TE)
                q.emit("vector", lambda e, tb=tb: e.tensor_copy(
                    out=s2row[0:1, tb * 128:(tb + 1) * 128],
                    in_=p1b.bitcast(BF16)[0:1, 0:128]), inc=VE)
            q.wait("tensor", VE)
            q.emit("tensor", lambda e: e.matmul(
                out=p1b[:, 0:512], lhsT=onesr_f[:], rhs=s2row[0:1, 0:512],
                start=True, stop=True), inc=TE)
            q.wait("scalar", TE)
            q.emit("scalar", lambda e: e.copy(out=sxrep[:], in_=p1b[:, 0:512]),
                   inc=SC)
            for mc in range(32):
                q.wait("vector", SC)
                q.emit("vector", lambda e, mc=mc: e.tensor_mul(
                    out=lnt1[:, 0:512], in0=hact[:, mc * 512:(mc + 1) * 512],
                    in1=sxrep[:]), inc=VE)
                q.emit("vector", lambda e, mc=mc: e.tensor_scalar(
                    out=hact[:, mc * 512:(mc + 1) * 512], in0=lnt1[:, 0:512],
                    scalar1=RND, scalar2=RND, op0=ALU.add, op1=ALU.subtract),
                    inc=VE)
            q.emit("vector", lambda e: e.reciprocal(
                out=s2row[0:1, 0:512], in_=s2row[0:1, 0:512]), inc=VE)
            q.emit("vector", lambda e: e.tensor_single_scalar(
                out=s2row[0:1, 0:512], in_=s2row[0:1, 0:512],
                scalar=cmean_sb[0:1, 2:3], op=ALU.mult), inc=VE)
            q.wait("tensor", VE)
            q.wait("tensor", SC)
            q.emit("tensor", lambda e: e.matmul(
                out=p1b[:, 0:512], lhsT=onesr_f[:], rhs=s2row[0:1, 0:512],
                start=True, stop=True), inc=TE)
            q.wait("scalar", TE)
            q.emit("scalar", lambda e: e.copy(out=dqd[:], in_=p1b[:, 0:512]), inc=SC)
            for dc in range(8):
                q.wait("gpsimd", TE)
                dma("gpsimd", wdr[:], wqd_scr[dc])
                q.wait("tensor", GP)
                q.wait("tensor", SC)
                q.wait("tensor", VE)
                for ic in range(32):
                    q.emit("tensor", lambda e, ic=ic: e.matmul(
                        out=p1a[:, 0:512], lhsT=wdr[:, ic * 128:(ic + 1) * 128],
                        rhs=hact[:, ic * 512:(ic + 1) * 512],
                        start=(ic == 0), stop=(ic == 31)),
                        inc=(TE if ic == 31 else None))
                q.wait("vector", TE)
                q.emit("vector", lambda e, dc=dc: e.tensor_mul(
                    out=lnt1[:, 0:512], in0=p1a[:, 0:512], in1=dqd[:]), inc=VE)
                q.emit("vector", lambda e, dc=dc, t0=t0: e.tensor_add(
                    out=x_sb[dc][:, t0:t0 + 512], in0=x_sb[dc][:, t0:t0 + 512],
                    in1=lnt1[:, 0:512]), inc=VE)
        q.barrier(ALLS)

    def finalize():
        with nc.Block() as block:
            @block.sync
            def _(e):
                for th in q.q["sync"]:
                    th(e)
            @block.vector
            def _(e):
                for th in q.q["vector"]:
                    th(e)
            @block.scalar
            def _(e):
                for th in q.q["scalar"]:
                    th(e)
            @block.tensor
            def _(e):
                for th in q.q["tensor"]:
                    th(e)
            @block.gpsimd
            def _(e):
                for th in q.q["gpsimd"]:
                    th(e)
    return nc, q, dict(out_h=out_h, x_sb=x_sb, SY=SY, finalize=finalize,
                       sems=ALLS, estack=estack)


def add_store_and_finalize(nc, q, env):
    SY = env["SY"]
    q.barrier(env["sems"], ["sync"])
    for dc in range(8):
        q.emit("sync", lambda e, dc=dc: e.dma_start(
            out=env["out_h"][dc], in_=env["x_sb"][dc][:]), inc=SY, inck=16)
    q.wait("sync", SY)
    env["finalize"]()
    env["estack"].close()
    return nc


def build_full(stage_limit=8):
    nc, q, env = build_program(stage_limit)
    return add_store_and_finalize(nc, q, env)


def kernel(**inputs):
    maps = stage_inputs(inputs)
    nc = build_full()
    res = run_bass_kernel_spmd(nc, maps, core_ids=list(range(NC)))
    return unstage_output(res.results)
